# version 08
# baseline (speedup 1.0000x reference)
import numpy as np
import ml_dtypes
import concourse.bass as bass
import concourse.mybir as mybir
from concourse.bass_utils import run_bass_kernel_spmd
from concourse.ap import AP

F32 = mybir.dt.float32
BF16 = mybir.dt.bfloat16
F8 = mybir.dt.float8e5
I32 = mybir.dt.int32
AF = mybir.ActivationFunctionType
ALU = mybir.AluOpType
AX = mybir.AxisListType

NCORES = 8
L = 4096
D = 1024
H = 8
DH = 128
NCH = 32
NSC = 16
EPS = 1e-5
MAGIC = 0x5F3759DF + 1

_DBG = {"taps": None}


def build_program():
    nc = bass.Bass("TRN2", target_bir_lowering=False, debug=False)

    x_h = nc.dram_tensor("x", [L, D], F32, kind="ExternalInput")
    wqkv_h = nc.dram_tensor("wqkv", [D, 3 * D], F32, kind="ExternalInput")
    wab_h = nc.dram_tensor("wab", [D, 16], F32, kind="ExternalInput")
    wo_h = nc.dram_tensor("wo", [D, D], F32, kind="ExternalInput")
    w1_h = nc.dram_tensor("w1", [D, 4 * D], F32, kind="ExternalInput")
    w2_h = nc.dram_tensor("w2", [4 * D, D], F32, kind="ExternalInput")
    bab_h = nc.dram_tensor("babi", [1, 16], F32, kind="ExternalInput")
    bo_h = nc.dram_tensor("boi", [1, D], F32, kind="ExternalInput")
    b1_h = nc.dram_tensor("b1i", [1, 4 * D], F32, kind="ExternalInput")
    b2_h = nc.dram_tensor("b2i", [1, D], F32, kind="ExternalInput")
    ct_h = nc.dram_tensor("ct", [L, 256], BF16, kind="ExternalInput")
    mk_h = nc.dram_tensor("mk", [L, D], BF16, kind="ExternalInput")
    idb_h = nc.dram_tensor("idb", [128, 128], BF16, kind="ExternalInput")
    idf_h = nc.dram_tensor("idf", [128, 128], F32, kind="ExternalInput")
    amk_h = nc.dram_tensor("amk", [128, 128], BF16, kind="ExternalInput")
    y_h = nc.dram_tensor("y_out", [L, D], F32, kind="ExternalOutput")

    dbg_handles = {}
    if _DBG["taps"]:
        for name, shp in _DBG["taps"].items():
            dbg_handles[name] = nc.dram_tensor("dbg_" + name, shp, F32,
                                               kind="ExternalOutput")

    A = nc.alloc_sbuf_tensor
    WQKV = A("WQKV", [128, 8 * 3072], BF16)
    WAB = A("WAB", [128, 128], BF16)
    WO = A("WO", [128, 8 * 1024], BF16)
    W1T = A("W1T", [128, 8 * 4096], F8)
    W2T = A("W2T", [128, 32 * 1024], F8)
    BAB = A("BAB", [1, 16], BF16)
    BO = A("BO", [1, D], BF16)
    B1 = A("B1", [128, 32], F8)
    B2 = A("B2", [1, D], F8)
    ONEB = A("ONEB", [1, 512], BF16)
    ONE8 = A("ONE8", [1, 512], F8)
    IDB = A("IDB", [128, 128], BF16)
    IDF = A("IDF", [128, 128], F32)
    AMK = A("AMK", [128, 128], BF16)

    X0 = A("X0", [128, 2048], F32)
    CT = A("CT", [128, 512], BF16)
    MB = A("MB", [128, 2048], BF16)
    X1 = A("X1", [128, 2048], F32)
    LT1 = A("LT1", [128, 1024], BF16)
    L1F = A("L1F", [128, 1024], BF16)
    QT = A("QT", [128, 1024], BF16)
    KT = A("KT", [128, 1024], BF16)
    VT = A("VT", [128, 1024], BF16)
    SWQ = A("SWQ", [128, 1024], BF16)
    SWK = A("SWK", [128, 1024], BF16)
    YB = A("YB", [128, 1024], BF16)
    Q2F = A("Q2F", [128, 1024], BF16)
    K2F = A("K2F", [128, 1024], BF16)
    AM = A("AM", [128, 1024], BF16)
    OF = A("OF", [128, 1024], BF16)
    SST = A("SST", [128, 1024], F32)
    SH16 = A("SH16", [128, 1024], BF16)
    GSIG = A("GSIG", [8, 256], F32)
    GBET = A("GBET", [8, 128], F32)
    GD = A("GD", [8, 128], F32)
    GRD = A("GRD", [8, 128], F32)
    GPK = A("GPK", [8, 384], F32)
    GT = A("GT", [128, 24], F32)
    ZER8 = A("ZER8", [8, 128], F32)
    ST_ = A("ST_", [128, 32], F32)
    SQJ = A("SQJ", [128, 1024], F8)
    L2T = A("L2T", [128, 1024], F32)
    L2F = A("L2F", [128, 2048], F8)
    HB = A("HB", [128, 512], BF16)
    H8 = A("H8", [128, 8192], F8)

    PSA = nc.alloc_psum_tensor("PSA", [128, 1024], F32)
    PSB = nc.alloc_psum_tensor("PSB", [128, 1024], F32)
    PSC = nc.alloc_psum_tensor("PSC", [128, 1024], F32)
    PSD = nc.alloc_psum_tensor("PSD", [128, 1024], F32)

    sems = {
        "LD": nc.alloc_semaphore(name="sLD"),
        "WL": nc.alloc_semaphore(name="sWL"),
        "ST": nc.alloc_semaphore(name="sST"),
        "PE": nc.alloc_semaphore(name="sPE"),
        "G": nc.alloc_semaphore(name="sG"),
        "V": nc.alloc_semaphore(name="sV"),
        "S": nc.alloc_semaphore(name="sS"),
    }

    def schedule(E, eng):
        cnt = {k: 0 for k in sems}
        snap = {}
        last = [None]

        def dma(q, sem, out, in_, label=None, **kw):
            cnt[sem] += 16
            if E == q:
                eng.dma_start(out=out, in_=in_, **kw).then_inc(sems[sem], 16)
            if label is not None:
                snap[label] = cnt[sem]

        def wt(q, sem, val):
            if E == q and val > 0:
                eng.wait_ge(sems[sem], val)

        def op(q, fn):
            if E == q:
                last[0] = fn(eng)

        def mark(q, sem, label=None):
            cnt[sem] += 1
            if E == q:
                last[0].then_inc(sems[sem], 1)
            if label is not None:
                snap[label] = cnt[sem]

        TEN, VEC, SCA, SYN, GPS = "tensor", "vector", "scalar", "sync", "gpsimd"

        def tap(name, ap):
            if dbg_handles and name in dbg_handles:
                wt(GPS, "V", cnt["V"])
                wt(GPS, "S", cnt["S"])
                wt(GPS, "PE", cnt["PE"])
                dma(GPS, "ST", dbg_handles[name].ap(), ap)

        if E == SCA:
            eng.preload_act_table(AF.Sigmoid)
        dma(GPS, "WL", WQKV[:],
            AP(wqkv_h, 0, [[3072, 128], [128 * 3072, 8], [1, 3072]]))
        dma(GPS, "WL", WAB[:],
            AP(wab_h, 0, [[16, 128], [128 * 16, 8], [1, 16]]))
        dma(GPS, "WL", BAB[:], bab_h.ap(), label="w_ab")
        dma(GPS, "WL", BO[:], bo_h.ap())
        dma(GPS, "WL", WO[:],
            AP(wo_h, 0, [[1024, 128], [128 * 1024, 8], [1, 1024]]), label="w_wo")
        dma(GPS, "WL", W1T[:],
            AP(w1_h, 0, [[4096, 128], [128 * 4096, 8], [1, 4096]]))
        dma(GPS, "WL", B1[:], AP(b1_h, 0, [[1, 128], [128, 32]]),
            allow_slow_non_contiguous=True)
        for j in range(4):
            dma(GPS, "WL", W2T[:, j * 8192:(j + 1) * 8192],
                AP(w2_h, j * 8 * 128 * 1024,
                   [[1024, 128], [128 * 1024, 8], [1, 1024]]))
        dma(GPS, "WL", B2[:], b2_h.ap(), label="w_ffn")

        dma(SYN, "LD", IDB[:], idb_h.ap())
        dma(SYN, "LD", IDF[:], idf_h.ap())
        dma(SYN, "LD", AMK[:], amk_h.ap(), label="consts")
        xap = x_h.ap()
        ctap = ct_h.ap()
        mkap = mk_h.ap()
        for k in range(2):
            dma(SYN, "LD", X0[:, k * 1024:(k + 1) * 1024],
                xap[k * 128:(k + 1) * 128, :], label=f"x0_{k}")
            dma(SYN, "LD", CT[:, k * 256:(k + 1) * 256],
                ctap[k * 128:(k + 1) * 128, :], label=f"ct_{k}")
            dma(SYN, "LD", MB[:, k * 1024:(k + 1) * 1024],
                mkap[k * 128:(k + 1) * 128, :], label=f"mk_{k}")

        op(VEC, lambda e: e.memset(SST[:], 0.0))
        op(VEC, lambda e: e.memset(ZER8[:], 0.0))
        op(VEC, lambda e: e.memset(ONEB[:], 1.0))
        op(VEC, lambda e: e.memset(ONE8[:], 1.0))
        mark(VEC, "V")

        AMKb = AP(AMK, 0, [[128, 128], [0, 8], [1, 128]])

        for ci in range(NCH):
            sc, c, sl = ci // 2, ci % 2, ci % 2
            x0 = X0[:, sl * 1024:(sl + 1) * 1024]

            wt(VEC, "LD", snap[f"x0_{ci}"])
            op(VEC, lambda e: e.reduce_sum(ST_[:, 0:1], x0, axis=AX.X))
            op(VEC, lambda e: e.drain())
            mark(VEC, "V")
            wt(SCA, "LD", snap[f"x0_{ci}"])
            if ci > 0:
                wt(SCA, "V", snap[f"a3v_{ci - 1}"])
            op(SCA, lambda e: e.activation(SQJ[:], x0, AF.Square,
                                           accum_out=ST_[:, 1:2]))
            mark(SCA, "S")
            wt(VEC, "S", cnt["S"])
            _ln_chain(eng, E, op, ST_, 0, 1, 6, 2, 3, 4, 5, 1)
            mark(VEC, "V", f"a3v_{ci}")
            tap(f"stats_{ci}", ST_[:, 0:8])
            op(VEC, lambda e: e.tensor_scalar(
                LT1[:], x0, ST_[:, 6:7], ST_[:, 4:5], ALU.add, ALU.mult))
            mark(VEC, "V")
            tap(f"lt1_{ci}", LT1[:])

            if ci == 0:
                wt(TEN, "LD", snap["consts"])
            wt(TEN, "V", cnt["V"])
            for i in range(8):
                op(TEN, lambda e, i=i: e.transpose(
                    PSB.bitcast(BF16)[:, i * 128:(i + 1) * 128],
                    LT1[:, i * 128:(i + 1) * 128], IDB[:]))
            mark(TEN, "PE")
            wt(SCA, "PE", cnt["PE"])
            op(SCA, lambda e: e.copy(L1F[:], PSB.bitcast(BF16)[:, 0:1024]))
            mark(SCA, "S")
            tap(f"l1f_{ci}", L1F[:])

            if ci == 0:
                wt(TEN, "WL", snap["w_ab"])
            wt(TEN, "S", cnt["S"])
            dest = {0: QT, 1: QT, 2: KT, 3: KT, 4: VT, 5: VT}
            for g in range(6):
                bank = PSA[:, (g % 2) * 512:((g % 2) + 1) * 512]
                if g >= 2:
                    wt(TEN, "S", snap[f"pj_{ci}_{g - 2}"])
                for i in range(8):
                    op(TEN, lambda e, i=i, g=g, bank=bank: e.matmul(
                        bank, L1F[:, i * 128:(i + 1) * 128],
                        WQKV[:, i * 3072 + g * 512: i * 3072 + (g + 1) * 512],
                        start=(i == 0), stop=(i == 7)))
                mark(TEN, "PE")
                wt(SCA, "PE", cnt["PE"])
                op(SCA, lambda e, g=g, bank=bank: e.copy(
                    dest[g][:, (g % 2) * 512:((g % 2) + 1) * 512], bank))
                mark(SCA, "S", f"pj_{ci}_{g}")
            for gg in range(2):
                out = PSB[0:8, 512 + gg * 128: 512 + (gg + 1) * 128]
                for i in range(8):
                    op(TEN, lambda e, i=i, gg=gg, out=out: e.matmul(
                        out, WAB[:, i * 16 + gg * 8: i * 16 + gg * 8 + 8],
                        L1F[:, i * 128:(i + 1) * 128],
                        start=(i == 0), stop=False))
                op(TEN, lambda e, gg=gg, out=out: e.matmul(
                    out, BAB[0:1, gg * 8:(gg + 1) * 8], ONEB[0:1, 0:128],
                    start=False, stop=True))
            mark(TEN, "PE")

            tap(f"qt_{ci}", QT[:])
            tap(f"kt_{ci}", KT[:])
            tap(f"vt_{ci}", VT[:])
            wt(SCA, "PE", cnt["PE"])
            if ci > 0:
                wt(SCA, "V", snap[f"a9v_{ci - 1}"])
            op(SCA, lambda e: e.activation(GSIG[:], PSB[0:8, 512:768], AF.Sigmoid))
            mark(SCA, "S")

            tap(f"gsig_{ci}", GSIG[:])
            wt(VEC, "S", cnt["S"])
            op(VEC, lambda e: e.tensor_scalar(
                GBET[:], GSIG[:, 0:128], 0.14900000393390656, 0.85,
                ALU.mult, ALU.add))
            op(VEC, lambda e: e.tensor_tensor_scan(
                GD[:], GBET[:], ZER8[:], 1.0, ALU.mult, ALU.add))
            op(VEC, lambda e: e.reciprocal(GRD[:], GD[:]))
            op(VEC, lambda e: e.drain())
            op(VEC, lambda e: e.tensor_scalar(
                GPK[:, 0:128], GD[:], GRD[:, 127:128], None, ALU.mult))
            op(VEC, lambda e: e.scalar_tensor_tensor(
                GPK[:, 128:256], GRD[:], GD[:, 127:128], GSIG[:, 128:256],
                ALU.mult, ALU.mult))
            op(VEC, lambda e: e.tensor_scalar(
                GPK[:, 256:384], ZER8[:], GD[:, 127:128], None, ALU.add))
            mark(VEC, "V", f"a9v_{ci}")
            tap(f"gpk_{ci}", GPK[:])

            wt(TEN, "V", cnt["V"])
            for j in range(3):
                op(TEN, lambda e, j=j: e.transpose(
                    PSB[:, 512 + j * 8: 512 + (j + 1) * 8],
                    GPK[0:8, j * 128:(j + 1) * 128], IDF[0:8, 0:8]))
            mark(TEN, "PE")
            wt(SCA, "PE", cnt["PE"])
            op(SCA, lambda e: e.copy(GT[:], PSB[:, 512:536]))
            mark(SCA, "S")
            tap(f"gt_{ci}", GT[:])

            wt(VEC, "LD", snap[f"ct_{ci}"])
            wt(VEC, "S", cnt["S"])
            for TSR, SW in ((QT, SWQ), (KT, SWK)):
                op(VEC, lambda e, TSR=TSR, SW=SW: e.tensor_copy(
                    AP(SW, 0, [[1024, 128], [128, 8], [1, 64]]),
                    AP(TSR, 64, [[1024, 128], [128, 8], [1, 64]])))
                op(VEC, lambda e, TSR=TSR, SW=SW: e.tensor_copy(
                    AP(SW, 64, [[1024, 128], [128, 8], [1, 64]]),
                    AP(TSR, 0, [[1024, 128], [128, 8], [1, 64]])))
            for TSR, SW, gofs in ((QT, SWQ, 0), (KT, SWK, 8)):
                op(VEC, lambda e, SW=SW: e.tensor_tensor(
                    SW[:], SW[:],
                    AP(CT, sl * 256 + 128, [[512, 128], [0, 8], [1, 128]]),
                    ALU.mult))
                op(VEC, lambda e, TSR=TSR: e.tensor_tensor(
                    TSR[:], TSR[:],
                    AP(CT, sl * 256, [[512, 128], [0, 8], [1, 128]]),
                    ALU.mult))
                op(VEC, lambda e, TSR=TSR, SW=SW: e.tensor_tensor(
                    TSR[:], TSR[:], SW[:], ALU.add))
                for h in range(H):
                    op(VEC, lambda e, TSR=TSR, h=h, gofs=gofs: e.tensor_scalar(
                        TSR[:, h * 128:(h + 1) * 128],
                        TSR[:, h * 128:(h + 1) * 128],
                        GT[:, gofs + h: gofs + h + 1], None, ALU.mult))
            mark(VEC, "V")
            tap(f"qr_{ci}", QT[:])
            tap(f"kr_{ci}", KT[:])

            wt(TEN, "V", cnt["V"])
            for src, dl in ((QT, 0), (KT, 1024)):
                for i in range(8):
                    op(TEN, lambda e, src=src, dl=dl, i=i: e.transpose(
                        PSB.bitcast(BF16)[:, dl + i * 128: dl + (i + 1) * 128],
                        src[:, i * 128:(i + 1) * 128], IDB[:]))
            mark(TEN, "PE")
            wt(VEC, "PE", cnt["PE"])
            op(VEC, lambda e: e.tensor_copy(Q2F[:], PSB.bitcast(BF16)[:, 0:1024]))
            mark(VEC, "V")
            wt(SCA, "PE", cnt["PE"])
            op(SCA, lambda e: e.copy(K2F[:], PSB.bitcast(BF16)[:, 1024:2048]))
            mark(SCA, "S")
            tap(f"q2f_{ci}", Q2F[:])
            tap(f"k2f_{ci}", K2F[:])
            wt(VEC, "S", cnt["S"])
            for h in range(H):
                op(VEC, lambda e, h=h: e.tensor_scalar(
                    SH16[:, h * 128:(h + 1) * 128],
                    SST[:, h * 128:(h + 1) * 128],
                    GT[:, 16 + h:17 + h], None, ALU.mult))
            mark(VEC, "V")

            wt(TEN, "V", cnt["V"])
            wt(TEN, "S", cnt["S"])
            for h in range(H):
                op(TEN, lambda e, h=h: e.matmul(
                    PSC[:, h * 128:(h + 1) * 128],
                    K2F[:, h * 128:(h + 1) * 128],
                    Q2F[:, h * 128:(h + 1) * 128], start=True, stop=True))
            mark(TEN, "PE")
            wt(VEC, "PE", cnt["PE"])
            op(VEC, lambda e: e.tensor_tensor(AM[:], PSC[:], AMKb, ALU.mult))
            mark(VEC, "V")
            tap(f"am_{ci}", AM[:])

            wt(TEN, "V", cnt["V"])
            for h in range(H):
                s_ = slice(h * 128, (h + 1) * 128)
                op(TEN, lambda e, s_=s_: e.matmul(
                    PSD[:, s_], VT[:, s_], AM[:, s_], start=True, stop=False))
                op(TEN, lambda e, s_=s_: e.matmul(
                    PSD[:, s_], SH16[:, s_], Q2F[:, s_], start=False, stop=True))
            mark(TEN, "PE", f"o_{ci}")
            for h in range(H):
                s_ = slice(h * 128, (h + 1) * 128)
                op(TEN, lambda e, s_=s_: e.matmul(
                    PSA[:, s_], KT[:, s_], VT[:, s_], start=True, stop=True))
            mark(TEN, "PE")

            wt(VEC, "PE", cnt["PE"])
            for h in range(H):
                s_ = slice(h * 128, (h + 1) * 128)
                op(VEC, lambda e, s_=s_, h=h: e.scalar_tensor_tensor(
                    SST[:, s_], SST[:, s_], GT[:, 16 + h:17 + h], PSA[:, s_],
                    ALU.mult, ALU.add))
            mark(VEC, "V")
            tap(f"sst_{ci}", SST[:])
            op(VEC, lambda e: e.tensor_copy(OF[:], PSD[:]))
            mark(VEC, "V")

            tap(f"of_{ci}", OF[:])
            if ci == 0:
                wt(TEN, "WL", snap["w_wo"])
            wt(TEN, "V", cnt["V"])
            for mh in range(2):
                for e8 in range(8):
                    op(TEN, lambda e, mh=mh, e8=e8: e.matmul(
                        PSC[:, mh * 512:(mh + 1) * 512],
                        OF[:, e8 * 128:(e8 + 1) * 128],
                        WO[:, e8 * 1024 + mh * 512: e8 * 1024 + (mh + 1) * 512],
                        start=(e8 == 0), stop=False))
                op(TEN, lambda e, mh=mh: e.matmul(
                    PSC[:, mh * 512:(mh + 1) * 512],
                    ONEB[0:1, 0:128], BO[0:1, mh * 512:(mh + 1) * 512],
                    start=False, stop=True))
            mark(TEN, "PE")

            wt(VEC, "PE", cnt["PE"])
            if sc > 0:
                wt(VEC, "ST", snap[f"st_{(sc - 1) * 2 + c}"])
            op(VEC, lambda e: e.tensor_tensor(
                X1[:, c * 1024:(c + 1) * 1024], PSC[:], x0, ALU.add))
            mark(VEC, "V", f"x1_{ci}")
            tap(f"x1_{ci}", X1[:, c * 1024:(c + 1) * 1024])

            if ci + 2 < NCH:
                wt(SYN, "V", snap[f"x1_{ci}"])
                dma(SYN, "LD", X0[:, sl * 1024:(sl + 1) * 1024],
                    xap[(ci + 2) * 128:(ci + 3) * 128, :], label=f"x0_{ci + 2}")
                dma(SYN, "LD", CT[:, sl * 256:(sl + 1) * 256],
                    ctap[(ci + 2) * 128:(ci + 3) * 128, :], label=f"ct_{ci + 2}")

            if c == 0:
                continue

            for cc in range(2):
                op(VEC, lambda e, cc=cc: e.reduce_sum(
                    ST_[:, 8 + cc:9 + cc], X1[:, cc * 1024:(cc + 1) * 1024],
                    axis=AX.X))
            op(VEC, lambda e: e.drain())
            mark(VEC, "V")
            wt(SCA, "V", cnt["V"])
            if sc > 0:
                wt(SCA, "V", snap[f"f2v_{sc - 1}"])
            for cc in range(2):
                op(SCA, lambda e, cc=cc: e.activation(
                    SQJ[:], X1[:, cc * 1024:(cc + 1) * 1024], AF.Square,
                    accum_out=ST_[:, 10 + cc:11 + cc]))
            mark(SCA, "S")
            wt(VEC, "S", cnt["S"])
            for cc in range(2):
                _ln_chain(eng, E, op, ST_, 8 + cc, 10 + cc, 20 + cc,
                          12 + cc, 14 + cc, 16 + cc, 18 + cc, 1)
            mark(VEC, "V", f"f2v_{sc}")
            tap(f"f2st_{sc}", ST_[:, 0:32])

            for cc in range(2):
                op(VEC, lambda e, cc=cc: e.tensor_scalar(
                    L2T[:], X1[:, cc * 1024:(cc + 1) * 1024],
                    ST_[:, 20 + cc:21 + cc], ST_[:, 16 + cc:17 + cc],
                    ALU.add, ALU.mult))
                mark(VEC, "V")
                wt(TEN, "V", cnt["V"])
                for i in range(8):
                    op(TEN, lambda e, i=i: e.transpose(
                        PSB[:, i * 128:(i + 1) * 128],
                        L2T[:, i * 128:(i + 1) * 128], IDF[:]))
                mark(TEN, "PE")
                wt(VEC, "PE", cnt["PE"])
                op(VEC, lambda e, cc=cc: e.tensor_copy(
                    AP(L2F, cc * 128, [[2048, 128], [256, 8], [1, 128]]),
                    AP(PSB, 0, [[1024, 128], [128, 8], [1, 128]])))
                mark(VEC, "V")

            tap(f"l2f_{sc}", L2F[:])
            if sc == 0:
                wt(TEN, "WL", snap["w_ffn"])
            for hf in range(2):
                wt(TEN, "V", cnt["V"])
                for ot in range(16):
                    og = hf * 16 + ot
                    bank = PSA[:, (ot % 2) * 512:(ot % 2) * 512 + 256]
                    if ot >= 2:
                        wt(TEN, "S", snap[f"h_{sc}_{hf}_{ot - 2}"])
                    elif hf > 0:
                        wt(TEN, "S", snap[f"h_{sc}_{hf - 1}_{14 + ot}"])
                    for i in range(8):
                        op(TEN, lambda e, i=i, og=og, bank=bank: e.matmul(
                            bank,
                            W1T[:, i * 4096 + og * 128: i * 4096 + (og + 1) * 128],
                            L2F[:, i * 256:(i + 1) * 256],
                            start=(i == 0), stop=(i == 7)))
                    mark(TEN, "PE")
                    wt(SCA, "PE", cnt["PE"])
                    if ot >= 2 or hf > 0:
                        pot = (ot - 2) % 16
                        phf = hf if ot >= 2 else hf - 1
                        wt(SCA, "V", snap[f"h8_{sc}_{phf}_{pot}"])
                    op(SCA, lambda e, og=og, bank=bank: e.activation(
                        HB[:, (og % 2) * 256:((og % 2) + 1) * 256], bank, AF.Relu,
                        bias=B1[:, og:og + 1]))
                    mark(SCA, "S", f"h_{sc}_{hf}_{ot}")
                    wt(VEC, "S", cnt["S"])
                    op(VEC, lambda e, og=og: e.tensor_copy(
                        H8[:, og * 256:(og + 1) * 256],
                        HB[:, (og % 2) * 256:((og % 2) + 1) * 256]))
                    mark(VEC, "V", f"h8_{sc}_{hf}_{ot}")

            wt(TEN, "V", cnt["V"])
            for ts in range(2):
                psy = PSC if ts == 0 else PSD
                for og in range(32):
                    for eh in range(2):
                        op(TEN, lambda e, og=og, eh=eh, psy=psy: e.matmul(
                            psy[:, eh * 512:(eh + 1) * 512],
                            H8[:, og * 256 + ts * 128: og * 256 + (ts + 1) * 128],
                            W2T[:, og * 1024 + eh * 512: og * 1024 + (eh + 1) * 512],
                            start=(og == 0), stop=False))
                for eh in range(2):
                    op(TEN, lambda e, eh=eh, psy=psy: e.matmul(
                        psy[:, eh * 512:(eh + 1) * 512],
                        ONE8[0:1, 0:128], B2[0:1, eh * 512:(eh + 1) * 512],
                        start=False, stop=True))
                mark(TEN, "PE")

            for ts in range(2):
                psy = PSC if ts == 0 else PSD
                wt(SCA, "PE", cnt["PE"])
                if ts > 0:
                    wt(SCA, "V", snap[f"f8_{sc}_{ts - 1}"])
                op(SCA, lambda e, psy=psy: e.copy(YB[:], psy[:]))
                mark(SCA, "S")
                wt(VEC, "S", cnt["S"])
                wt(VEC, "LD", snap[f"mk_{sc * 2 + ts}"])
                op(VEC, lambda e, ts=ts: e.tensor_tensor(
                    L2T[:], YB[:], MB[:, ts * 1024:(ts + 1) * 1024], ALU.mult))
                op(VEC, lambda e, ts=ts: e.tensor_tensor(
                    X1[:, ts * 1024:(ts + 1) * 1024], L2T[:],
                    X1[:, ts * 1024:(ts + 1) * 1024], ALU.add))
                mark(VEC, "V", f"f8_{sc}_{ts}")
                wt(GPS, "V", snap[f"f8_{sc}_{ts}"])
                dma(GPS, "ST", y_h.ap()[(sc * 2 + ts) * 128:(sc * 2 + ts + 1) * 128, :],
                    X1[:, ts * 1024:(ts + 1) * 1024], label=f"st_{sc * 2 + ts}")
                if sc + 1 < NSC:
                    wt(SYN, "V", snap[f"f8_{sc}_{ts}"])
                    dma(SYN, "LD", MB[:, ts * 1024:(ts + 1) * 1024],
                        mkap[((sc + 1) * 2 + ts) * 128:((sc + 1) * 2 + ts + 1) * 128, :],
                        label=f"mk_{(sc + 1) * 2 + ts}")

        wt(GPS, "ST", cnt["ST"])

    def _ln_chain(eng, E, op, ST, c_sum, c_sq, c_negm, c_m2, c_var, c_y, c_t, w):
        sl = lambda c: ST[:, c:c + w]
        sli = lambda c: ST[:, c:c + w].bitcast(I32)
        dr = lambda: op("vector", lambda e: e.drain())
        op("vector", lambda e: e.tensor_scalar(
            sl(c_var), sl(c_sq), 1.0 / D, EPS, ALU.mult, ALU.add))
        op("vector", lambda e: e.tensor_scalar_mul(sl(c_negm), sl(c_sum), -1.0 / D))
        dr()
        op("vector", lambda e: e.tensor_scalar(
            sl(c_m2), sl(c_sum), -1.0 / D, sl(c_negm), ALU.mult, ALU.mult))
        dr()
        op("vector", lambda e: e.tensor_scalar(
            sl(c_var), sl(c_var), sl(c_m2), None, ALU.subtract))
        dr()
        op("vector", lambda e: e.tensor_scalar(
            sli(c_y), sli(c_var), 1, 0xFFFFFFFF,
            ALU.arith_shift_right, ALU.bitwise_xor))
        dr()
        op("vector", lambda e: e.tensor_single_scalar(
            sli(c_y), sli(c_y), MAGIC, ALU.add))
        dr()
        for _ in range(3):
            op("vector", lambda e: e.tensor_scalar(
                sl(c_t), sl(c_var), sl(c_y), None, ALU.mult))
            dr()
            op("vector", lambda e: e.tensor_scalar(
                sl(c_t), sl(c_t), sl(c_y), -0.5, ALU.mult, ALU.mult))
            dr()
            op("vector", lambda e: e.tensor_scalar(
                sl(c_t), sl(c_t), 1.5, None, ALU.add))
            dr()
            op("vector", lambda e: e.tensor_scalar(
                sl(c_y), sl(c_y), sl(c_t), None, ALU.mult))
            dr()

    with nc.Block() as block:
        @block.sync
        def _(eng):
            schedule("sync", eng)

        @block.gpsimd
        def _(eng):
            schedule("gpsimd", eng)

        @block.tensor
        def _(eng):
            schedule("tensor", eng)

        @block.vector
        def _(eng):
            schedule("vector", eng)

        @block.scalar
        def _(eng):
            schedule("scalar", eng)

    return nc


_NC_CACHE = [None]


def _host_prep(L0_attn_Wq, L0_attn_Wk, L0_attn_Wv, L0_attn_Wbeta, L0_attn_bbeta,
               L0_attn_Walpha, L0_attn_balpha, L0_attn_Wo, L0_attn_bo,
               L0_ffn_W1, L0_ffn_b1, L0_ffn_W2, L0_ffn_b2):
    c = np.ascontiguousarray

    perm = np.concatenate([np.arange(0, DH, 2), np.arange(1, DH, 2)])
    wq = np.asarray(L0_attn_Wq, dtype=np.float32)
    wk = np.asarray(L0_attn_Wk, dtype=np.float32)
    wq_p = wq.reshape(H, DH, D)[:, perm, :].reshape(D, D)
    wk_p = wk.reshape(H, DH, D)[:, perm, :].reshape(D, D)
    wqkv = c(np.concatenate(
        [wq_p.T, wk_p.T, np.asarray(L0_attn_Wv, np.float32).T], axis=1))
    wab = c(np.concatenate(
        [np.asarray(L0_attn_Wbeta, np.float32).T,
         np.asarray(L0_attn_Walpha, np.float32).T], axis=1))
    bab = c(np.concatenate(
        [np.asarray(L0_attn_bbeta, np.float32),
         np.asarray(L0_attn_balpha, np.float32)]).reshape(1, 16))
    wo = c(np.asarray(L0_attn_Wo, np.float32).T)
    bo = c(np.asarray(L0_attn_bo, np.float32).reshape(1, D))
    w1 = c(np.asarray(L0_ffn_W1, np.float32).T)
    b1 = c(np.asarray(L0_ffn_b1, np.float32).reshape(1, 4 * D))
    w2 = c(np.asarray(L0_ffn_W2, np.float32).T)
    b2 = c(np.asarray(L0_ffn_b2, np.float32).reshape(1, D))

    half = DH // 2
    inv_freq = (np.float32(1.0) /
                np.power(np.float32(10000.0),
                         np.arange(half, dtype=np.float32) / np.float32(half)))
    freqs = np.arange(L, dtype=np.float32)[:, None] * inv_freq[None, :]
    cos = np.cos(freqs).astype(np.float32)
    sin = np.sin(freqs).astype(np.float32)
    ct = c(np.concatenate([cos, cos, -sin, sin], axis=1)).astype(ml_dtypes.bfloat16)

    idb = np.eye(128, dtype=ml_dtypes.bfloat16)
    idf = np.eye(128, dtype=np.float32)
    amk = np.triu(np.ones((128, 128), dtype=np.float32)).astype(ml_dtypes.bfloat16)

    import jax
    with jax.default_device(jax.devices("cpu")[0]):
        keep = np.asarray(jax.random.bernoulli(
            jax.random.key(42), 0.8, (NCORES, L, D)))
    mask = np.where(keep, np.float32(1.25), np.float32(0.0)).astype(ml_dtypes.bfloat16)

    shared = dict(wqkv=wqkv, wab=wab, wo=wo, w1=w1, w2=w2, babi=bab, boi=bo,
                  b1i=b1, b2i=b2, ct=ct, idb=idb, idf=idf, amk=amk)
    return shared, mask


def kernel(L0_attn_x, L0_attn_Wq, L0_attn_Wk, L0_attn_Wv, L0_attn_Wbeta,
           L0_attn_bbeta, L0_attn_Walpha, L0_attn_balpha, L0_attn_Wo,
           L0_attn_bo, L0_ffn_W1, L0_ffn_b1, L0_ffn_W2, L0_ffn_b2):
    shared, mask = _host_prep(
        L0_attn_Wq, L0_attn_Wk, L0_attn_Wv, L0_attn_Wbeta, L0_attn_bbeta,
        L0_attn_Walpha, L0_attn_balpha, L0_attn_Wo, L0_attn_bo,
        L0_ffn_W1, L0_ffn_b1, L0_ffn_W2, L0_ffn_b2)

    x = np.asarray(L0_attn_x, dtype=np.float32)
    in_maps = []
    for b in range(NCORES):
        m = dict(shared)
        m["x"] = np.ascontiguousarray(x[b])
        m["mk"] = np.ascontiguousarray(mask[b])
        in_maps.append(m)

    if _NC_CACHE[0] is None:
        _NC_CACHE[0] = build_program()
    nc = _NC_CACHE[0]
    res = run_bass_kernel_spmd(nc, in_maps, core_ids=list(range(NCORES)))
    out = np.stack([res.results[b]["y_out"] for b in range(NCORES)], axis=0)
    return out.astype(np.float32)


# version 16
# speedup vs baseline: 1.2601x; 1.2601x over previous
import numpy as np
import ml_dtypes
import concourse.bass as bass
import concourse.mybir as mybir
from concourse.bass_utils import run_bass_kernel_spmd
from concourse.ap import AP

F32 = mybir.dt.float32
BF16 = mybir.dt.bfloat16
F8 = mybir.dt.float8e5
I32 = mybir.dt.int32
AF = mybir.ActivationFunctionType
ALU = mybir.AluOpType
AX = mybir.AxisListType

NCORES = 8
L = 4096
D = 1024
H = 8
DH = 128
NCH = 32
NSC = 16
EPS = 1e-5
MAGIC = 0x5F3759DF + 1

_DBG = {"taps": None}


def build_program():
    nc = bass.Bass("TRN2", target_bir_lowering=False, debug=False)

    x_h = nc.dram_tensor("x", [L, D], F32, kind="ExternalInput")
    wqkv_h = nc.dram_tensor("wqkv", [D, 3 * D], F32, kind="ExternalInput")
    wab_h = nc.dram_tensor("wab", [D, 16], F32, kind="ExternalInput")
    wo_h = nc.dram_tensor("wo", [D, D], F32, kind="ExternalInput")
    w1_h = nc.dram_tensor("w1", [D, 4 * D], F32, kind="ExternalInput")
    w2_h = nc.dram_tensor("w2", [4 * D, D], F32, kind="ExternalInput")
    bab_h = nc.dram_tensor("babi", [1, 16], F32, kind="ExternalInput")
    bo_h = nc.dram_tensor("boi", [1, D], F32, kind="ExternalInput")
    b1_h = nc.dram_tensor("b1i", [1, 4 * D], F32, kind="ExternalInput")
    b2_h = nc.dram_tensor("b2i", [1, D], F32, kind="ExternalInput")
    ct_h = nc.dram_tensor("ct", [L, 256], BF16, kind="ExternalInput")
    mk_h = nc.dram_tensor("mk", [L, D], BF16, kind="ExternalInput")
    idb_h = nc.dram_tensor("idb", [128, 128], BF16, kind="ExternalInput")
    idf_h = nc.dram_tensor("idf", [128, 128], F32, kind="ExternalInput")
    amk_h = nc.dram_tensor("amk", [128, 128], BF16, kind="ExternalInput")
    y_h = nc.dram_tensor("y_out", [L, D], F32, kind="ExternalOutput")

    dbg_handles = {}
    if _DBG["taps"]:
        for name, shp in _DBG["taps"].items():
            dbg_handles[name] = nc.dram_tensor("dbg_" + name, shp, F32,
                                               kind="ExternalOutput")

    A = nc.alloc_sbuf_tensor
    WQKV = A("WQKV", [128, 8 * 3072], BF16)
    WAB = A("WAB", [128, 128], BF16)
    WO = A("WO", [128, 8 * 1024], BF16)
    W1T = A("W1T", [128, 8 * 4096], F8)
    W2T = A("W2T", [128, 32 * 1024], F8)
    BAB = A("BAB", [1, 16], BF16)
    BO = A("BO", [1, D], BF16)
    B1 = A("B1", [128, 32], F8)
    B2 = A("B2", [1, D], F8)
    ONEB = A("ONEB", [1, 512], BF16)
    ONE8 = A("ONE8", [1, 512], F8)
    IDB = A("IDB", [128, 128], BF16)
    IDF = A("IDF", [128, 128], F32)
    AMK = A("AMK", [128, 128], BF16)

    X0 = A("X0", [128, 2048], F32)
    CT = A("CT", [128, 512], BF16)
    MB = A("MB", [128, 2048], BF16)
    X1 = A("X1", [128, 2048], F32)
    LT1 = A("LT1", [128, 2048], BF16)
    L1F = A("L1F", [128, 1024], BF16)
    QT = A("QT", [128, 1024], BF16)
    KT = A("KT", [128, 1024], BF16)
    VT = A("VT", [128, 1024], BF16)
    SWQ = A("SWQ", [128, 1024], BF16)
    YB = A("YB", [128, 1024], BF16)
    Q2F = A("Q2F", [128, 1024], BF16)
    K2F = A("K2F", [128, 1024], BF16)
    AM = A("AM", [128, 1024], BF16)
    OF = A("OF", [128, 1024], BF16)
    SST = A("SST", [128, 1024], F32)
    SH16 = A("SH16", [128, 1024], BF16)
    GSIG = A("GSIG", [8, 256], F32)
    GBET = A("GBET", [8, 128], F32)
    GD = A("GD", [8, 128], F32)
    GRD = A("GRD", [8, 128], F32)
    GPK = A("GPK", [8, 384], F32)
    GT = A("GT", [128, 24], F32)
    ZER8 = A("ZER8", [8, 128], F32)
    ST_ = A("ST_", [128, 32], F32)
    SQJ = A("SQJ", [128, 1024], F8)
    L2T = A("L2T", [128, 1024], F32)
    L2F = A("L2F", [128, 2048], F8)
    HB = A("HB", [128, 512], BF16)
    H8 = A("H8", [128, 8192], F8)

    PSA = nc.alloc_psum_tensor("PSA", [128, 1024], F32)
    PSB = nc.alloc_psum_tensor("PSB", [128, 1024], F32)
    PSC = nc.alloc_psum_tensor("PSC", [128, 1024], F32)
    PSD = nc.alloc_psum_tensor("PSD", [128, 1024], F32)

    sems = {
        "LD": nc.alloc_semaphore(name="sLD"),
        "WL": nc.alloc_semaphore(name="sWL"),
        "ST": nc.alloc_semaphore(name="sST"),
        "PE": nc.alloc_semaphore(name="sPE"),
        "G": nc.alloc_semaphore(name="sG"),
        "V": nc.alloc_semaphore(name="sV"),
        "S": nc.alloc_semaphore(name="sS"),
    }

    def schedule(E, eng):
        cnt = {k: 0 for k in sems}
        snap = {}
        last = [None]

        def dma(q, sem, out, in_, label=None, **kw):
            cnt[sem] += 16
            if E == q:
                eng.dma_start(out=out, in_=in_, **kw).then_inc(sems[sem], 16)
            if label is not None:
                snap[label] = cnt[sem]

        def wt(q, sem, val):
            if E == q and val > 0:
                eng.wait_ge(sems[sem], val)

        def op(q, fn):
            if E == q:
                last[0] = fn(eng)

        def mark(q, sem, label=None):
            cnt[sem] += 1
            if E == q:
                last[0].then_inc(sems[sem], 1)
            if label is not None:
                snap[label] = cnt[sem]

        TEN, VEC, SCA, SYN, GPS = "tensor", "vector", "scalar", "sync", "gpsimd"

        psL = PSA
        psH = [PSA[:, 0:256], PSA[:, 512:768]]
        psY = PSA
        psT = PSB.bitcast(BF16)[:, 0:1024]
        psP = [PSB[:, 512:1024], PSC[:, 0:512]]
        psAg = [PSC[:, 512:1024], PSD[:, 0:512]]
        psOg = [PSD[:, 512:1024], PSD[:, 0:512]]
        psT1g = [PSC[:, 512:1024], PSD[:, 512:1024]]
        psAB = [PSD[0:8, 512:640], PSD[0:8, 0:128]]
        psG = PSD[:, 512:536]
        psW = [PSC[:, 512:1024], PSD[:, 0:512]]

        Q = []
        qpos = [0]

        def drain(n=1):
            while n > 0 and qpos[0] < len(Q):
                Q[qpos[0]][1]()
                qpos[0] += 1
                n -= 1

        def drain_to(name):
            while qpos[0] < len(Q):
                nm, fn = Q[qpos[0]]
                fn()
                qpos[0] += 1
                if nm == name:
                    return

        def drain_all():
            drain(len(Q) - qpos[0])

        if E == SCA:
            eng.preload_act_table(AF.Sigmoid)
        dma(GPS, "WL", WQKV[:],
            AP(wqkv_h, 0, [[3072, 128], [128 * 3072, 8], [1, 3072]]))
        dma(GPS, "WL", WAB[:],
            AP(wab_h, 0, [[16, 128], [128 * 16, 8], [1, 16]]))
        dma(GPS, "WL", BAB[:], bab_h.ap(), label="w_ab")
        dma(GPS, "WL", BO[:], bo_h.ap())
        dma(GPS, "WL", WO[:],
            AP(wo_h, 0, [[1024, 128], [128 * 1024, 8], [1, 1024]]), label="w_wo")
        dma(GPS, "WL", W1T[:],
            AP(w1_h, 0, [[4096, 128], [128 * 4096, 8], [1, 4096]]))
        dma(GPS, "WL", B1[:], AP(b1_h, 0, [[1, 128], [128, 32]]),
            allow_slow_non_contiguous=True)
        for j in range(4):
            dma(GPS, "WL", W2T[:, j * 8192:(j + 1) * 8192],
                AP(w2_h, j * 8 * 128 * 1024,
                   [[1024, 128], [128 * 1024, 8], [1, 1024]]))
        dma(GPS, "WL", B2[:], b2_h.ap(), label="w_ffn")

        dma(SYN, "LD", IDB[:], idb_h.ap())
        dma(SYN, "LD", IDF[:], idf_h.ap())
        dma(SYN, "LD", AMK[:], amk_h.ap(), label="consts")
        xap = x_h.ap()
        ctap = ct_h.ap()
        mkap = mk_h.ap()
        for k in range(2):
            dma(SYN, "LD", X0[:, k * 1024:(k + 1) * 1024],
                xap[k * 128:(k + 1) * 128, :], label=f"x0_{k}")
            dma(SYN, "LD", CT[:, k * 256:(k + 1) * 256],
                ctap[k * 128:(k + 1) * 128, :], label=f"ct_{k}")
            dma(SYN, "LD", MB[:, k * 1024:(k + 1) * 1024],
                mkap[k * 128:(k + 1) * 128, :], label=f"mk_{k}")

        op(VEC, lambda e: e.memset(SST[:], 0.0))
        op(VEC, lambda e: e.memset(ZER8[:], 0.0))
        op(VEC, lambda e: e.memset(ONEB[:], 1.0))
        op(VEC, lambda e: e.memset(ONE8[:], 1.0))
        mark(VEC, "V")

        AMKb4 = AP(AMK, 0, [[128, 128], [0, 4], [1, 128]])

        def build_ffn_quanta(s):
            def qW1(k):
                def f():
                    if k == 0:
                        wt(TEN, "V", snap[f"l2f_{s}_1"])
                        if s == 0:
                            wt(TEN, "WL", snap["w_ffn"])
                    for ot in range(4 * k, 4 * k + 4):
                        bank = psH[ot % 2]
                        if ot >= 2:
                            wt(TEN, "S", snap[f"h_{s}_{ot - 2}"])
                        for i in range(8):
                            op(TEN, lambda e, i=i, ot=ot, bank=bank: e.matmul(
                                bank,
                                W1T[:, i * 4096 + ot * 128: i * 4096 + (ot + 1) * 128],
                                L2F[:, i * 256:(i + 1) * 256],
                                start=(i == 0), stop=(i == 7)))
                        mark(TEN, "PE", f"w1pe_{s}_{ot}")
                        wt(SCA, "PE", snap[f"w1pe_{s}_{ot}"])
                        if ot >= 2:
                            wt(SCA, "V", snap[f"h8_{s}_{ot - 2}"])
                        op(SCA, lambda e, ot=ot, bank=bank: e.activation(
                            HB[:, (ot % 2) * 256:((ot % 2) + 1) * 256], bank,
                            AF.Relu, bias=B1[:, ot:ot + 1]))
                        mark(SCA, "S", f"h_{s}_{ot}")
                        wt(VEC, "S", snap[f"h_{s}_{ot}"])
                        op(VEC, lambda e, ot=ot: e.tensor_copy(
                            H8[:, ot * 256:(ot + 1) * 256],
                            HB[:, (ot % 2) * 256:((ot % 2) + 1) * 256]))
                        mark(VEC, "V", f"h8_{s}_{ot}")
                return f

            def qW2(ts, m):
                def f():
                    if m == 0:
                        wt(TEN, "V", snap[f"h8_{s}_31"])
                        if ts == 1:
                            wt(TEN, "S", snap[f"yb_{s}_0"])
                    for og in range(4 * m, 4 * m + 4):
                        for eh in range(2):
                            op(TEN, lambda e, og=og, eh=eh, ts=ts: e.matmul(
                                psY[:, eh * 512:(eh + 1) * 512],
                                H8[:, og * 256 + ts * 128: og * 256 + (ts + 1) * 128],
                                W2T[:, og * 1024 + eh * 512: og * 1024 + (eh + 1) * 512],
                                start=(og == 0), stop=False))
                    if m == 7:
                        for eh in range(2):
                            op(TEN, lambda e, eh=eh: e.matmul(
                                psY[:, eh * 512:(eh + 1) * 512],
                                ONE8[0:1, 0:128], B2[0:1, eh * 512:(eh + 1) * 512],
                                start=False, stop=True))
                        mark(TEN, "PE", f"w2_{s}_{ts}")
                return f

            def qEpi(ts):
                def f():
                    wt(SCA, "PE", snap[f"w2_{s}_{ts}"])
                    if ts == 1:
                        wt(SCA, "V", snap[f"f8_{s}_0"])
                    op(SCA, lambda e: e.copy(YB[:], psY[:]))
                    mark(SCA, "S", f"yb_{s}_{ts}")
                    wt(VEC, "S", snap[f"yb_{s}_{ts}"])
                    wt(VEC, "LD", snap[f"mk_{s * 2 + ts}"])
                    op(VEC, lambda e, ts=ts: e.tensor_tensor(
                        L2T[:], YB[:], MB[:, ts * 1024:(ts + 1) * 1024], ALU.mult))
                    op(VEC, lambda e, ts=ts: e.tensor_tensor(
                        X1[:, ts * 1024:(ts + 1) * 1024], L2T[:],
                        X1[:, ts * 1024:(ts + 1) * 1024], ALU.add))
                    mark(VEC, "V", f"f8_{s}_{ts}")
                    wt(GPS, "V", snap[f"f8_{s}_{ts}"])
                    dma(GPS, "ST",
                        y_h.ap()[(s * 2 + ts) * 128:(s * 2 + ts + 1) * 128, :],
                        X1[:, ts * 1024:(ts + 1) * 1024], label=f"st_{s * 2 + ts}")
                    if s + 1 < NSC:
                        wt(SYN, "V", snap[f"f8_{s}_{ts}"])
                        dma(SYN, "LD", MB[:, ts * 1024:(ts + 1) * 1024],
                            mkap[((s + 1) * 2 + ts) * 128:((s + 1) * 2 + ts + 1) * 128, :],
                            label=f"mk_{(s + 1) * 2 + ts}")
                return f

            for k in range(8):
                Q.append((f"w1_{s}_{k}", qW1(k)))
            for m in range(8):
                Q.append((f"w2_{s}_0_{m}", qW2(0, m)))
            Q.append((f"epi_{s}_0", qEpi(0)))
            for m in range(8):
                Q.append((f"w2_{s}_1_{m}", qW2(1, m)))
            Q.append((f"epi_{s}_1", qEpi(1)))

        def emit_ln1(ci):
            sl_ = ci % 2
            x0_ = X0[:, sl_ * 1024:(sl_ + 1) * 1024]
            wt(VEC, "LD", snap[f"x0_{ci}"])
            op(VEC, lambda e: e.reduce_sum(ST_[:, 0:1], x0_, axis=AX.X))
            op(VEC, lambda e: e.drain())
            mark(VEC, "V")
            wt(SCA, "LD", snap[f"x0_{ci}"])
            if ci > 0:
                wt(SCA, "V", snap[f"a3v_{ci - 1}"])
            op(SCA, lambda e: e.activation(SQJ[:], x0_, AF.Square,
                                           accum_out=ST_[:, 1:2]))
            mark(SCA, "S")
            wt(VEC, "S", cnt["S"])
            _ln_chain(eng, E, op, ST_, 0, 1, 6, 2, 3, 4, 5, 1)
            mark(VEC, "V", f"a3v_{ci}")
            op(VEC, lambda e, sl_=sl_, x0_=x0_: e.tensor_scalar(
                LT1[:, sl_ * 1024:(sl_ + 1) * 1024], x0_,
                ST_[:, 6:7], ST_[:, 4:5], ALU.add, ALU.mult))
            mark(VEC, "V", f"lt1_{ci}")

        emit_ln1(0)
        for sc in range(NSC):
            for c in range(2):
                ci = 2 * sc + c
                sl = ci % 2
                x0 = X0[:, sl * 1024:(sl + 1) * 1024]
                LT1c = LT1[:, sl * 1024:(sl + 1) * 1024]
                drain(1)

                if ci == 0:
                    wt(TEN, "LD", snap["consts"])
                wt(TEN, "V", snap[f"lt1_{ci}"])
                if ci > 0:
                    wt(TEN, "S", snap[f"k2f_{ci - 1}"])
                for i in range(8):
                    op(TEN, lambda e, i=i, LT1c=LT1c: e.transpose(
                        psT[:, i * 128:(i + 1) * 128],
                        LT1c[:, i * 128:(i + 1) * 128], IDB[:]))
                mark(TEN, "PE", f"ln1tr_{ci}")
                wt(SCA, "PE", snap[f"ln1tr_{ci}"])
                op(SCA, lambda e: e.copy(L1F[:], psT))
                mark(SCA, "S", f"l1f_{ci}")
                drain(1)

                if ci == 0:
                    wt(TEN, "WL", snap["w_ab"])
                wt(TEN, "S", snap[f"l1f_{ci}"])
                if ci > 0:
                    wt(TEN, "V", snap[f"x1_{ci - 1}"])
                dest = {0: QT, 1: QT, 2: KT, 3: KT, 4: VT, 5: VT}
                for g in range(6):
                    bank = psP[g % 2]
                    if g >= 2:
                        wt(TEN, "S", snap[f"pj_{ci}_{g - 2}"])
                    elif ci > 0:
                        wt(TEN, "S", snap[f"pj_{ci - 1}_{4 + g}"])
                    for i in range(8):
                        op(TEN, lambda e, i=i, g=g, bank=bank: e.matmul(
                            bank, L1F[:, i * 128:(i + 1) * 128],
                            WQKV[:, i * 3072 + g * 512: i * 3072 + (g + 1) * 512],
                            start=(i == 0), stop=(i == 7)))
                    mark(TEN, "PE", f"pjpe_{ci}_{g}")
                    wt(SCA, "PE", snap[f"pjpe_{ci}_{g}"])
                    op(SCA, lambda e, g=g, bank=bank: e.copy(
                        dest[g][:, (g % 2) * 512:((g % 2) + 1) * 512], bank))
                    mark(SCA, "S", f"pj_{ci}_{g}")
                for gg in range(2):
                    out = psAB[gg]
                    for i in range(8):
                        op(TEN, lambda e, i=i, gg=gg, out=out: e.matmul(
                            out, WAB[:, i * 16 + gg * 8: i * 16 + gg * 8 + 8],
                            L1F[:, i * 128:(i + 1) * 128],
                            start=(i == 0), stop=False))
                    op(TEN, lambda e, gg=gg, out=out: e.matmul(
                        out, BAB[0:1, gg * 8:(gg + 1) * 8], ONEB[0:1, 0:128],
                        start=False, stop=True))
                mark(TEN, "PE", f"abpe_{ci}")
                wt(SCA, "PE", snap[f"abpe_{ci}"])
                if ci > 0:
                    wt(SCA, "V", snap[f"a9v_{ci - 1}"])
                op(SCA, lambda e: e.activation(GSIG[:, 0:128], psAB[0], AF.Sigmoid))
                op(SCA, lambda e: e.activation(GSIG[:, 128:256], psAB[1], AF.Sigmoid))
                mark(SCA, "S", f"sig_{ci}")

                wt(VEC, "S", snap[f"sig_{ci}"])
                op(VEC, lambda e: e.tensor_scalar(
                    GBET[:], GSIG[:, 0:128], 0.14900000393390656, 0.85,
                    ALU.mult, ALU.add))
                op(VEC, lambda e: e.tensor_tensor_scan(
                    GD[:], GBET[:], ZER8[:], 1.0, ALU.mult, ALU.add))
                op(VEC, lambda e: e.reciprocal(GRD[:], GD[:]))
                op(VEC, lambda e: e.drain())
                op(VEC, lambda e: e.tensor_scalar(
                    GPK[:, 0:128], GD[:], GRD[:, 127:128], None, ALU.mult))
                op(VEC, lambda e: e.scalar_tensor_tensor(
                    GPK[:, 128:256], GRD[:], GD[:, 127:128], GSIG[:, 128:256],
                    ALU.mult, ALU.mult))
                op(VEC, lambda e: e.tensor_scalar(
                    GPK[:, 256:384], ZER8[:], GD[:, 127:128], None, ALU.add))
                mark(VEC, "V", f"a9v_{ci}")

                wt(TEN, "V", snap[f"a9v_{ci}"])
                wt(TEN, "S", snap[f"sig_{ci}"])
                for j in range(3):
                    op(TEN, lambda e, j=j: e.transpose(
                        psG[:, j * 8:(j + 1) * 8],
                        GPK[0:8, j * 128:(j + 1) * 128], IDF[0:8, 0:8]))
                mark(TEN, "PE", f"gpe_{ci}")
                wt(SCA, "PE", snap[f"gpe_{ci}"])
                op(SCA, lambda e: e.copy(GT[:], psG))
                mark(SCA, "S", f"gt_{ci}")

                wt(VEC, "LD", snap[f"ct_{ci}"])
                wt(VEC, "S", cnt["S"])
                for TSR, gofs, lbl in ((QT, 0, f"qr_{ci}"), (KT, 8, f"kr_{ci}")):
                    op(VEC, lambda e, TSR=TSR: e.tensor_copy(
                        AP(SWQ, 0, [[1024, 128], [128, 8], [1, 64]]),
                        AP(TSR, 64, [[1024, 128], [128, 8], [1, 64]])))
                    op(VEC, lambda e, TSR=TSR: e.tensor_copy(
                        AP(SWQ, 64, [[1024, 128], [128, 8], [1, 64]]),
                        AP(TSR, 0, [[1024, 128], [128, 8], [1, 64]])))
                    op(VEC, lambda e: e.tensor_tensor(
                        SWQ[:], SWQ[:],
                        AP(CT, sl * 256 + 128, [[512, 128], [0, 8], [1, 128]]),
                        ALU.mult))
                    op(VEC, lambda e, TSR=TSR: e.tensor_tensor(
                        TSR[:], TSR[:],
                        AP(CT, sl * 256, [[512, 128], [0, 8], [1, 128]]),
                        ALU.mult))
                    op(VEC, lambda e, TSR=TSR: e.tensor_tensor(
                        TSR[:], TSR[:], SWQ[:], ALU.add))
                    for h in range(H):
                        op(VEC, lambda e, TSR=TSR, h=h, gofs=gofs: e.tensor_scalar(
                            TSR[:, h * 128:(h + 1) * 128],
                            TSR[:, h * 128:(h + 1) * 128],
                            GT[:, gofs + h: gofs + h + 1], None, ALU.mult))
                    mark(VEC, "V", lbl)
                drain(1)

                wt(TEN, "V", snap[f"qr_{ci}"])
                for i in range(8):
                    op(TEN, lambda e, i=i: e.transpose(
                        psT[:, i * 128:(i + 1) * 128],
                        QT[:, i * 128:(i + 1) * 128], IDB[:]))
                mark(TEN, "PE", f"qtr_{ci}")
                wt(VEC, "PE", snap[f"qtr_{ci}"])
                op(VEC, lambda e: e.tensor_copy(Q2F[:], psT))
                mark(VEC, "V", f"q2f_{ci}")
                wt(TEN, "V", max(snap[f"q2f_{ci}"], snap[f"kr_{ci}"]))
                for i in range(8):
                    op(TEN, lambda e, i=i: e.transpose(
                        psT[:, i * 128:(i + 1) * 128],
                        KT[:, i * 128:(i + 1) * 128], IDB[:]))
                mark(TEN, "PE", f"ktr_{ci}")
                wt(SCA, "PE", snap[f"ktr_{ci}"])
                op(SCA, lambda e: e.copy(K2F[:], psT))
                mark(SCA, "S", f"k2f_{ci}")

                for h in range(H):
                    op(VEC, lambda e, h=h: e.tensor_scalar(
                        SH16[:, h * 128:(h + 1) * 128],
                        SST[:, h * 128:(h + 1) * 128],
                        GT[:, 16 + h:17 + h], None, ALU.mult))
                mark(VEC, "V", f"shat_{ci}")
                drain(1)

                wt(TEN, "V", snap[f"q2f_{ci}"])
                wt(TEN, "S", snap[f"k2f_{ci}"])
                if ci > 0:
                    wt(TEN, "V", snap[f"x1_{ci - 1}"])
                for g in range(2):
                    for j, h in enumerate(range(4 * g, 4 * g + 4)):
                        op(TEN, lambda e, g=g, j=j, h=h: e.matmul(
                            psAg[g][:, j * 128:(j + 1) * 128],
                            K2F[:, h * 128:(h + 1) * 128],
                            Q2F[:, h * 128:(h + 1) * 128],
                            start=True, stop=True))
                    mark(TEN, "PE", f"ape_{ci}_{g}")
                for g in range(2):
                    wt(VEC, "PE", snap[f"ape_{ci}_{g}"])
                    op(VEC, lambda e, g=g: e.tensor_tensor(
                        AM[:, g * 512:(g + 1) * 512], psAg[g], AMKb4, ALU.mult))
                    mark(VEC, "V", f"am_{ci}_{g}")
                drain(1)

                def o_group(g):
                    for j, h in enumerate(range(4 * g, 4 * g + 4)):
                        s_ = slice(h * 128, (h + 1) * 128)
                        js = slice(j * 128, (j + 1) * 128)
                        op(TEN, lambda e, g=g, s_=s_, js=js: e.matmul(
                            psOg[g][:, js], VT[:, s_],
                            AM[:, s_], start=True, stop=False))
                        op(TEN, lambda e, g=g, s_=s_, js=js: e.matmul(
                            psOg[g][:, js], SH16[:, s_], Q2F[:, s_],
                            start=False, stop=True))
                    mark(TEN, "PE", f"ope_{ci}_{g}")

                def t_group(g):
                    for j, h in enumerate(range(4 * g, 4 * g + 4)):
                        s_ = slice(h * 128, (h + 1) * 128)
                        js = slice(j * 128, (j + 1) * 128)
                        op(TEN, lambda e, g=g, s_=s_, js=js: e.matmul(
                            psT1g[g][:, js], KT[:, s_], VT[:, s_],
                            start=True, stop=True))
                    mark(TEN, "PE", f"t1pe_{ci}_{g}")

                def s_update(g):
                    wt(VEC, "PE", snap[f"t1pe_{ci}_{g}"])
                    for j, h in enumerate(range(4 * g, 4 * g + 4)):
                        s_ = slice(h * 128, (h + 1) * 128)
                        js = slice(j * 128, (j + 1) * 128)
                        op(VEC, lambda e, g=g, s_=s_, js=js, h=h:
                           e.scalar_tensor_tensor(
                               SST[:, s_], SST[:, s_], GT[:, 16 + h:17 + h],
                               psT1g[g][:, js], ALU.mult, ALU.add))
                    mark(VEC, "V", f"supd_{ci}_{g}")

                def of_evac(g):
                    wt(SCA, "PE", snap[f"ope_{ci}_{g}"])
                    op(SCA, lambda e, g=g: e.copy(
                        OF[:, g * 512:(g + 1) * 512], psOg[g]))
                    mark(SCA, "S", f"of_{ci}_{g}")

                wt(TEN, "V", snap[f"am_{ci}_0"])
                wt(TEN, "S", snap[f"gt_{ci}"])
                o_group(0)
                t_group(0)
                wt(TEN, "V", snap[f"am_{ci}_1"])
                o_group(1)
                of_evac(0)
                s_update(0)
                wt(TEN, "S", snap[f"of_{ci}_0"])
                t_group(1)
                of_evac(1)
                s_update(1)
                if ci + 1 < NCH:
                    emit_ln1(ci + 1)
                drain(1)

                if ci == 0:
                    wt(TEN, "WL", snap["w_wo"])
                wt(TEN, "V", snap[f"supd_{ci}_0"])
                wt(TEN, "S", snap[f"of_{ci}_1"])
                for mh in range(2):
                    for e8 in range(8):
                        op(TEN, lambda e, mh=mh, e8=e8: e.matmul(
                            psW[mh], OF[:, e8 * 128:(e8 + 1) * 128],
                            WO[:, e8 * 1024 + mh * 512: e8 * 1024 + (mh + 1) * 512],
                            start=(e8 == 0), stop=False))
                    op(TEN, lambda e, mh=mh: e.matmul(
                        psW[mh], ONEB[0:1, 0:128], BO[0:1, mh * 512:(mh + 1) * 512],
                        start=False, stop=True))
                mark(TEN, "PE", f"wo_{ci}")

                if sc > 0:
                    drain_to(f"epi_{sc - 1}_{c}")

                wt(VEC, "PE", snap[f"wo_{ci}"])
                if sc > 0:
                    wt(VEC, "ST", snap[f"st_{(sc - 1) * 2 + c}"])
                op(VEC, lambda e, c=c: e.tensor_tensor(
                    X1[:, c * 1024:c * 1024 + 512], psW[0], x0[:, 0:512], ALU.add))
                op(VEC, lambda e, c=c: e.tensor_tensor(
                    X1[:, c * 1024 + 512:(c + 1) * 1024], psW[1],
                    x0[:, 512:1024], ALU.add))
                mark(VEC, "V", f"x1_{ci}")

                op(VEC, lambda e, c=c: e.reduce_sum(
                    ST_[:, 8 + c:9 + c], X1[:, c * 1024:(c + 1) * 1024], axis=AX.X))
                op(VEC, lambda e: e.drain())
                mark(VEC, "V")
                wt(SCA, "V", snap[f"x1_{ci}"])
                op(SCA, lambda e, c=c: e.activation(
                    SQJ[:], X1[:, c * 1024:(c + 1) * 1024], AF.Square,
                    accum_out=ST_[:, 10 + c:11 + c]))
                mark(SCA, "S")

                if ci + 2 < NCH:
                    wt(SYN, "V", snap[f"x1_{ci}"])
                    dma(SYN, "LD", X0[:, sl * 1024:(sl + 1) * 1024],
                        xap[(ci + 2) * 128:(ci + 3) * 128, :], label=f"x0_{ci + 2}")
                    dma(SYN, "LD", CT[:, sl * 256:(sl + 1) * 256],
                        ctap[(ci + 2) * 128:(ci + 3) * 128, :], label=f"ct_{ci + 2}")

            drain_all()
            wt(VEC, "S", cnt["S"])
            for c in range(2):
                _ln_chain(eng, E, op, ST_, 8 + c, 10 + c, 20 + c,
                          12 + c, 14 + c, 16 + c, 18 + c, 1)
                op(VEC, lambda e, c=c: e.tensor_scalar(
                    L2T[:], X1[:, c * 1024:(c + 1) * 1024],
                    ST_[:, 20 + c:21 + c], ST_[:, 16 + c:17 + c],
                    ALU.add, ALU.mult))
                mark(VEC, "V", f"ln2n_{sc}_{c}")
                wt(TEN, "V", snap[f"ln2n_{sc}_{c}"])
                if c == 0 and sc > 0:
                    wt(TEN, "S", snap[f"yb_{sc - 1}_1"])
                for i in range(8):
                    op(TEN, lambda e, i=i: e.transpose(
                        psL[:, i * 128:(i + 1) * 128],
                        L2T[:, i * 128:(i + 1) * 128], IDF[:]))
                mark(TEN, "PE", f"f3pe_{sc}_{c}")
                wt(VEC, "PE", snap[f"f3pe_{sc}_{c}"])
                op(VEC, lambda e, c=c: e.tensor_copy(
                    AP(L2F, c * 128, [[2048, 128], [256, 8], [1, 128]]),
                    AP(PSA, 0, [[1024, 128], [128, 8], [1, 128]])))
                mark(VEC, "V", f"l2f_{sc}_{c}")
            build_ffn_quanta(sc)

        drain_all()
        wt(GPS, "ST", cnt["ST"])

    def _ln_chain(eng, E, op, ST, c_sum, c_sq, c_negm, c_m2, c_var, c_y, c_t, w):
        sl = lambda c: ST[:, c:c + w]
        sli = lambda c: ST[:, c:c + w].bitcast(I32)
        dr = lambda: op("vector", lambda e: e.drain())
        op("vector", lambda e: e.tensor_scalar(
            sl(c_var), sl(c_sq), 1.0 / D, EPS, ALU.mult, ALU.add))
        op("vector", lambda e: e.tensor_scalar_mul(sl(c_negm), sl(c_sum), -1.0 / D))
        dr()
        op("vector", lambda e: e.tensor_scalar(
            sl(c_m2), sl(c_sum), -1.0 / D, sl(c_negm), ALU.mult, ALU.mult))
        dr()
        op("vector", lambda e: e.tensor_scalar(
            sl(c_var), sl(c_var), sl(c_m2), None, ALU.subtract))
        dr()
        op("vector", lambda e: e.tensor_scalar(
            sli(c_y), sli(c_var), 1, 0xFFFFFFFF,
            ALU.arith_shift_right, ALU.bitwise_xor))
        dr()
        op("vector", lambda e: e.tensor_single_scalar(
            sli(c_y), sli(c_y), MAGIC, ALU.add))
        dr()
        for _ in range(2):
            op("vector", lambda e: e.tensor_scalar(
                sl(c_t), sl(c_var), sl(c_y), None, ALU.mult))
            dr()
            op("vector", lambda e: e.tensor_scalar(
                sl(c_t), sl(c_t), sl(c_y), -0.5, ALU.mult, ALU.mult))
            dr()
            op("vector", lambda e: e.tensor_scalar(
                sl(c_t), sl(c_t), 1.5, None, ALU.add))
            dr()
            op("vector", lambda e: e.tensor_scalar(
                sl(c_y), sl(c_y), sl(c_t), None, ALU.mult))
            dr()

    with nc.Block() as block:
        @block.sync
        def _(eng):
            schedule("sync", eng)

        @block.gpsimd
        def _(eng):
            schedule("gpsimd", eng)

        @block.tensor
        def _(eng):
            schedule("tensor", eng)

        @block.vector
        def _(eng):
            schedule("vector", eng)

        @block.scalar
        def _(eng):
            schedule("scalar", eng)

    return nc


_NC_CACHE = [None]


def _host_prep(L0_attn_Wq, L0_attn_Wk, L0_attn_Wv, L0_attn_Wbeta, L0_attn_bbeta,
               L0_attn_Walpha, L0_attn_balpha, L0_attn_Wo, L0_attn_bo,
               L0_ffn_W1, L0_ffn_b1, L0_ffn_W2, L0_ffn_b2):
    c = np.ascontiguousarray

    perm = np.concatenate([np.arange(0, DH, 2), np.arange(1, DH, 2)])
    wq = np.asarray(L0_attn_Wq, dtype=np.float32)
    wk = np.asarray(L0_attn_Wk, dtype=np.float32)
    wq_p = wq.reshape(H, DH, D)[:, perm, :].reshape(D, D)
    wk_p = wk.reshape(H, DH, D)[:, perm, :].reshape(D, D)
    wqkv = c(np.concatenate(
        [wq_p.T, wk_p.T, np.asarray(L0_attn_Wv, np.float32).T], axis=1))
    wab = c(np.concatenate(
        [np.asarray(L0_attn_Wbeta, np.float32).T,
         np.asarray(L0_attn_Walpha, np.float32).T], axis=1))
    bab = c(np.concatenate(
        [np.asarray(L0_attn_bbeta, np.float32),
         np.asarray(L0_attn_balpha, np.float32)]).reshape(1, 16))
    wo = c(np.asarray(L0_attn_Wo, np.float32).T)
    bo = c(np.asarray(L0_attn_bo, np.float32).reshape(1, D))
    w1 = c(np.asarray(L0_ffn_W1, np.float32).T)
    b1 = c(np.asarray(L0_ffn_b1, np.float32).reshape(1, 4 * D))
    w2 = c(np.asarray(L0_ffn_W2, np.float32).T)
    b2 = c(np.asarray(L0_ffn_b2, np.float32).reshape(1, D))

    half = DH // 2
    inv_freq = (np.float32(1.0) /
                np.power(np.float32(10000.0),
                         np.arange(half, dtype=np.float32) / np.float32(half)))
    freqs = np.arange(L, dtype=np.float32)[:, None] * inv_freq[None, :]
    cos = np.cos(freqs).astype(np.float32)
    sin = np.sin(freqs).astype(np.float32)
    ct = c(np.concatenate([cos, cos, -sin, sin], axis=1)).astype(ml_dtypes.bfloat16)

    idb = np.eye(128, dtype=ml_dtypes.bfloat16)
    idf = np.eye(128, dtype=np.float32)
    amk = np.triu(np.ones((128, 128), dtype=np.float32)).astype(ml_dtypes.bfloat16)

    import jax
    with jax.default_device(jax.devices("cpu")[0]):
        keep = np.asarray(jax.random.bernoulli(
            jax.random.key(42), 0.8, (NCORES, L, D)))
    mask = np.where(keep, np.float32(1.25), np.float32(0.0)).astype(ml_dtypes.bfloat16)

    shared = dict(wqkv=wqkv, wab=wab, wo=wo, w1=w1, w2=w2, babi=bab, boi=bo,
                  b1i=b1, b2i=b2, ct=ct, idb=idb, idf=idf, amk=amk)
    return shared, mask


def kernel(L0_attn_x, L0_attn_Wq, L0_attn_Wk, L0_attn_Wv, L0_attn_Wbeta,
           L0_attn_bbeta, L0_attn_Walpha, L0_attn_balpha, L0_attn_Wo,
           L0_attn_bo, L0_ffn_W1, L0_ffn_b1, L0_ffn_W2, L0_ffn_b2):
    shared, mask = _host_prep(
        L0_attn_Wq, L0_attn_Wk, L0_attn_Wv, L0_attn_Wbeta, L0_attn_bbeta,
        L0_attn_Walpha, L0_attn_balpha, L0_attn_Wo, L0_attn_bo,
        L0_ffn_W1, L0_ffn_b1, L0_ffn_W2, L0_ffn_b2)

    x = np.asarray(L0_attn_x, dtype=np.float32)
    in_maps = []
    for b in range(NCORES):
        m = dict(shared)
        m["x"] = np.ascontiguousarray(x[b])
        m["mk"] = np.ascontiguousarray(mask[b])
        in_maps.append(m)

    if _NC_CACHE[0] is None:
        _NC_CACHE[0] = build_program()
    nc = _NC_CACHE[0]
    res = run_bass_kernel_spmd(nc, in_maps, core_ids=list(range(NCORES)))
    out = np.stack([res.results[b]["y_out"] for b in range(NCORES)], axis=0)
    return out.astype(np.float32)


# version 17
# speedup vs baseline: 1.2878x; 1.0220x over previous
import numpy as np
import ml_dtypes
import concourse.bass as bass
import concourse.mybir as mybir
from concourse.bass_utils import run_bass_kernel_spmd
from concourse.ap import AP

F32 = mybir.dt.float32
BF16 = mybir.dt.bfloat16
F8 = mybir.dt.float8e5
I32 = mybir.dt.int32
AF = mybir.ActivationFunctionType
ALU = mybir.AluOpType
AX = mybir.AxisListType

NCORES = 8
L = 4096
D = 1024
H = 8
DH = 128
NCH = 32
NSC = 16
EPS = 1e-5
MAGIC = 0x5F3759DF + 1

_DBG = {"taps": None}


def build_program():
    nc = bass.Bass("TRN2", target_bir_lowering=False, debug=False)

    x_h = nc.dram_tensor("x", [L, D], F32, kind="ExternalInput")
    wqkv_h = nc.dram_tensor("wqkv", [D, 3 * D], F32, kind="ExternalInput")
    wab_h = nc.dram_tensor("wab", [D, 16], F32, kind="ExternalInput")
    wo_h = nc.dram_tensor("wo", [D, D], F32, kind="ExternalInput")
    w1_h = nc.dram_tensor("w1", [D, 4 * D], F32, kind="ExternalInput")
    w2_h = nc.dram_tensor("w2", [4 * D, D], F32, kind="ExternalInput")
    bab_h = nc.dram_tensor("babi", [1, 16], F32, kind="ExternalInput")
    bo_h = nc.dram_tensor("boi", [1, D], F32, kind="ExternalInput")
    b1_h = nc.dram_tensor("b1i", [1, 4 * D], F32, kind="ExternalInput")
    b2_h = nc.dram_tensor("b2i", [1, D], F32, kind="ExternalInput")
    ct_h = nc.dram_tensor("ct", [L, 256], BF16, kind="ExternalInput")
    mk_h = nc.dram_tensor("mk", [L, D], BF16, kind="ExternalInput")
    idb_h = nc.dram_tensor("idb", [128, 128], BF16, kind="ExternalInput")
    idf_h = nc.dram_tensor("idf", [128, 128], F32, kind="ExternalInput")
    amk_h = nc.dram_tensor("amk", [128, 128], BF16, kind="ExternalInput")
    y_h = nc.dram_tensor("y_out", [L, D], F32, kind="ExternalOutput")

    dbg_handles = {}
    if _DBG["taps"]:
        for name, shp in _DBG["taps"].items():
            dbg_handles[name] = nc.dram_tensor("dbg_" + name, shp, F32,
                                               kind="ExternalOutput")

    A = nc.alloc_sbuf_tensor
    WQKV = A("WQKV", [128, 8 * 3072], BF16)
    WAB = A("WAB", [128, 128], BF16)
    WO = A("WO", [128, 8 * 1024], BF16)
    W1T = A("W1T", [128, 8 * 4096], F8)
    W2T = A("W2T", [128, 32 * 1024], F8)
    BAB = A("BAB", [8, 2], BF16)
    BO = A("BO", [1, D], BF16)
    B1 = A("B1", [128, 32], F8)
    B2 = A("B2", [1, D], F8)
    ONEB = A("ONEB", [1, 512], BF16)
    ONE8 = A("ONE8", [1, 512], F8)
    IDB = A("IDB", [128, 128], BF16)
    C128 = A("C128", [128, 128], BF16)
    BOR = A("BOR", [128, 1024], BF16)
    IDF = A("IDF", [128, 128], F32)
    AMK = A("AMK", [128, 128], BF16)

    X0 = A("X0", [128, 2048], F32)
    CT = A("CT", [128, 512], BF16)
    MB = A("MB", [128, 2048], BF16)
    X1 = A("X1", [128, 2048], F32)
    LT1 = A("LT1", [128, 2048], BF16)
    L1F = A("L1F", [128, 1024], BF16)
    QT = A("QT", [128, 1024], BF16)
    KT = A("KT", [128, 1024], BF16)
    VT = A("VT", [128, 1024], BF16)
    SWQ = A("SWQ", [128, 1024], BF16)
    YB = A("YB", [128, 1024], BF16)
    Q2F = A("Q2F", [128, 1024], BF16)
    K2F = A("K2F", [128, 1024], BF16)
    AM = A("AM", [128, 1024], BF16)
    OF = A("OF", [128, 1024], BF16)
    SST = A("SST", [128, 1024], F32)
    SH16 = A("SH16", [128, 1024], BF16)
    GSIG = A("GSIG", [8, 256], F32)
    GBET = A("GBET", [8, 128], F32)
    GD = A("GD", [8, 128], F32)
    GRD = A("GRD", [8, 128], F32)
    GPK = A("GPK", [8, 384], F32)
    GT = A("GT", [128, 24], F32)
    ZER8 = A("ZER8", [8, 128], F32)
    ST_ = A("ST_", [128, 32], F32)
    SQJ = A("SQJ", [128, 1024], F8)
    L2T = A("L2T", [128, 1024], F32)
    L2F = A("L2F", [128, 2048], F8)
    HB = A("HB", [128, 512], BF16)
    H8 = A("H8", [128, 8192], F8)

    PSA = nc.alloc_psum_tensor("PSA", [128, 1024], F32)
    PSB = nc.alloc_psum_tensor("PSB", [128, 1024], F32)
    PSC = nc.alloc_psum_tensor("PSC", [128, 1024], F32)
    PSD = nc.alloc_psum_tensor("PSD", [128, 1024], F32)

    sems = {
        "LD": nc.alloc_semaphore(name="sLD"),
        "WL": nc.alloc_semaphore(name="sWL"),
        "ST": nc.alloc_semaphore(name="sST"),
        "PE": nc.alloc_semaphore(name="sPE"),
        "G": nc.alloc_semaphore(name="sG"),
        "V": nc.alloc_semaphore(name="sV"),
        "S": nc.alloc_semaphore(name="sS"),
    }

    def schedule(E, eng):
        cnt = {k: 0 for k in sems}
        snap = {}
        last = [None]

        def dma(q, sem, out, in_, label=None, **kw):
            cnt[sem] += 16
            if E == q:
                eng.dma_start(out=out, in_=in_, **kw).then_inc(sems[sem], 16)
            if label is not None:
                snap[label] = cnt[sem]

        def wt(q, sem, val):
            if E == q and val > 0:
                eng.wait_ge(sems[sem], val)

        def op(q, fn):
            if E == q:
                last[0] = fn(eng)

        def mark(q, sem, label=None):
            cnt[sem] += 1
            if E == q:
                last[0].then_inc(sems[sem], 1)
            if label is not None:
                snap[label] = cnt[sem]

        TEN, VEC, SCA, SYN, GPS = "tensor", "vector", "scalar", "sync", "gpsimd"

        psL = PSA
        psH = [PSA[:, 0:256], PSA[:, 512:768]]
        psY = PSA
        psT = PSB.bitcast(BF16)[:, 0:1024]
        psP = [PSB[:, 512:1024], PSC[:, 0:512]]
        psAg = [PSC[:, 512:1024], PSD[:, 0:512]]
        psOg = [PSD[:, 512:1024], PSD[:, 0:512]]
        psT1g = [PSC[:, 512:1024], PSD[:, 512:1024]]
        psAB = [PSD[0:8, 512:640], PSD[0:8, 0:128]]
        psG = PSD[:, 512:536]
        psW = [PSC[:, 512:1024], PSD[:, 0:512]]

        Q = []
        qpos = [0]

        def drain(n=1):
            while n > 0 and qpos[0] < len(Q):
                Q[qpos[0]][1]()
                qpos[0] += 1
                n -= 1

        def drain_to(name):
            while qpos[0] < len(Q):
                nm, fn = Q[qpos[0]]
                fn()
                qpos[0] += 1
                if nm == name:
                    return

        def drain_all():
            drain(len(Q) - qpos[0])

        if E == SCA:
            eng.preload_act_table(AF.Sigmoid)
        dma(GPS, "WL", BAB[:], AP(bab_h, 0, [[1, 8], [8, 2]]),
            allow_slow_non_contiguous=True)
        dma(GPS, "WL", BO[:], bo_h.ap(), label="w_bo")
        dma(GPS, "WL", WQKV[:],
            AP(wqkv_h, 0, [[3072, 128], [128 * 3072, 8], [1, 3072]]))
        dma(GPS, "WL", WAB[:],
            AP(wab_h, 0, [[16, 128], [128 * 16, 8], [1, 16]]), label="w_ab")
        dma(GPS, "WL", WO[:],
            AP(wo_h, 0, [[1024, 128], [128 * 1024, 8], [1, 1024]]), label="w_wo")
        dma(GPS, "WL", W1T[:],
            AP(w1_h, 0, [[4096, 128], [128 * 4096, 8], [1, 4096]]))
        dma(GPS, "WL", B1[:], AP(b1_h, 0, [[1, 128], [128, 32]]),
            allow_slow_non_contiguous=True)
        for j in range(4):
            dma(GPS, "WL", W2T[:, j * 8192:(j + 1) * 8192],
                AP(w2_h, j * 8 * 128 * 1024,
                   [[1024, 128], [128 * 1024, 8], [1, 1024]]))
        dma(GPS, "WL", B2[:], b2_h.ap(), label="w_ffn")

        dma(SYN, "LD", IDB[:], idb_h.ap())
        dma(SYN, "LD", IDF[:], idf_h.ap())
        dma(SYN, "LD", AMK[:], amk_h.ap(), label="consts")
        xap = x_h.ap()
        ctap = ct_h.ap()
        mkap = mk_h.ap()
        for k in range(2):
            dma(SYN, "LD", X0[:, k * 1024:(k + 1) * 1024],
                xap[k * 128:(k + 1) * 128, :], label=f"x0_{k}")
            dma(SYN, "LD", CT[:, k * 256:(k + 1) * 256],
                ctap[k * 128:(k + 1) * 128, :], label=f"ct_{k}")
            dma(SYN, "LD", MB[:, k * 1024:(k + 1) * 1024],
                mkap[k * 128:(k + 1) * 128, :], label=f"mk_{k}")

        op(VEC, lambda e: e.memset(SST[:], 0.0))
        op(VEC, lambda e: e.memset(ZER8[:], 0.0))
        op(VEC, lambda e: e.memset(ONEB[:], 1.0))
        op(VEC, lambda e: e.memset(ONE8[:], 1.0))
        op(VEC, lambda e: e.memset(C128[:], 1.0 / 128.0))
        mark(VEC, "V", "vinit")
        wt(TEN, "WL", snap["w_bo"])
        wt(TEN, "V", snap["vinit"])
        for mh in range(2):
            op(TEN, lambda e, mh=mh: e.matmul(
                PSC[:, mh * 512:(mh + 1) * 512], ONEB[0:1, 0:128],
                BO[0:1, mh * 512:(mh + 1) * 512], start=True, stop=True))
        mark(TEN, "PE", "borpe")
        wt(VEC, "PE", snap["borpe"])
        op(VEC, lambda e: e.tensor_copy(BOR[:], PSC[:]))
        mark(VEC, "V", "bor")

        AMKb4 = AP(AMK, 0, [[128, 128], [0, 4], [1, 128]])

        def build_ffn_quanta(s):
            def qW1(k):
                def f():
                    if k == 0:
                        wt(TEN, "V", snap[f"l2f_{s}_1"])
                        if s == 0:
                            wt(TEN, "WL", snap["w_ffn"])
                    for ot in range(4 * k, 4 * k + 4):
                        bank = psH[ot % 2]
                        if ot >= 2:
                            wt(TEN, "S", snap[f"h_{s}_{ot - 2}"])
                        for i in range(8):
                            op(TEN, lambda e, i=i, ot=ot, bank=bank: e.matmul(
                                bank,
                                W1T[:, i * 4096 + ot * 128: i * 4096 + (ot + 1) * 128],
                                L2F[:, i * 256:(i + 1) * 256],
                                start=(i == 0), stop=(i == 7)))
                        mark(TEN, "PE", f"w1pe_{s}_{ot}")
                        wt(SCA, "PE", snap[f"w1pe_{s}_{ot}"])
                        if ot >= 2:
                            wt(SCA, "V", snap[f"h8_{s}_{ot - 2}"])
                        op(SCA, lambda e, ot=ot, bank=bank: e.activation(
                            HB[:, (ot % 2) * 256:((ot % 2) + 1) * 256], bank,
                            AF.Relu, bias=B1[:, ot:ot + 1]))
                        mark(SCA, "S", f"h_{s}_{ot}")
                        wt(VEC, "S", snap[f"h_{s}_{ot}"])
                        op(VEC, lambda e, ot=ot: e.tensor_copy(
                            H8[:, ot * 256:(ot + 1) * 256],
                            HB[:, (ot % 2) * 256:((ot % 2) + 1) * 256]))
                        mark(VEC, "V", f"h8_{s}_{ot}")
                return f

            def qW2(ts, m):
                def f():
                    if m == 0:
                        wt(TEN, "V", snap[f"h8_{s}_31"])
                        if ts == 1:
                            wt(TEN, "S", snap[f"yb_{s}_0"])
                    for og in range(4 * m, 4 * m + 4):
                        for eh in range(2):
                            op(TEN, lambda e, og=og, eh=eh, ts=ts: e.matmul(
                                psY[:, eh * 512:(eh + 1) * 512],
                                H8[:, og * 256 + ts * 128: og * 256 + (ts + 1) * 128],
                                W2T[:, og * 1024 + eh * 512: og * 1024 + (eh + 1) * 512],
                                start=(og == 0), stop=False))
                    if m == 7:
                        for eh in range(2):
                            op(TEN, lambda e, eh=eh: e.matmul(
                                psY[:, eh * 512:(eh + 1) * 512],
                                ONE8[0:1, 0:128], B2[0:1, eh * 512:(eh + 1) * 512],
                                start=False, stop=True))
                        mark(TEN, "PE", f"w2_{s}_{ts}")
                return f

            def qEpi(ts):
                def f():
                    wt(SCA, "PE", snap[f"w2_{s}_{ts}"])
                    if ts == 1:
                        wt(SCA, "V", snap[f"f8_{s}_0"])
                    op(SCA, lambda e: e.copy(YB[:], psY[:]))
                    mark(SCA, "S", f"yb_{s}_{ts}")
                    wt(VEC, "S", snap[f"yb_{s}_{ts}"])
                    wt(VEC, "LD", snap[f"mk_{s * 2 + ts}"])
                    op(VEC, lambda e, ts=ts: e.tensor_tensor(
                        L2T[:], YB[:], MB[:, ts * 1024:(ts + 1) * 1024], ALU.mult))
                    op(VEC, lambda e, ts=ts: e.tensor_tensor(
                        X1[:, ts * 1024:(ts + 1) * 1024], L2T[:],
                        X1[:, ts * 1024:(ts + 1) * 1024], ALU.add))
                    mark(VEC, "V", f"f8_{s}_{ts}")
                    wt(GPS, "V", snap[f"f8_{s}_{ts}"])
                    dma(GPS, "ST",
                        y_h.ap()[(s * 2 + ts) * 128:(s * 2 + ts + 1) * 128, :],
                        X1[:, ts * 1024:(ts + 1) * 1024], label=f"st_{s * 2 + ts}")
                    if s + 1 < NSC:
                        wt(SYN, "V", snap[f"f8_{s}_{ts}"])
                        dma(SYN, "LD", MB[:, ts * 1024:(ts + 1) * 1024],
                            mkap[((s + 1) * 2 + ts) * 128:((s + 1) * 2 + ts + 1) * 128, :],
                            label=f"mk_{(s + 1) * 2 + ts}")
                return f

            for k in range(8):
                Q.append((f"w1_{s}_{k}", qW1(k)))
            for m in range(8):
                Q.append((f"w2_{s}_0_{m}", qW2(0, m)))
            Q.append((f"epi_{s}_0", qEpi(0)))
            for m in range(8):
                Q.append((f"w2_{s}_1_{m}", qW2(1, m)))
            Q.append((f"epi_{s}_1", qEpi(1)))

        def emit_ln1(ci):
            sl_ = ci % 2
            x0_ = X0[:, sl_ * 1024:(sl_ + 1) * 1024]
            wt(VEC, "LD", snap[f"x0_{ci}"])
            op(VEC, lambda e: e.reduce_sum(ST_[:, 0:1], x0_, axis=AX.X))
            op(VEC, lambda e: e.drain())
            mark(VEC, "V")
            wt(SCA, "LD", snap[f"x0_{ci}"])
            if ci > 0:
                wt(SCA, "V", snap[f"a3v_{ci - 1}"])
            op(SCA, lambda e: e.activation(SQJ[:], x0_, AF.Square,
                                           accum_out=ST_[:, 1:2]))
            mark(SCA, "S")
            wt(VEC, "S", cnt["S"])
            _ln_chain(eng, E, op, ST_, 0, 1, 6, 2, 3, 4, 5, 1)
            mark(VEC, "V", f"a3v_{ci}")
            op(VEC, lambda e, sl_=sl_, x0_=x0_: e.tensor_scalar(
                LT1[:, sl_ * 1024:(sl_ + 1) * 1024], x0_,
                ST_[:, 6:7], ST_[:, 4:5], ALU.add, ALU.mult))
            mark(VEC, "V", f"lt1_{ci}")

        emit_ln1(0)
        for sc in range(NSC):
            for c in range(2):
                ci = 2 * sc + c
                sl = ci % 2
                x0 = X0[:, sl * 1024:(sl + 1) * 1024]
                LT1c = LT1[:, sl * 1024:(sl + 1) * 1024]
                drain(1)

                if ci == 0:
                    wt(TEN, "LD", snap["consts"])
                wt(TEN, "V", snap[f"lt1_{ci}"])
                if ci > 0:
                    wt(TEN, "S", snap[f"k2f_{ci - 1}"])
                for i in range(8):
                    op(TEN, lambda e, i=i, LT1c=LT1c: e.transpose(
                        psT[:, i * 128:(i + 1) * 128],
                        LT1c[:, i * 128:(i + 1) * 128], IDB[:]))
                mark(TEN, "PE", f"ln1tr_{ci}")
                wt(SCA, "PE", snap[f"ln1tr_{ci}"])
                op(SCA, lambda e: e.copy(L1F[:], psT))
                mark(SCA, "S", f"l1f_{ci}")
                drain(1)

                if ci == 0:
                    wt(TEN, "WL", snap["w_ab"])
                wt(TEN, "S", snap[f"l1f_{ci}"])
                if ci > 0:
                    wt(TEN, "V", snap[f"x1_{ci - 1}"])
                dest = {0: QT, 1: QT, 2: KT, 3: KT, 4: VT, 5: VT}
                for g in range(6):
                    bank = psP[g % 2]
                    if g >= 2:
                        wt(TEN, "S", snap[f"pj_{ci}_{g - 2}"])
                    elif ci > 0:
                        wt(TEN, "S", snap[f"pj_{ci - 1}_{4 + g}"])
                    else:
                        wt(TEN, "V", snap["bor"])
                    for i in range(8):
                        op(TEN, lambda e, i=i, g=g, bank=bank: e.matmul(
                            bank, L1F[:, i * 128:(i + 1) * 128],
                            WQKV[:, i * 3072 + g * 512: i * 3072 + (g + 1) * 512],
                            start=(i == 0), stop=(i == 7)))
                    mark(TEN, "PE", f"pjpe_{ci}_{g}")
                    wt(SCA, "PE", snap[f"pjpe_{ci}_{g}"])
                    op(SCA, lambda e, g=g, bank=bank: e.copy(
                        dest[g][:, (g % 2) * 512:((g % 2) + 1) * 512], bank))
                    mark(SCA, "S", f"pj_{ci}_{g}")
                for gg in range(2):
                    out = psAB[gg]
                    for i in range(8):
                        op(TEN, lambda e, i=i, gg=gg, out=out: e.matmul(
                            out, WAB[:, i * 16 + gg * 8: i * 16 + gg * 8 + 8],
                            L1F[:, i * 128:(i + 1) * 128],
                            start=(i == 0), stop=(i == 7)))
                mark(TEN, "PE", f"abpe_{ci}")
                wt(SCA, "PE", snap[f"abpe_{ci}"])
                if ci > 0:
                    wt(SCA, "V", snap[f"a9v_{ci - 1}"])
                op(SCA, lambda e: e.activation(GSIG[:, 0:128], psAB[0], AF.Sigmoid,
                                               bias=BAB[:, 0:1]))
                op(SCA, lambda e: e.activation(GSIG[:, 128:256], psAB[1], AF.Sigmoid,
                                               bias=BAB[:, 1:2]))
                mark(SCA, "S", f"sig_{ci}")

                wt(VEC, "S", snap[f"sig_{ci}"])
                op(VEC, lambda e: e.tensor_scalar(
                    GBET[:], GSIG[:, 0:128], 0.14900000393390656, 0.85,
                    ALU.mult, ALU.add))
                op(VEC, lambda e: e.tensor_tensor_scan(
                    GD[:], GBET[:], ZER8[:], 1.0, ALU.mult, ALU.add))
                op(VEC, lambda e: e.reciprocal(GRD[:], GD[:]))
                op(VEC, lambda e: e.drain())
                op(VEC, lambda e: e.tensor_scalar(
                    GPK[:, 0:128], GD[:], GRD[:, 127:128], None, ALU.mult))
                op(VEC, lambda e: e.scalar_tensor_tensor(
                    GPK[:, 128:256], GRD[:], GD[:, 127:128], GSIG[:, 128:256],
                    ALU.mult, ALU.mult))
                op(VEC, lambda e: e.tensor_scalar(
                    GPK[:, 256:384], ZER8[:], GD[:, 127:128], None, ALU.add))
                mark(VEC, "V", f"a9v_{ci}")

                wt(TEN, "V", snap[f"a9v_{ci}"])
                wt(TEN, "S", snap[f"sig_{ci}"])
                for j in range(3):
                    op(TEN, lambda e, j=j: e.transpose(
                        psG[:, j * 8:(j + 1) * 8],
                        GPK[0:8, j * 128:(j + 1) * 128], IDF[0:8, 0:8]))
                mark(TEN, "PE", f"gpe_{ci}")
                wt(SCA, "PE", snap[f"gpe_{ci}"])
                op(SCA, lambda e: e.copy(GT[:], psG))
                mark(SCA, "S", f"gt_{ci}")

                wt(VEC, "LD", snap[f"ct_{ci}"])
                wt(VEC, "S", cnt["S"])
                for TSR, gofs, lbl in ((QT, 0, f"qr_{ci}"), (KT, 8, f"kr_{ci}")):
                    op(VEC, lambda e, TSR=TSR: e.tensor_copy(
                        AP(SWQ, 0, [[1024, 128], [128, 8], [1, 64]]),
                        AP(TSR, 64, [[1024, 128], [128, 8], [1, 64]])))
                    op(VEC, lambda e, TSR=TSR: e.tensor_copy(
                        AP(SWQ, 64, [[1024, 128], [128, 8], [1, 64]]),
                        AP(TSR, 0, [[1024, 128], [128, 8], [1, 64]])))
                    op(VEC, lambda e: e.tensor_tensor(
                        SWQ[:], SWQ[:],
                        AP(CT, sl * 256 + 128, [[512, 128], [0, 8], [1, 128]]),
                        ALU.mult))
                    op(VEC, lambda e, TSR=TSR: e.tensor_tensor(
                        TSR[:], TSR[:],
                        AP(CT, sl * 256, [[512, 128], [0, 8], [1, 128]]),
                        ALU.mult))
                    op(VEC, lambda e, TSR=TSR: e.tensor_tensor(
                        TSR[:], TSR[:], SWQ[:], ALU.add))
                    for h in range(H):
                        op(VEC, lambda e, TSR=TSR, h=h, gofs=gofs: e.tensor_scalar(
                            TSR[:, h * 128:(h + 1) * 128],
                            TSR[:, h * 128:(h + 1) * 128],
                            GT[:, gofs + h: gofs + h + 1], None, ALU.mult))
                    mark(VEC, "V", lbl)
                drain(1)

                wt(TEN, "V", snap[f"qr_{ci}"])
                for i in range(8):
                    op(TEN, lambda e, i=i: e.transpose(
                        psT[:, i * 128:(i + 1) * 128],
                        QT[:, i * 128:(i + 1) * 128], IDB[:]))
                mark(TEN, "PE", f"qtr_{ci}")
                wt(VEC, "PE", snap[f"qtr_{ci}"])
                op(VEC, lambda e: e.tensor_copy(Q2F[:], psT))
                mark(VEC, "V", f"q2f_{ci}")
                wt(TEN, "V", max(snap[f"q2f_{ci}"], snap[f"kr_{ci}"]))
                for i in range(8):
                    op(TEN, lambda e, i=i: e.transpose(
                        psT[:, i * 128:(i + 1) * 128],
                        KT[:, i * 128:(i + 1) * 128], IDB[:]))
                mark(TEN, "PE", f"ktr_{ci}")
                wt(SCA, "PE", snap[f"ktr_{ci}"])
                op(SCA, lambda e: e.copy(K2F[:], psT))
                mark(SCA, "S", f"k2f_{ci}")

                for h in range(H):
                    op(VEC, lambda e, h=h: e.tensor_scalar(
                        SH16[:, h * 128:(h + 1) * 128],
                        SST[:, h * 128:(h + 1) * 128],
                        GT[:, 16 + h:17 + h], None, ALU.mult))
                mark(VEC, "V", f"shat_{ci}")
                drain(1)

                wt(TEN, "V", snap[f"q2f_{ci}"])
                wt(TEN, "S", snap[f"k2f_{ci}"])
                wt(TEN, "V", snap[f"x1_{ci - 1}"] if ci > 0 else snap["bor"])
                for g in range(2):
                    for j, h in enumerate(range(4 * g, 4 * g + 4)):
                        op(TEN, lambda e, g=g, j=j, h=h: e.matmul(
                            psAg[g][:, j * 128:(j + 1) * 128],
                            K2F[:, h * 128:(h + 1) * 128],
                            Q2F[:, h * 128:(h + 1) * 128],
                            start=True, stop=True))
                    mark(TEN, "PE", f"ape_{ci}_{g}")
                for g in range(2):
                    wt(VEC, "PE", snap[f"ape_{ci}_{g}"])
                    op(VEC, lambda e, g=g: e.tensor_tensor(
                        AM[:, g * 512:(g + 1) * 512], psAg[g], AMKb4, ALU.mult))
                    mark(VEC, "V", f"am_{ci}_{g}")
                drain(1)

                def o_group(g):
                    for j, h in enumerate(range(4 * g, 4 * g + 4)):
                        s_ = slice(h * 128, (h + 1) * 128)
                        js = slice(j * 128, (j + 1) * 128)
                        op(TEN, lambda e, g=g, s_=s_, js=js: e.matmul(
                            psOg[g][:, js], VT[:, s_],
                            AM[:, s_], start=True, stop=False))
                        op(TEN, lambda e, g=g, s_=s_, js=js: e.matmul(
                            psOg[g][:, js], SH16[:, s_], Q2F[:, s_],
                            start=False, stop=True))
                    mark(TEN, "PE", f"ope_{ci}_{g}")

                def t_group(g):
                    for j, h in enumerate(range(4 * g, 4 * g + 4)):
                        s_ = slice(h * 128, (h + 1) * 128)
                        js = slice(j * 128, (j + 1) * 128)
                        op(TEN, lambda e, g=g, s_=s_, js=js: e.matmul(
                            psT1g[g][:, js], KT[:, s_], VT[:, s_],
                            start=True, stop=True))
                    mark(TEN, "PE", f"t1pe_{ci}_{g}")

                def s_update(g):
                    wt(VEC, "PE", snap[f"t1pe_{ci}_{g}"])
                    for j, h in enumerate(range(4 * g, 4 * g + 4)):
                        s_ = slice(h * 128, (h + 1) * 128)
                        js = slice(j * 128, (j + 1) * 128)
                        op(VEC, lambda e, g=g, s_=s_, js=js, h=h:
                           e.scalar_tensor_tensor(
                               SST[:, s_], SST[:, s_], GT[:, 16 + h:17 + h],
                               psT1g[g][:, js], ALU.mult, ALU.add))
                    mark(VEC, "V", f"supd_{ci}_{g}")

                def of_evac(g):
                    wt(SCA, "PE", snap[f"ope_{ci}_{g}"])
                    op(SCA, lambda e, g=g: e.copy(
                        OF[:, g * 512:(g + 1) * 512], psOg[g]))
                    mark(SCA, "S", f"of_{ci}_{g}")

                wt(TEN, "V", snap[f"am_{ci}_0"])
                wt(TEN, "S", snap[f"gt_{ci}"])
                o_group(0)
                t_group(0)
                wt(TEN, "V", snap[f"am_{ci}_1"])
                o_group(1)
                of_evac(0)
                s_update(0)
                wt(TEN, "S", snap[f"of_{ci}_0"])
                t_group(1)
                of_evac(1)
                s_update(1)
                if ci + 1 < NCH:
                    emit_ln1(ci + 1)
                if c == 1:
                    _ln_chain(eng, E, op, ST_, 8, 10, 20, 12, 14, 16, 18, 1)
                    mark(VEC, "V")
                drain(1)

                if ci == 0:
                    wt(TEN, "WL", snap["w_wo"])
                wt(TEN, "V", snap[f"supd_{ci}_0"])
                wt(TEN, "S", snap[f"of_{ci}_1"])
                for mh in range(2):
                    for e8 in range(8):
                        op(TEN, lambda e, mh=mh, e8=e8: e.matmul(
                            psW[mh], OF[:, e8 * 128:(e8 + 1) * 128],
                            WO[:, e8 * 1024 + mh * 512: e8 * 1024 + (mh + 1) * 512],
                            start=(e8 == 0), stop=False))
                    op(TEN, lambda e, mh=mh: e.matmul(
                        psW[mh], C128[:], BOR[:, mh * 512:(mh + 1) * 512],
                        start=False, stop=True))
                mark(TEN, "PE", f"wo_{ci}")

                if sc > 0:
                    drain_to(f"epi_{sc - 1}_{c}")

                wt(VEC, "PE", snap[f"wo_{ci}"])
                if sc > 0:
                    wt(VEC, "ST", snap[f"st_{(sc - 1) * 2 + c}"])
                op(VEC, lambda e, c=c: e.tensor_tensor(
                    X1[:, c * 1024:c * 1024 + 512], psW[0], x0[:, 0:512], ALU.add))
                op(VEC, lambda e, c=c: e.tensor_tensor(
                    X1[:, c * 1024 + 512:(c + 1) * 1024], psW[1],
                    x0[:, 512:1024], ALU.add))
                mark(VEC, "V", f"x1_{ci}")

                op(VEC, lambda e, c=c: e.reduce_sum(
                    ST_[:, 8 + c:9 + c], X1[:, c * 1024:(c + 1) * 1024], axis=AX.X))
                op(VEC, lambda e: e.drain())
                mark(VEC, "V")
                wt(SCA, "V", snap[f"x1_{ci}"])
                op(SCA, lambda e, c=c: e.activation(
                    SQJ[:], X1[:, c * 1024:(c + 1) * 1024], AF.Square,
                    accum_out=ST_[:, 10 + c:11 + c]))
                mark(SCA, "S", f"fsq_{ci}")
                if c == 1:
                    wt(VEC, "S", snap[f"fsq_{ci}"])
                    _ln_chain(eng, E, op, ST_, 9, 11, 21, 13, 15, 17, 19, 1)
                    mark(VEC, "V")

                if ci + 2 < NCH:
                    wt(SYN, "V", snap[f"x1_{ci}"])
                    dma(SYN, "LD", X0[:, sl * 1024:(sl + 1) * 1024],
                        xap[(ci + 2) * 128:(ci + 3) * 128, :], label=f"x0_{ci + 2}")
                    dma(SYN, "LD", CT[:, sl * 256:(sl + 1) * 256],
                        ctap[(ci + 2) * 128:(ci + 3) * 128, :], label=f"ct_{ci + 2}")

            drain_all()
            wt(VEC, "S", cnt["S"])
            for c in range(2):
                op(VEC, lambda e, c=c: e.tensor_scalar(
                    L2T[:], X1[:, c * 1024:(c + 1) * 1024],
                    ST_[:, 20 + c:21 + c], ST_[:, 16 + c:17 + c],
                    ALU.add, ALU.mult))
                mark(VEC, "V", f"ln2n_{sc}_{c}")
                wt(TEN, "V", snap[f"ln2n_{sc}_{c}"])
                if c == 0 and sc > 0:
                    wt(TEN, "S", snap[f"yb_{sc - 1}_1"])
                for i in range(8):
                    op(TEN, lambda e, i=i: e.transpose(
                        psL[:, i * 128:(i + 1) * 128],
                        L2T[:, i * 128:(i + 1) * 128], IDF[:]))
                mark(TEN, "PE", f"f3pe_{sc}_{c}")
                wt(VEC, "PE", snap[f"f3pe_{sc}_{c}"])
                op(VEC, lambda e, c=c: e.tensor_copy(
                    AP(L2F, c * 128, [[2048, 128], [256, 8], [1, 128]]),
                    AP(PSA, 0, [[1024, 128], [128, 8], [1, 128]])))
                mark(VEC, "V", f"l2f_{sc}_{c}")
            build_ffn_quanta(sc)

        drain_all()
        wt(GPS, "ST", cnt["ST"])

    def _ln_chain(eng, E, op, ST, c_sum, c_sq, c_negm, c_m2, c_var, c_y, c_t, w):
        sl = lambda c: ST[:, c:c + w]
        sli = lambda c: ST[:, c:c + w].bitcast(I32)
        dr = lambda: op("vector", lambda e: e.drain())
        op("vector", lambda e: e.tensor_scalar(
            sl(c_var), sl(c_sq), 1.0 / D, EPS, ALU.mult, ALU.add))
        op("vector", lambda e: e.tensor_scalar_mul(sl(c_negm), sl(c_sum), -1.0 / D))
        dr()
        op("vector", lambda e: e.tensor_scalar(
            sl(c_m2), sl(c_sum), -1.0 / D, sl(c_negm), ALU.mult, ALU.mult))
        dr()
        op("vector", lambda e: e.tensor_scalar(
            sl(c_var), sl(c_var), sl(c_m2), None, ALU.subtract))
        dr()
        op("vector", lambda e: e.tensor_scalar(
            sli(c_y), sli(c_var), 1, 0xFFFFFFFF,
            ALU.arith_shift_right, ALU.bitwise_xor))
        dr()
        op("vector", lambda e: e.tensor_single_scalar(
            sli(c_y), sli(c_y), MAGIC, ALU.add))
        dr()
        for _ in range(2):
            op("vector", lambda e: e.tensor_scalar(
                sl(c_t), sl(c_var), sl(c_y), None, ALU.mult))
            dr()
            op("vector", lambda e: e.tensor_scalar(
                sl(c_t), sl(c_t), sl(c_y), -0.5, ALU.mult, ALU.mult))
            dr()
            op("vector", lambda e: e.tensor_scalar(
                sl(c_t), sl(c_t), 1.5, None, ALU.add))
            dr()
            op("vector", lambda e: e.tensor_scalar(
                sl(c_y), sl(c_y), sl(c_t), None, ALU.mult))
            dr()

    with nc.Block() as block:
        @block.sync
        def _(eng):
            schedule("sync", eng)

        @block.gpsimd
        def _(eng):
            schedule("gpsimd", eng)

        @block.tensor
        def _(eng):
            schedule("tensor", eng)

        @block.vector
        def _(eng):
            schedule("vector", eng)

        @block.scalar
        def _(eng):
            schedule("scalar", eng)

    return nc


_NC_CACHE = [None]


def _host_prep(L0_attn_Wq, L0_attn_Wk, L0_attn_Wv, L0_attn_Wbeta, L0_attn_bbeta,
               L0_attn_Walpha, L0_attn_balpha, L0_attn_Wo, L0_attn_bo,
               L0_ffn_W1, L0_ffn_b1, L0_ffn_W2, L0_ffn_b2):
    c = np.ascontiguousarray

    perm = np.concatenate([np.arange(0, DH, 2), np.arange(1, DH, 2)])
    wq = np.asarray(L0_attn_Wq, dtype=np.float32)
    wk = np.asarray(L0_attn_Wk, dtype=np.float32)
    wq_p = wq.reshape(H, DH, D)[:, perm, :].reshape(D, D)
    wk_p = wk.reshape(H, DH, D)[:, perm, :].reshape(D, D)
    wqkv = c(np.concatenate(
        [wq_p.T, wk_p.T, np.asarray(L0_attn_Wv, np.float32).T], axis=1))
    wab = c(np.concatenate(
        [np.asarray(L0_attn_Wbeta, np.float32).T,
         np.asarray(L0_attn_Walpha, np.float32).T], axis=1))
    bab = c(np.concatenate(
        [np.asarray(L0_attn_bbeta, np.float32),
         np.asarray(L0_attn_balpha, np.float32)]).reshape(1, 16))
    wo = c(np.asarray(L0_attn_Wo, np.float32).T)
    bo = c(np.asarray(L0_attn_bo, np.float32).reshape(1, D))
    w1 = c(np.asarray(L0_ffn_W1, np.float32).T)
    b1 = c(np.asarray(L0_ffn_b1, np.float32).reshape(1, 4 * D))
    w2 = c(np.asarray(L0_ffn_W2, np.float32).T)
    b2 = c(np.asarray(L0_ffn_b2, np.float32).reshape(1, D))

    half = DH // 2
    inv_freq = (np.float32(1.0) /
                np.power(np.float32(10000.0),
                         np.arange(half, dtype=np.float32) / np.float32(half)))
    freqs = np.arange(L, dtype=np.float32)[:, None] * inv_freq[None, :]
    cos = np.cos(freqs).astype(np.float32)
    sin = np.sin(freqs).astype(np.float32)
    ct = c(np.concatenate([cos, cos, -sin, sin], axis=1)).astype(ml_dtypes.bfloat16)

    idb = np.eye(128, dtype=ml_dtypes.bfloat16)
    idf = np.eye(128, dtype=np.float32)
    amk = np.triu(np.ones((128, 128), dtype=np.float32)).astype(ml_dtypes.bfloat16)

    import jax
    with jax.default_device(jax.devices("cpu")[0]):
        keep = np.asarray(jax.random.bernoulli(
            jax.random.key(42), 0.8, (NCORES, L, D)))
    mask = np.where(keep, np.float32(1.25), np.float32(0.0)).astype(ml_dtypes.bfloat16)

    shared = dict(wqkv=wqkv, wab=wab, wo=wo, w1=w1, w2=w2, babi=bab, boi=bo,
                  b1i=b1, b2i=b2, ct=ct, idb=idb, idf=idf, amk=amk)
    return shared, mask


def kernel(L0_attn_x, L0_attn_Wq, L0_attn_Wk, L0_attn_Wv, L0_attn_Wbeta,
           L0_attn_bbeta, L0_attn_Walpha, L0_attn_balpha, L0_attn_Wo,
           L0_attn_bo, L0_ffn_W1, L0_ffn_b1, L0_ffn_W2, L0_ffn_b2):
    shared, mask = _host_prep(
        L0_attn_Wq, L0_attn_Wk, L0_attn_Wv, L0_attn_Wbeta, L0_attn_bbeta,
        L0_attn_Walpha, L0_attn_balpha, L0_attn_Wo, L0_attn_bo,
        L0_ffn_W1, L0_ffn_b1, L0_ffn_W2, L0_ffn_b2)

    x = np.asarray(L0_attn_x, dtype=np.float32)
    in_maps = []
    for b in range(NCORES):
        m = dict(shared)
        m["x"] = np.ascontiguousarray(x[b])
        m["mk"] = np.ascontiguousarray(mask[b])
        in_maps.append(m)

    if _NC_CACHE[0] is None:
        _NC_CACHE[0] = build_program()
    nc = _NC_CACHE[0]
    res = run_bass_kernel_spmd(nc, in_maps, core_ids=list(range(NCORES)))
    out = np.stack([res.results[b]["y_out"] for b in range(NCORES)], axis=0)
    return out.astype(np.float32)


# version 20
# speedup vs baseline: 1.3358x; 1.0373x over previous
import numpy as np
import ml_dtypes
import concourse.bass as bass
import concourse.mybir as mybir
from concourse.bass_utils import run_bass_kernel_spmd
from concourse.ap import AP

F32 = mybir.dt.float32
BF16 = mybir.dt.bfloat16
F8 = mybir.dt.float8e5
I32 = mybir.dt.int32
AF = mybir.ActivationFunctionType
ALU = mybir.AluOpType
AX = mybir.AxisListType

NCORES = 8
L = 4096
D = 1024
H = 8
DH = 128
NCH = 32
NSC = 16
EPS = 1e-5
MAGIC = 0x5F3759DF + 1

_DBG = {"taps": None}


def build_program():
    nc = bass.Bass("TRN2", target_bir_lowering=False, debug=False)

    x_h = nc.dram_tensor("x", [L, D], F32, kind="ExternalInput")
    wqkv_h = nc.dram_tensor("wqkv", [D, 3 * D], F32, kind="ExternalInput")
    wab_h = nc.dram_tensor("wab", [D, 16], F32, kind="ExternalInput")
    wo_h = nc.dram_tensor("wo", [D, D], F32, kind="ExternalInput")
    w1_h = nc.dram_tensor("w1", [D, 4 * D], F32, kind="ExternalInput")
    w2_h = nc.dram_tensor("w2", [4 * D, D], F32, kind="ExternalInput")
    bab_h = nc.dram_tensor("babi", [1, 16], F32, kind="ExternalInput")
    bo_h = nc.dram_tensor("boi", [1, D], F32, kind="ExternalInput")
    b1_h = nc.dram_tensor("b1i", [1, 4 * D], F32, kind="ExternalInput")
    b2_h = nc.dram_tensor("b2i", [1, D], F32, kind="ExternalInput")
    ct_h = nc.dram_tensor("ct", [L, 256], BF16, kind="ExternalInput")
    mk_h = nc.dram_tensor("mk", [L, D], BF16, kind="ExternalInput")
    idb_h = nc.dram_tensor("idb", [128, 128], BF16, kind="ExternalInput")
    idf_h = nc.dram_tensor("idf", [128, 128], F32, kind="ExternalInput")
    amk_h = nc.dram_tensor("amk", [128, 128], BF16, kind="ExternalInput")
    y_h = nc.dram_tensor("y_out", [L, D], F32, kind="ExternalOutput")

    dbg_handles = {}
    if _DBG["taps"]:
        for name, shp in _DBG["taps"].items():
            dbg_handles[name] = nc.dram_tensor("dbg_" + name, shp, F32,
                                               kind="ExternalOutput")

    A = nc.alloc_sbuf_tensor
    WQKV = A("WQKV", [128, 8 * 3072], BF16)
    WAB = A("WAB", [128, 128], BF16)
    WO = A("WO", [128, 8 * 1024], BF16)
    W1T = A("W1T", [128, 8 * 4096], F8)
    W2T = A("W2T", [128, 32 * 1024], F8)
    BAB = A("BAB", [8, 2], BF16)
    BO = A("BO", [1, D], BF16)
    B1 = A("B1", [128, 32], F8)
    B2 = A("B2", [1, D], F8)
    ONEB = A("ONEB", [1, 512], BF16)
    ONE8 = A("ONE8", [1, 512], F8)
    IDB = A("IDB", [128, 128], BF16)
    C128 = A("C128", [128, 128], BF16)
    BOR = A("BOR", [128, 1024], BF16)
    IDF = A("IDF", [128, 128], F32)
    AMK = A("AMK", [128, 128], BF16)

    X0 = A("X0", [128, 2048], F32)
    CT = A("CT", [128, 512], BF16)
    MB = A("MB", [128, 2048], BF16)
    X1 = A("X1", [128, 2048], F32)
    LT1 = A("LT1", [128, 2048], BF16)
    L1F = A("L1F", [128, 1024], BF16)
    QT = A("QT", [128, 1024], BF16)
    KT = A("KT", [128, 1024], BF16)
    VT = A("VT", [128, 1024], BF16)
    SWQ = A("SWQ", [128, 1024], BF16)
    YB = A("YB", [128, 1024], BF16)
    Q2F = A("Q2F", [128, 1024], BF16)
    K2F = A("K2F", [128, 1024], BF16)
    AM = A("AM", [128, 1024], BF16)
    OF = A("OF", [128, 1024], BF16)
    SST = A("SST", [128, 1024], F32)
    SH16 = A("SH16", [128, 1024], BF16)
    GSIG = A("GSIG", [8, 256], F32)
    GBET = A("GBET", [8, 128], F32)
    GD = A("GD", [8, 128], F32)
    GRD = A("GRD", [8, 128], F32)
    GPK = A("GPK", [8, 384], F32)
    GT = A("GT", [128, 24], F32)
    ZER8 = A("ZER8", [8, 128], F32)
    ST_ = A("ST_", [128, 32], F32)
    SQJ = A("SQJ", [128, 1024], F8)
    L2T = A("L2T", [128, 1024], F32)
    L2F = A("L2F", [128, 2048], F8)
    HB = A("HB", [128, 512], BF16)
    H8 = A("H8", [128, 8192], F8)

    PSA = nc.alloc_psum_tensor("PSA", [128, 1024], F32)
    PSB = nc.alloc_psum_tensor("PSB", [128, 1024], F32)
    PSC = nc.alloc_psum_tensor("PSC", [128, 1024], F32)
    PSD = nc.alloc_psum_tensor("PSD", [128, 1024], F32)

    sems = {
        "LD": nc.alloc_semaphore(name="sLD"),
        "WL": nc.alloc_semaphore(name="sWL"),
        "ST": nc.alloc_semaphore(name="sST"),
        "PE": nc.alloc_semaphore(name="sPE"),
        "G": nc.alloc_semaphore(name="sG"),
        "V": nc.alloc_semaphore(name="sV"),
        "S": nc.alloc_semaphore(name="sS"),
    }

    def schedule(E, eng):
        cnt = {k: 0 for k in sems}
        snap = {}
        last = [None]

        def dma(q, sem, out, in_, label=None, **kw):
            cnt[sem] += 16
            if E == q:
                eng.dma_start(out=out, in_=in_, **kw).then_inc(sems[sem], 16)
            if label is not None:
                snap[label] = cnt[sem]

        def wt(q, sem, val):
            if E == q and val > 0:
                eng.wait_ge(sems[sem], val)

        def op(q, fn):
            if E == q:
                last[0] = fn(eng)

        def mark(q, sem, label=None):
            cnt[sem] += 1
            if E == q:
                last[0].then_inc(sems[sem], 1)
            if label is not None:
                snap[label] = cnt[sem]

        TEN, VEC, SCA, SYN, GPS = "tensor", "vector", "scalar", "sync", "gpsimd"

        psL = PSA
        psH = [PSA[:, 0:256], PSA[:, 512:768]]
        psY = PSA
        psT = PSB.bitcast(BF16)[:, 0:1024]
        psP = [PSB[:, 512:1024], PSC[:, 0:512]]
        psAg = [PSC[:, 512:1024], PSD[:, 0:512]]
        psOg = [PSD[:, 512:1024], PSD[:, 0:512]]
        psT1g = [PSC[:, 512:1024], PSD[:, 512:1024]]
        psAB = [PSD[0:8, 512:640], PSD[0:8, 0:128]]
        psG = PSD[:, 512:536]
        psW = [PSC[:, 512:1024], PSD[:, 0:512]]

        Q = []
        qpos = [0]

        def drain(n=1):
            while n > 0 and qpos[0] < len(Q):
                Q[qpos[0]][1]()
                qpos[0] += 1
                n -= 1

        def drain_to(name):
            while qpos[0] < len(Q):
                nm, fn = Q[qpos[0]]
                fn()
                qpos[0] += 1
                if nm == name:
                    return

        def drain_all():
            drain(len(Q) - qpos[0])

        if E == SCA:
            eng.preload_act_table(AF.Sigmoid)
        dma(GPS, "WL", BAB[:], AP(bab_h, 0, [[1, 8], [8, 2]]),
            allow_slow_non_contiguous=True)
        dma(GPS, "WL", BO[:], bo_h.ap(), label="w_bo")
        dma(GPS, "WL", WQKV[:],
            AP(wqkv_h, 0, [[3072, 128], [128 * 3072, 8], [1, 3072]]))
        dma(GPS, "WL", WAB[:],
            AP(wab_h, 0, [[16, 128], [128 * 16, 8], [1, 16]]), label="w_ab")
        dma(GPS, "WL", WO[:],
            AP(wo_h, 0, [[1024, 128], [128 * 1024, 8], [1, 1024]]), label="w_wo")
        dma(GPS, "WL", W1T[:],
            AP(w1_h, 0, [[4096, 128], [128 * 4096, 8], [1, 4096]]))
        dma(GPS, "WL", B1[:], AP(b1_h, 0, [[1, 128], [128, 32]]),
            allow_slow_non_contiguous=True)
        for j in range(4):
            dma(GPS, "WL", W2T[:, j * 8192:(j + 1) * 8192],
                AP(w2_h, j * 8 * 128 * 1024,
                   [[1024, 128], [128 * 1024, 8], [1, 1024]]))
        dma(GPS, "WL", B2[:], b2_h.ap(), label="w_ffn")

        dma(SYN, "LD", IDB[:], idb_h.ap())
        dma(SYN, "LD", IDF[:], idf_h.ap())
        dma(SYN, "LD", AMK[:], amk_h.ap(), label="consts")
        xap = x_h.ap()
        ctap = ct_h.ap()
        mkap = mk_h.ap()
        for k in range(2):
            dma(SYN, "LD", X0[:, k * 1024:(k + 1) * 1024],
                xap[k * 128:(k + 1) * 128, :], label=f"x0_{k}")
            dma(SYN, "LD", CT[:, k * 256:(k + 1) * 256],
                ctap[k * 128:(k + 1) * 128, :], label=f"ct_{k}")
            dma(SYN, "LD", MB[:, k * 1024:(k + 1) * 1024],
                mkap[k * 128:(k + 1) * 128, :], label=f"mk_{k}")

        op(VEC, lambda e: e.memset(SST[:], 0.0))
        op(VEC, lambda e: e.memset(ZER8[:], 0.0))
        op(VEC, lambda e: e.memset(ONEB[:], 1.0))
        op(VEC, lambda e: e.memset(ONE8[:], 1.0))
        op(VEC, lambda e: e.memset(C128[:], 1.0 / 128.0))
        mark(VEC, "V", "vinit")
        wt(TEN, "WL", snap["w_bo"])
        wt(TEN, "V", snap["vinit"])
        for mh in range(2):
            op(TEN, lambda e, mh=mh: e.matmul(
                PSC[:, mh * 512:(mh + 1) * 512], ONEB[0:1, 0:128],
                BO[0:1, mh * 512:(mh + 1) * 512], start=True, stop=True))
        mark(TEN, "PE", "borpe")
        wt(VEC, "PE", snap["borpe"])
        op(VEC, lambda e: e.tensor_copy(BOR[:], PSC[:]))
        mark(VEC, "V", "bor")

        AMKb4 = AP(AMK, 0, [[128, 128], [0, 4], [1, 128]])

        def build_ffn_quanta(s):
            def qW1(k):
                def f():
                    if k == 0:
                        wt(TEN, "V", snap[f"l2f_{s}_1"])
                        if s == 0:
                            wt(TEN, "WL", snap["w_ffn"])
                    for ot in range(4 * k, 4 * k + 4):
                        bank = psH[ot % 2]
                        if ot >= 2:
                            wt(TEN, "S", snap[f"h_{s}_{ot - 2}"])
                        for i in range(8):
                            op(TEN, lambda e, i=i, ot=ot, bank=bank: e.matmul(
                                bank,
                                W1T[:, i * 4096 + ot * 128: i * 4096 + (ot + 1) * 128],
                                L2F[:, i * 256:(i + 1) * 256],
                                start=(i == 0), stop=(i == 7)))
                        mark(TEN, "PE", f"w1pe_{s}_{ot}")
                        wt(SCA, "PE", snap[f"w1pe_{s}_{ot}"])
                        if ot >= 2 and ot % 2 == 0:
                            wt(SCA, "V", snap[f"h8_{s}_{ot - 2}"])
                        op(SCA, lambda e, ot=ot, bank=bank: e.activation(
                            HB[:, (ot % 2) * 256:((ot % 2) + 1) * 256], bank,
                            AF.Relu, bias=B1[:, ot:ot + 1]))
                        mark(SCA, "S", f"h_{s}_{ot}")
                        if ot % 2 == 0:
                            wt(VEC, "S", snap[f"h_{s}_{ot}"])
                            op(VEC, lambda e, ot=ot: e.tensor_copy(
                                H8[:, ot * 256:(ot + 1) * 256],
                                HB[:, (ot % 2) * 256:((ot % 2) + 1) * 256]))
                            mark(VEC, "V", f"h8_{s}_{ot}")
                        else:
                            op(SCA, lambda e, ot=ot: e.copy(
                                H8[:, ot * 256:(ot + 1) * 256],
                                HB[:, (ot % 2) * 256:((ot % 2) + 1) * 256]))
                            mark(SCA, "S", f"h8_{s}_{ot}")
                return f

            def qW2(ts, m):
                def f():
                    if m == 0:
                        wt(TEN, "V", snap[f"h8_{s}_30"])
                        wt(TEN, "S", snap[f"h8_{s}_31"])
                        if ts == 1:
                            wt(TEN, "S", snap[f"yb_{s}_0"])
                    for og in range(4 * m, 4 * m + 4):
                        for eh in range(2):
                            op(TEN, lambda e, og=og, eh=eh, ts=ts: e.matmul(
                                psY[:, eh * 512:(eh + 1) * 512],
                                H8[:, og * 256 + ts * 128: og * 256 + (ts + 1) * 128],
                                W2T[:, og * 1024 + eh * 512: og * 1024 + (eh + 1) * 512],
                                start=(og == 0), stop=False))
                    if m == 7:
                        for eh in range(2):
                            op(TEN, lambda e, eh=eh: e.matmul(
                                psY[:, eh * 512:(eh + 1) * 512],
                                ONE8[0:1, 0:128], B2[0:1, eh * 512:(eh + 1) * 512],
                                start=False, stop=True))
                        mark(TEN, "PE", f"w2_{s}_{ts}")
                return f

            def qEpi(ts):
                def f():
                    wt(SCA, "PE", snap[f"w2_{s}_{ts}"])
                    if ts == 1:
                        wt(SCA, "V", snap[f"f8_{s}_0"])
                    op(SCA, lambda e: e.copy(YB[:], psY[:]))
                    mark(SCA, "S", f"yb_{s}_{ts}")
                    wt(VEC, "S", snap[f"yb_{s}_{ts}"])
                    wt(VEC, "LD", snap[f"mk_{s * 2 + ts}"])
                    op(VEC, lambda e, ts=ts: e.tensor_tensor(
                        L2T[:], YB[:], MB[:, ts * 1024:(ts + 1) * 1024], ALU.mult))
                    op(VEC, lambda e, ts=ts: e.tensor_tensor(
                        X1[:, ts * 1024:(ts + 1) * 1024], L2T[:],
                        X1[:, ts * 1024:(ts + 1) * 1024], ALU.add))
                    mark(VEC, "V", f"f8_{s}_{ts}")
                    wt(GPS, "V", snap[f"f8_{s}_{ts}"])
                    dma(GPS, "ST",
                        y_h.ap()[(s * 2 + ts) * 128:(s * 2 + ts + 1) * 128, :],
                        X1[:, ts * 1024:(ts + 1) * 1024], label=f"st_{s * 2 + ts}")
                    if s + 1 < NSC:
                        wt(SYN, "V", snap[f"f8_{s}_{ts}"])
                        dma(SYN, "LD", MB[:, ts * 1024:(ts + 1) * 1024],
                            mkap[((s + 1) * 2 + ts) * 128:((s + 1) * 2 + ts + 1) * 128, :],
                            label=f"mk_{(s + 1) * 2 + ts}")
                return f

            for k in range(8):
                Q.append((f"w1_{s}_{k}", qW1(k)))
            for m in range(8):
                Q.append((f"w2_{s}_0_{m}", qW2(0, m)))
            Q.append((f"epi_{s}_0", qEpi(0)))
            for m in range(8):
                Q.append((f"w2_{s}_1_{m}", qW2(1, m)))
            Q.append((f"epi_{s}_1", qEpi(1)))

        def ln1_stats(ci):
            sl_ = ci % 2
            x0_ = X0[:, sl_ * 1024:(sl_ + 1) * 1024]
            wt(VEC, "LD", snap[f"x0_{ci}"])
            op(VEC, lambda e: e.reduce_sum(ST_[:, 0:1], x0_, axis=AX.X))
            op(VEC, lambda e: e.drain())
            mark(VEC, "V")
            wt(SCA, "LD", snap[f"x0_{ci}"])
            if ci > 0:
                wt(SCA, "V", snap[f"a3v_{ci - 1}"])
            op(SCA, lambda e: e.activation(SQJ[:], x0_, AF.Square,
                                           accum_out=ST_[:, 1:2]))
            mark(SCA, "S", f"sq1_{ci}")

        def ln1_norm(ci):
            sl_ = ci % 2
            x0_ = X0[:, sl_ * 1024:(sl_ + 1) * 1024]
            mark(VEC, "V", f"a3v_{ci}")
            op(VEC, lambda e, sl_=sl_, x0_=x0_: e.tensor_scalar(
                LT1[:, sl_ * 1024:(sl_ + 1) * 1024], x0_,
                ST_[:, 6:7], ST_[:, 4:5], ALU.add, ALU.mult))
            mark(VEC, "V", f"lt1_{ci}")

        LN1S = (0, 1, 6, 2, 3, 4, 5)
        LN2S = [(8, 10, 20, 12, 14, 16, 18), (9, 11, 21, 13, 15, 17, 19)]

        def emit_ln1(ci):
            ln1_stats(ci)
            wt(VEC, "S", snap[f"sq1_{ci}"])
            _ln_chains(eng, E, op, ST_, [LN1S])
            ln1_norm(ci)

        emit_ln1(0)
        for sc in range(NSC):
            for c in range(2):
                ci = 2 * sc + c
                sl = ci % 2
                x0 = X0[:, sl * 1024:(sl + 1) * 1024]
                LT1c = LT1[:, sl * 1024:(sl + 1) * 1024]
                drain(1)

                if ci == 0:
                    wt(TEN, "LD", snap["consts"])
                wt(TEN, "V", snap[f"lt1_{ci}"])
                if ci > 0:
                    wt(TEN, "S", snap[f"k2f_{ci - 1}"])
                for i in range(8):
                    op(TEN, lambda e, i=i, LT1c=LT1c: e.transpose(
                        psT[:, i * 128:(i + 1) * 128],
                        LT1c[:, i * 128:(i + 1) * 128], IDB[:]))
                mark(TEN, "PE", f"ln1tr_{ci}")
                wt(SCA, "PE", snap[f"ln1tr_{ci}"])
                op(SCA, lambda e: e.copy(L1F[:], psT))
                mark(SCA, "S", f"l1f_{ci}")
                drain(1)

                if ci == 0:
                    wt(TEN, "WL", snap["w_ab"])
                wt(TEN, "S", snap[f"l1f_{ci}"])
                if ci > 0:
                    wt(TEN, "V", snap[f"x1_{ci - 1}"])
                dest = {0: QT, 1: QT, 2: KT, 3: KT, 4: VT, 5: VT}
                def ab_group(gg):
                    out = psAB[gg]
                    for i in range(8):
                        op(TEN, lambda e, i=i, gg=gg, out=out: e.matmul(
                            out, WAB[:, i * 16 + gg * 8: i * 16 + gg * 8 + 8],
                            L1F[:, i * 128:(i + 1) * 128],
                            start=(i == 0), stop=(i == 7)))

                for g in range(6):
                    bank = psP[g % 2]
                    if g >= 2:
                        wt(TEN, "S", snap[f"pj_{ci}_{g - 2}"])
                    elif ci > 0:
                        wt(TEN, "S", snap[f"pj_{ci - 1}_{4 + g}"])
                    else:
                        wt(TEN, "V", snap["bor"])
                    if g == 4:
                        ab_group(0)
                    for i in range(8):
                        op(TEN, lambda e, i=i, g=g, bank=bank: e.matmul(
                            bank, L1F[:, i * 128:(i + 1) * 128],
                            WQKV[:, i * 3072 + g * 512: i * 3072 + (g + 1) * 512],
                            start=(i == 0), stop=(i == 7)))
                    mark(TEN, "PE", f"pjpe_{ci}_{g}")
                    wt(SCA, "PE", snap[f"pjpe_{ci}_{g}"])
                    op(SCA, lambda e, g=g, bank=bank: e.copy(
                        dest[g][:, (g % 2) * 512:((g % 2) + 1) * 512], bank))
                    mark(SCA, "S", f"pj_{ci}_{g}")
                ab_group(1)
                mark(TEN, "PE", f"abpe_{ci}")
                drain(1)
                wt(SCA, "PE", snap[f"abpe_{ci}"])
                if ci > 0:
                    wt(SCA, "V", snap[f"a9v_{ci - 1}"])
                op(SCA, lambda e: e.activation(GSIG[:, 0:128], psAB[0], AF.Sigmoid,
                                               bias=BAB[:, 0:1]))
                op(SCA, lambda e: e.activation(GSIG[:, 128:256], psAB[1], AF.Sigmoid,
                                               bias=BAB[:, 1:2]))
                mark(SCA, "S", f"sig_{ci}")

                wt(VEC, "S", snap[f"sig_{ci}"])
                op(VEC, lambda e: e.tensor_scalar(
                    GBET[:], GSIG[:, 0:128], 0.14900000393390656, 0.85,
                    ALU.mult, ALU.add))
                op(VEC, lambda e: e.tensor_tensor_scan(
                    GD[:], GBET[:], ZER8[:], 1.0, ALU.mult, ALU.add))
                op(VEC, lambda e: e.reciprocal(GRD[:], GD[:]))
                op(VEC, lambda e: e.drain())
                op(VEC, lambda e: e.tensor_scalar(
                    GPK[:, 0:128], GD[:], GRD[:, 127:128], None, ALU.mult))
                op(VEC, lambda e: e.scalar_tensor_tensor(
                    GPK[:, 128:256], GRD[:], GD[:, 127:128], GSIG[:, 128:256],
                    ALU.mult, ALU.mult))
                op(VEC, lambda e: e.tensor_scalar(
                    GPK[:, 256:384], ZER8[:], GD[:, 127:128], None, ALU.add))
                mark(VEC, "V", f"a9v_{ci}")

                wt(TEN, "V", snap[f"a9v_{ci}"])
                wt(TEN, "S", snap[f"sig_{ci}"])
                for j in range(3):
                    op(TEN, lambda e, j=j: e.transpose(
                        psG[:, j * 8:(j + 1) * 8],
                        GPK[0:8, j * 128:(j + 1) * 128], IDF[0:8, 0:8]))
                mark(TEN, "PE", f"gpe_{ci}")
                wt(SCA, "PE", snap[f"gpe_{ci}"])
                op(SCA, lambda e: e.copy(GT[:], psG))
                mark(SCA, "S", f"gt_{ci}")

                wt(VEC, "LD", snap[f"ct_{ci}"])
                wt(VEC, "S", cnt["S"])
                for TSR, gofs, lbl in ((QT, 0, f"qr_{ci}"), (KT, 8, f"kr_{ci}")):
                    op(VEC, lambda e, TSR=TSR: e.tensor_copy(
                        AP(SWQ, 0, [[1024, 128], [128, 8], [1, 64]]),
                        AP(TSR, 64, [[1024, 128], [128, 8], [1, 64]])))
                    op(VEC, lambda e, TSR=TSR: e.tensor_copy(
                        AP(SWQ, 64, [[1024, 128], [128, 8], [1, 64]]),
                        AP(TSR, 0, [[1024, 128], [128, 8], [1, 64]])))
                    op(VEC, lambda e: e.tensor_tensor(
                        SWQ[:], SWQ[:],
                        AP(CT, sl * 256 + 128, [[512, 128], [0, 8], [1, 128]]),
                        ALU.mult))
                    op(VEC, lambda e, TSR=TSR: e.tensor_tensor(
                        TSR[:], TSR[:],
                        AP(CT, sl * 256, [[512, 128], [0, 8], [1, 128]]),
                        ALU.mult))
                    op(VEC, lambda e, TSR=TSR: e.tensor_tensor(
                        TSR[:], TSR[:], SWQ[:], ALU.add))
                    for h in range(H):
                        op(VEC, lambda e, TSR=TSR, h=h, gofs=gofs: e.tensor_scalar(
                            TSR[:, h * 128:(h + 1) * 128],
                            TSR[:, h * 128:(h + 1) * 128],
                            GT[:, gofs + h: gofs + h + 1], None, ALU.mult))
                    mark(VEC, "V", lbl)
                drain(1)

                wt(TEN, "V", snap[f"qr_{ci}"])
                for i in range(8):
                    op(TEN, lambda e, i=i: e.transpose(
                        psT[:, i * 128:(i + 1) * 128],
                        QT[:, i * 128:(i + 1) * 128], IDB[:]))
                mark(TEN, "PE", f"qtr_{ci}")
                wt(VEC, "PE", snap[f"qtr_{ci}"])
                op(VEC, lambda e: e.tensor_copy(Q2F[:], psT))
                mark(VEC, "V", f"q2f_{ci}")
                wt(TEN, "V", max(snap[f"q2f_{ci}"], snap[f"kr_{ci}"]))
                for i in range(8):
                    op(TEN, lambda e, i=i: e.transpose(
                        psT[:, i * 128:(i + 1) * 128],
                        KT[:, i * 128:(i + 1) * 128], IDB[:]))
                mark(TEN, "PE", f"ktr_{ci}")
                wt(SCA, "PE", snap[f"ktr_{ci}"])
                op(SCA, lambda e: e.copy(K2F[:], psT))
                mark(SCA, "S", f"k2f_{ci}")

                for h in range(H):
                    op(VEC, lambda e, h=h: e.tensor_scalar(
                        SH16[:, h * 128:(h + 1) * 128],
                        SST[:, h * 128:(h + 1) * 128],
                        GT[:, 16 + h:17 + h], None, ALU.mult))
                mark(VEC, "V", f"shat_{ci}")
                drain(1)

                wt(TEN, "V", snap[f"q2f_{ci}"])
                wt(TEN, "S", snap[f"k2f_{ci}"])
                wt(TEN, "V", snap[f"x1_{ci - 1}"] if ci > 0 else snap["bor"])
                for g in range(2):
                    for j, h in enumerate(range(4 * g, 4 * g + 4)):
                        op(TEN, lambda e, g=g, j=j, h=h: e.matmul(
                            psAg[g][:, j * 128:(j + 1) * 128],
                            K2F[:, h * 128:(h + 1) * 128],
                            Q2F[:, h * 128:(h + 1) * 128],
                            start=True, stop=True))
                    mark(TEN, "PE", f"ape_{ci}_{g}")
                for g in range(2):
                    wt(VEC, "PE", snap[f"ape_{ci}_{g}"])
                    op(VEC, lambda e, g=g: e.tensor_tensor(
                        AM[:, g * 512:(g + 1) * 512], psAg[g], AMKb4, ALU.mult))
                    mark(VEC, "V", f"am_{ci}_{g}")
                drain(1)

                def o_group(g):
                    for j, h in enumerate(range(4 * g, 4 * g + 4)):
                        s_ = slice(h * 128, (h + 1) * 128)
                        js = slice(j * 128, (j + 1) * 128)
                        op(TEN, lambda e, g=g, s_=s_, js=js: e.matmul(
                            psOg[g][:, js], VT[:, s_],
                            AM[:, s_], start=True, stop=False))
                        op(TEN, lambda e, g=g, s_=s_, js=js: e.matmul(
                            psOg[g][:, js], SH16[:, s_], Q2F[:, s_],
                            start=False, stop=True))
                    mark(TEN, "PE", f"ope_{ci}_{g}")

                def t_group(g):
                    for j, h in enumerate(range(4 * g, 4 * g + 4)):
                        s_ = slice(h * 128, (h + 1) * 128)
                        js = slice(j * 128, (j + 1) * 128)
                        op(TEN, lambda e, g=g, s_=s_, js=js: e.matmul(
                            psT1g[g][:, js], KT[:, s_], VT[:, s_],
                            start=True, stop=True))
                    mark(TEN, "PE", f"t1pe_{ci}_{g}")

                def s_update(g):
                    wt(VEC, "PE", snap[f"t1pe_{ci}_{g}"])
                    for j, h in enumerate(range(4 * g, 4 * g + 4)):
                        s_ = slice(h * 128, (h + 1) * 128)
                        js = slice(j * 128, (j + 1) * 128)
                        op(VEC, lambda e, g=g, s_=s_, js=js, h=h:
                           e.scalar_tensor_tensor(
                               SST[:, s_], SST[:, s_], GT[:, 16 + h:17 + h],
                               psT1g[g][:, js], ALU.mult, ALU.add))
                    mark(VEC, "V", f"supd_{ci}_{g}")

                def of_evac(g):
                    wt(SCA, "PE", snap[f"ope_{ci}_{g}"])
                    op(SCA, lambda e, g=g: e.copy(
                        OF[:, g * 512:(g + 1) * 512], psOg[g]))
                    mark(SCA, "S", f"of_{ci}_{g}")

                wt(TEN, "V", snap[f"am_{ci}_0"])
                wt(TEN, "S", snap[f"gt_{ci}"])
                o_group(0)
                t_group(0)
                wt(TEN, "V", snap[f"am_{ci}_1"])
                o_group(1)
                of_evac(0)
                s_update(0)
                wt(TEN, "S", snap[f"of_{ci}_0"])
                t_group(1)
                of_evac(1)
                s_update(1)
                if c == 0:
                    if ci + 1 < NCH:
                        emit_ln1(ci + 1)
                else:
                    if ci + 1 < NCH:
                        ln1_stats(ci + 1)
                        wt(VEC, "S", snap[f"sq1_{ci + 1}"])
                        _ln_chains(eng, E, op, ST_, [LN1S, LN2S[0]])
                        ln1_norm(ci + 1)
                    else:
                        wt(VEC, "S", cnt["S"])
                        _ln_chains(eng, E, op, ST_, [LN2S[0]])
                        mark(VEC, "V")
                drain(1)

                if ci == 0:
                    wt(TEN, "WL", snap["w_wo"])
                wt(TEN, "V", snap[f"supd_{ci}_0"])
                wt(TEN, "S", snap[f"of_{ci}_1"])
                for mh in range(2):
                    for e8 in range(8):
                        op(TEN, lambda e, mh=mh, e8=e8: e.matmul(
                            psW[mh], OF[:, e8 * 128:(e8 + 1) * 128],
                            WO[:, e8 * 1024 + mh * 512: e8 * 1024 + (mh + 1) * 512],
                            start=(e8 == 0), stop=False))
                    op(TEN, lambda e, mh=mh: e.matmul(
                        psW[mh], C128[:], BOR[:, mh * 512:(mh + 1) * 512],
                        start=False, stop=True))
                mark(TEN, "PE", f"wo_{ci}")

                if sc > 0:
                    drain_to(f"epi_{sc - 1}_{c}")
                if c == 1:
                    op(VEC, lambda e: e.tensor_scalar(
                        L2T[:], X1[:, 0:1024],
                        ST_[:, 20:21], ST_[:, 16:17], ALU.add, ALU.mult))
                    mark(VEC, "V", f"ln2n_{sc}_0")
                    wt(TEN, "V", snap[f"ln2n_{sc}_0"])
                    if sc > 0:
                        wt(TEN, "S", snap[f"yb_{sc - 1}_1"])
                    for i in range(8):
                        op(TEN, lambda e, i=i: e.transpose(
                            psL[:, i * 128:(i + 1) * 128],
                            L2T[:, i * 128:(i + 1) * 128], IDF[:]))
                    mark(TEN, "PE", f"f3pe_{sc}_0")
                    wt(VEC, "PE", snap[f"f3pe_{sc}_0"])
                    op(VEC, lambda e: e.tensor_copy(
                        AP(L2F, 0, [[2048, 128], [256, 8], [1, 128]]),
                        AP(PSA, 0, [[1024, 128], [128, 8], [1, 128]])))
                    mark(VEC, "V", f"l2f_{sc}_0")

                wt(VEC, "PE", snap[f"wo_{ci}"])
                if sc > 0:
                    wt(VEC, "ST", snap[f"st_{(sc - 1) * 2 + c}"])
                op(VEC, lambda e, c=c: e.tensor_tensor(
                    X1[:, c * 1024:c * 1024 + 512], psW[0], x0[:, 0:512], ALU.add))
                op(VEC, lambda e, c=c: e.tensor_tensor(
                    X1[:, c * 1024 + 512:(c + 1) * 1024], psW[1],
                    x0[:, 512:1024], ALU.add))
                mark(VEC, "V", f"x1_{ci}")

                op(VEC, lambda e, c=c: e.reduce_sum(
                    ST_[:, 8 + c:9 + c], X1[:, c * 1024:(c + 1) * 1024], axis=AX.X))
                op(VEC, lambda e: e.drain())
                mark(VEC, "V")
                wt(SCA, "V", snap[f"x1_{ci}"])
                op(SCA, lambda e, c=c: e.activation(
                    SQJ[:], X1[:, c * 1024:(c + 1) * 1024], AF.Square,
                    accum_out=ST_[:, 10 + c:11 + c]))
                mark(SCA, "S", f"fsq_{ci}")
                if c == 1:
                    wt(VEC, "S", snap[f"fsq_{ci}"])
                    _ln_chains(eng, E, op, ST_, [LN2S[1]])
                    mark(VEC, "V")

                if ci + 2 < NCH:
                    wt(SYN, "V", snap[f"x1_{ci}"])
                    dma(SYN, "LD", X0[:, sl * 1024:(sl + 1) * 1024],
                        xap[(ci + 2) * 128:(ci + 3) * 128, :], label=f"x0_{ci + 2}")
                    dma(SYN, "LD", CT[:, sl * 256:(sl + 1) * 256],
                        ctap[(ci + 2) * 128:(ci + 3) * 128, :], label=f"ct_{ci + 2}")

            drain_all()
            op(VEC, lambda e: e.tensor_scalar(
                L2T[:], X1[:, 1024:2048],
                ST_[:, 21:22], ST_[:, 17:18], ALU.add, ALU.mult))
            mark(VEC, "V", f"ln2n_{sc}_1")
            wt(TEN, "V", snap[f"ln2n_{sc}_1"])
            for i in range(8):
                op(TEN, lambda e, i=i: e.transpose(
                    psL[:, i * 128:(i + 1) * 128],
                    L2T[:, i * 128:(i + 1) * 128], IDF[:]))
            mark(TEN, "PE", f"f3pe_{sc}_1")
            wt(VEC, "PE", snap[f"f3pe_{sc}_1"])
            op(VEC, lambda e: e.tensor_copy(
                AP(L2F, 128, [[2048, 128], [256, 8], [1, 128]]),
                AP(PSA, 0, [[1024, 128], [128, 8], [1, 128]])))
            mark(VEC, "V", f"l2f_{sc}_1")
            build_ffn_quanta(sc)

        drain_all()
        wt(GPS, "ST", cnt["ST"])

    def _ln_chain(eng, E, op, ST, c_sum, c_sq, c_negm, c_m2, c_var, c_y, c_t, w):
        _ln_chains(eng, E, op, ST, [(c_sum, c_sq, c_negm, c_m2, c_var, c_y, c_t)])

    def _ln_chains(eng, E, op, ST, specs):
        n = len(specs)
        dr = lambda: op("vector", lambda e: e.drain())

        def rnd(fn):
            for sp in specs:
                c_sum, c_sq, c_negm, c_m2, c_var, c_y, c_t = sp
                sl = lambda c: ST[:, c:c + 1]
                sli = lambda c: ST[:, c:c + 1].bitcast(I32)
                fn(sl, sli, c_sum, c_sq, c_negm, c_m2, c_var, c_y, c_t)
            if n < 3:
                dr()

        rnd(lambda sl, sli, c_sum, c_sq, c_negm, c_m2, c_var, c_y, c_t:
            op("vector", lambda e: e.tensor_scalar(
                sl(c_var), sl(c_sq), 1.0 / D, EPS, ALU.mult, ALU.add)))
        rnd(lambda sl, sli, c_sum, c_sq, c_negm, c_m2, c_var, c_y, c_t:
            op("vector", lambda e: e.tensor_scalar_mul(
                sl(c_negm), sl(c_sum), -1.0 / D)))
        rnd(lambda sl, sli, c_sum, c_sq, c_negm, c_m2, c_var, c_y, c_t:
            op("vector", lambda e: e.tensor_scalar(
                sl(c_m2), sl(c_sum), -1.0 / D, sl(c_negm), ALU.mult, ALU.mult)))
        rnd(lambda sl, sli, c_sum, c_sq, c_negm, c_m2, c_var, c_y, c_t:
            op("vector", lambda e: e.tensor_scalar(
                sl(c_var), sl(c_var), sl(c_m2), None, ALU.subtract)))
        rnd(lambda sl, sli, c_sum, c_sq, c_negm, c_m2, c_var, c_y, c_t:
            op("vector", lambda e: e.tensor_scalar(
                sli(c_y), sli(c_var), 1, 0xFFFFFFFF,
                ALU.arith_shift_right, ALU.bitwise_xor)))
        rnd(lambda sl, sli, c_sum, c_sq, c_negm, c_m2, c_var, c_y, c_t:
            op("vector", lambda e: e.tensor_single_scalar(
                sli(c_y), sli(c_y), MAGIC, ALU.add)))
        for _ in range(2):
            rnd(lambda sl, sli, c_sum, c_sq, c_negm, c_m2, c_var, c_y, c_t:
                op("vector", lambda e: e.tensor_scalar(
                    sl(c_t), sl(c_var), sl(c_y), None, ALU.mult)))
            rnd(lambda sl, sli, c_sum, c_sq, c_negm, c_m2, c_var, c_y, c_t:
                op("vector", lambda e: e.tensor_scalar(
                    sl(c_t), sl(c_t), sl(c_y), -0.5, ALU.mult, ALU.mult)))
            rnd(lambda sl, sli, c_sum, c_sq, c_negm, c_m2, c_var, c_y, c_t:
                op("vector", lambda e: e.tensor_scalar(
                    sl(c_t), sl(c_t), 1.5, None, ALU.add)))
            rnd(lambda sl, sli, c_sum, c_sq, c_negm, c_m2, c_var, c_y, c_t:
                op("vector", lambda e: e.tensor_scalar(
                    sl(c_y), sl(c_y), sl(c_t), None, ALU.mult)))

    with nc.Block() as block:
        @block.sync
        def _(eng):
            schedule("sync", eng)

        @block.gpsimd
        def _(eng):
            schedule("gpsimd", eng)

        @block.tensor
        def _(eng):
            schedule("tensor", eng)

        @block.vector
        def _(eng):
            schedule("vector", eng)

        @block.scalar
        def _(eng):
            schedule("scalar", eng)

    return nc


_NC_CACHE = [None]


def _host_prep(L0_attn_Wq, L0_attn_Wk, L0_attn_Wv, L0_attn_Wbeta, L0_attn_bbeta,
               L0_attn_Walpha, L0_attn_balpha, L0_attn_Wo, L0_attn_bo,
               L0_ffn_W1, L0_ffn_b1, L0_ffn_W2, L0_ffn_b2):
    c = np.ascontiguousarray

    perm = np.concatenate([np.arange(0, DH, 2), np.arange(1, DH, 2)])
    wq = np.asarray(L0_attn_Wq, dtype=np.float32)
    wk = np.asarray(L0_attn_Wk, dtype=np.float32)
    wq_p = wq.reshape(H, DH, D)[:, perm, :].reshape(D, D)
    wk_p = wk.reshape(H, DH, D)[:, perm, :].reshape(D, D)
    wqkv = c(np.concatenate(
        [wq_p.T, wk_p.T, np.asarray(L0_attn_Wv, np.float32).T], axis=1))
    wab = c(np.concatenate(
        [np.asarray(L0_attn_Wbeta, np.float32).T,
         np.asarray(L0_attn_Walpha, np.float32).T], axis=1))
    bab = c(np.concatenate(
        [np.asarray(L0_attn_bbeta, np.float32),
         np.asarray(L0_attn_balpha, np.float32)]).reshape(1, 16))
    wo = c(np.asarray(L0_attn_Wo, np.float32).T)
    bo = c(np.asarray(L0_attn_bo, np.float32).reshape(1, D))
    w1 = c(np.asarray(L0_ffn_W1, np.float32).T)
    b1 = c(np.asarray(L0_ffn_b1, np.float32).reshape(1, 4 * D))
    w2 = c(np.asarray(L0_ffn_W2, np.float32).T)
    b2 = c(np.asarray(L0_ffn_b2, np.float32).reshape(1, D))

    half = DH // 2
    inv_freq = (np.float32(1.0) /
                np.power(np.float32(10000.0),
                         np.arange(half, dtype=np.float32) / np.float32(half)))
    freqs = np.arange(L, dtype=np.float32)[:, None] * inv_freq[None, :]
    cos = np.cos(freqs).astype(np.float32)
    sin = np.sin(freqs).astype(np.float32)
    ct = c(np.concatenate([cos, cos, -sin, sin], axis=1)).astype(ml_dtypes.bfloat16)

    idb = np.eye(128, dtype=ml_dtypes.bfloat16)
    idf = np.eye(128, dtype=np.float32)
    amk = np.triu(np.ones((128, 128), dtype=np.float32)).astype(ml_dtypes.bfloat16)

    import jax
    with jax.default_device(jax.devices("cpu")[0]):
        keep = np.asarray(jax.random.bernoulli(
            jax.random.key(42), 0.8, (NCORES, L, D)))
    mask = np.where(keep, np.float32(1.25), np.float32(0.0)).astype(ml_dtypes.bfloat16)

    shared = dict(wqkv=wqkv, wab=wab, wo=wo, w1=w1, w2=w2, babi=bab, boi=bo,
                  b1i=b1, b2i=b2, ct=ct, idb=idb, idf=idf, amk=amk)
    return shared, mask


def kernel(L0_attn_x, L0_attn_Wq, L0_attn_Wk, L0_attn_Wv, L0_attn_Wbeta,
           L0_attn_bbeta, L0_attn_Walpha, L0_attn_balpha, L0_attn_Wo,
           L0_attn_bo, L0_ffn_W1, L0_ffn_b1, L0_ffn_W2, L0_ffn_b2):
    shared, mask = _host_prep(
        L0_attn_Wq, L0_attn_Wk, L0_attn_Wv, L0_attn_Wbeta, L0_attn_bbeta,
        L0_attn_Walpha, L0_attn_balpha, L0_attn_Wo, L0_attn_bo,
        L0_ffn_W1, L0_ffn_b1, L0_ffn_W2, L0_ffn_b2)

    x = np.asarray(L0_attn_x, dtype=np.float32)
    in_maps = []
    for b in range(NCORES):
        m = dict(shared)
        m["x"] = np.ascontiguousarray(x[b])
        m["mk"] = np.ascontiguousarray(mask[b])
        in_maps.append(m)

    if _NC_CACHE[0] is None:
        _NC_CACHE[0] = build_program()
    nc = _NC_CACHE[0]
    res = run_bass_kernel_spmd(nc, in_maps, core_ids=list(range(NCORES)))
    out = np.stack([res.results[b]["y_out"] for b in range(NCORES)], axis=0)
    return out.astype(np.float32)


# version 21
# speedup vs baseline: 1.4260x; 1.0676x over previous
import numpy as np
import ml_dtypes
import concourse.bass as bass
import concourse.mybir as mybir
from concourse.bass_utils import run_bass_kernel_spmd
from concourse.ap import AP

F32 = mybir.dt.float32
BF16 = mybir.dt.bfloat16
F8 = mybir.dt.float8e5
I32 = mybir.dt.int32
AF = mybir.ActivationFunctionType
ALU = mybir.AluOpType
AX = mybir.AxisListType

NCORES = 8
L = 4096
D = 1024
H = 8
DH = 128
NCH = 32
NSC = 16
EPS = 1e-5
MAGIC = 0x5F3759DF + 1

_DBG = {"taps": None}


def build_program():
    nc = bass.Bass("TRN2", target_bir_lowering=False, debug=False)

    x_h = nc.dram_tensor("x", [L, D], F32, kind="ExternalInput")
    wqkv_h = nc.dram_tensor("wqkv", [D, 3 * D], F32, kind="ExternalInput")
    wab_h = nc.dram_tensor("wab", [D, 16], F32, kind="ExternalInput")
    wo_h = nc.dram_tensor("wo", [D, D], F32, kind="ExternalInput")
    w1_h = nc.dram_tensor("w1", [D, 4 * D], F32, kind="ExternalInput")
    w2_h = nc.dram_tensor("w2", [4 * D, D], F32, kind="ExternalInput")
    bab_h = nc.dram_tensor("babi", [1, 16], F32, kind="ExternalInput")
    bo_h = nc.dram_tensor("boi", [1, D], F32, kind="ExternalInput")
    b1_h = nc.dram_tensor("b1i", [1, 4 * D], F32, kind="ExternalInput")
    b2_h = nc.dram_tensor("b2i", [1, D], F32, kind="ExternalInput")
    ct_h = nc.dram_tensor("ct", [L, 256], BF16, kind="ExternalInput")
    mk_h = nc.dram_tensor("mk", [L, D], BF16, kind="ExternalInput")
    idb_h = nc.dram_tensor("idb", [128, 128], BF16, kind="ExternalInput")
    idf_h = nc.dram_tensor("idf", [128, 128], F32, kind="ExternalInput")
    amk_h = nc.dram_tensor("amk", [128, 128], BF16, kind="ExternalInput")
    y_h = nc.dram_tensor("y_out", [L, D], F32, kind="ExternalOutput")

    dbg_handles = {}
    if _DBG["taps"]:
        for name, shp in _DBG["taps"].items():
            dbg_handles[name] = nc.dram_tensor("dbg_" + name, shp, F32,
                                               kind="ExternalOutput")

    A = nc.alloc_sbuf_tensor
    WQKV = A("WQKV", [128, 8 * 3072], BF16)
    WAB = A("WAB", [128, 128], BF16)
    WO = A("WO", [128, 8 * 1024], BF16)
    W1T = A("W1T", [128, 8 * 4096], F8)
    W2T = A("W2T", [128, 32 * 1024], F8)
    BAB = A("BAB", [8, 2], BF16)
    BO = A("BO", [1, D], BF16)
    B1 = A("B1", [128, 32], F8)
    B2 = A("B2", [1, D], F8)
    ONEB = A("ONEB", [1, 512], BF16)
    ONE8 = A("ONE8", [1, 512], F8)
    IDB = A("IDB", [128, 128], BF16)
    C128 = A("C128", [128, 128], BF16)
    BOR = A("BOR", [128, 1024], BF16)
    IDF = A("IDF", [128, 128], F32)
    AMK = A("AMK", [128, 128], BF16)

    X0 = A("X0", [128, 2048], F32)
    CT = A("CT", [128, 512], BF16)
    MB = A("MB", [128, 2048], BF16)
    X1 = A("X1", [128, 2048], F32)
    LT1 = A("LT1", [128, 2048], BF16)
    L1F = A("L1F", [128, 1024], BF16)
    QT = A("QT", [128, 1024], BF16)
    KT = A("KT", [128, 1024], BF16)
    VT = A("VT", [128, 1024], BF16)
    SWQ = A("SWQ", [128, 1024], BF16)
    YB = A("YB", [128, 1024], BF16)
    Q2F = A("Q2F", [128, 1024], BF16)
    K2F = A("K2F", [128, 1024], BF16)
    AM = A("AM", [128, 1024], BF16)
    OF = A("OF", [128, 1024], BF16)
    SST = A("SST", [128, 1024], F32)
    SH16 = A("SH16", [128, 1024], BF16)
    GSIG = A("GSIG", [8, 256], F32)
    GBET = A("GBET", [8, 128], F32)
    GD = A("GD", [8, 128], F32)
    GRD = A("GRD", [8, 128], F32)
    GPK = A("GPK", [8, 384], F32)
    GT = A("GT", [128, 24], F32)
    ZER8 = A("ZER8", [8, 128], F32)
    ST_ = A("ST_", [128, 32], F32)
    SQJ = A("SQJ", [128, 1024], F8)
    L2T = A("L2T", [128, 1024], F32)
    L2F = A("L2F", [128, 2048], F8)
    HB = A("HB", [128, 512], BF16)
    H8 = A("H8", [128, 8192], F8)

    PSA = nc.alloc_psum_tensor("PSA", [128, 1024], F32)
    PSB = nc.alloc_psum_tensor("PSB", [128, 1024], F32)
    PSC = nc.alloc_psum_tensor("PSC", [128, 1024], F32)
    PSD = nc.alloc_psum_tensor("PSD", [128, 1024], F32)

    sems = {
        "LD": nc.alloc_semaphore(name="sLD"),
        "WL": nc.alloc_semaphore(name="sWL"),
        "ST": nc.alloc_semaphore(name="sST"),
        "PE": nc.alloc_semaphore(name="sPE"),
        "G": nc.alloc_semaphore(name="sG"),
        "V": nc.alloc_semaphore(name="sV"),
        "S": nc.alloc_semaphore(name="sS"),
    }

    def schedule(E, eng):
        cnt = {k: 0 for k in sems}
        snap = {}
        last = [None]

        def dma(q, sem, out, in_, label=None, **kw):
            cnt[sem] += 16
            if E == q:
                eng.dma_start(out=out, in_=in_, **kw).then_inc(sems[sem], 16)
            if label is not None:
                snap[label] = cnt[sem]

        def wt(q, sem, val):
            if E == q and val > 0:
                eng.wait_ge(sems[sem], val)

        def op(q, fn):
            if E == q:
                last[0] = fn(eng)

        def mark(q, sem, label=None):
            cnt[sem] += 1
            if E == q:
                last[0].then_inc(sems[sem], 1)
            if label is not None:
                snap[label] = cnt[sem]

        TEN, VEC, SCA, SYN, GPS = "tensor", "vector", "scalar", "sync", "gpsimd"

        psL = PSA
        psH = [PSA[:, 0:256], PSA[:, 512:768]]
        psY = PSA
        psT = PSB.bitcast(BF16)[:, 0:1024]
        psP = [PSB[:, 512:1024], PSC[:, 0:512]]
        psAg = [PSC[:, 512:1024], PSD[:, 0:512]]
        psOg = [PSD[:, 512:1024], PSD[:, 0:512]]
        psT1g = [PSC[:, 512:1024], PSD[:, 512:1024]]
        psAB = [PSD[0:8, 512:640], PSD[0:8, 0:128]]
        psG = PSD[:, 512:536]
        psW = [PSC[:, 512:1024], PSD[:, 0:512]]

        Q = []
        qpos = [0]

        def drain(n=1):
            while n > 0 and qpos[0] < len(Q):
                Q[qpos[0]][1]()
                qpos[0] += 1
                n -= 1

        def drain_to(name):
            while qpos[0] < len(Q):
                nm, fn = Q[qpos[0]]
                fn()
                qpos[0] += 1
                if nm == name:
                    return

        def drain_all():
            drain(len(Q) - qpos[0])

        if E == SCA:
            eng.preload_act_table(AF.Sigmoid)
        dma(GPS, "WL", BAB[:], AP(bab_h, 0, [[1, 8], [8, 2]]),
            allow_slow_non_contiguous=True)
        dma(GPS, "WL", BO[:], bo_h.ap(), label="w_bo")
        dma(GPS, "WL", WQKV[:],
            AP(wqkv_h, 0, [[3072, 128], [128 * 3072, 8], [1, 3072]]))
        dma(GPS, "WL", WAB[:],
            AP(wab_h, 0, [[16, 128], [128 * 16, 8], [1, 16]]), label="w_ab")
        dma(GPS, "WL", WO[:],
            AP(wo_h, 0, [[1024, 128], [128 * 1024, 8], [1, 1024]]), label="w_wo")
        dma(GPS, "WL", W1T[:],
            AP(w1_h, 0, [[4096, 128], [128 * 4096, 8], [1, 4096]]))
        dma(GPS, "WL", B1[:], AP(b1_h, 0, [[1, 128], [128, 32]]),
            allow_slow_non_contiguous=True)
        for j in range(4):
            dma(GPS, "WL", W2T[:, j * 8192:(j + 1) * 8192],
                AP(w2_h, j * 8 * 128 * 1024,
                   [[1024, 128], [128 * 1024, 8], [1, 1024]]))
        dma(GPS, "WL", B2[:], b2_h.ap(), label="w_ffn")

        dma(SYN, "LD", IDB[:], idb_h.ap())
        dma(SYN, "LD", IDF[:], idf_h.ap())
        dma(SYN, "LD", AMK[:], amk_h.ap(), label="consts")
        xap = x_h.ap()
        ctap = ct_h.ap()
        mkap = mk_h.ap()
        for k in range(2):
            dma(SYN, "LD", X0[:, k * 1024:(k + 1) * 1024],
                xap[k * 128:(k + 1) * 128, :], label=f"x0_{k}")
            dma(SYN, "LD", CT[:, k * 256:(k + 1) * 256],
                ctap[k * 128:(k + 1) * 128, :], label=f"ct_{k}")
            dma(SYN, "LD", MB[:, k * 1024:(k + 1) * 1024],
                mkap[k * 128:(k + 1) * 128, :], label=f"mk_{k}")

        op(VEC, lambda e: e.memset(SST[:], 0.0))
        op(VEC, lambda e: e.memset(ZER8[:], 0.0))
        op(VEC, lambda e: e.memset(ONEB[:], 1.0))
        op(VEC, lambda e: e.memset(ONE8[:], 1.0))
        op(VEC, lambda e: e.memset(C128[:], 1.0 / 128.0))
        mark(VEC, "V", "vinit")
        wt(TEN, "WL", snap["w_bo"])
        wt(TEN, "V", snap["vinit"])
        for mh in range(2):
            op(TEN, lambda e, mh=mh: e.matmul(
                PSC[:, mh * 512:(mh + 1) * 512], ONEB[0:1, 0:128],
                BO[0:1, mh * 512:(mh + 1) * 512], start=True, stop=True))
        mark(TEN, "PE", "borpe")
        wt(VEC, "PE", snap["borpe"])
        op(VEC, lambda e: e.tensor_copy(BOR[:], PSC[:]))
        mark(VEC, "V", "bor")

        AMKb4 = AP(AMK, 0, [[128, 128], [0, 4], [1, 128]])

        def build_ffn_quanta(s):
            def qW1(k):
                def f():
                    if k == 0:
                        wt(TEN, "V", snap[f"l2f_{s}_1"])
                        if s == 0:
                            wt(TEN, "WL", snap["w_ffn"])
                    for ot in range(4 * k, 4 * k + 4):
                        bank = psH[ot % 2]
                        if ot >= 2:
                            wt(TEN, "S", snap[f"h_{s}_{ot - 2}"])
                        for i in range(8):
                            op(TEN, lambda e, i=i, ot=ot, bank=bank: e.matmul(
                                bank,
                                W1T[:, i * 4096 + ot * 128: i * 4096 + (ot + 1) * 128],
                                L2F[:, i * 256:(i + 1) * 256],
                                start=(i == 0), stop=(i == 7)))
                        mark(TEN, "PE", f"w1pe_{s}_{ot}")
                        wt(SCA, "PE", snap[f"w1pe_{s}_{ot}"])
                        if ot >= 2 and ot % 2 == 0:
                            wt(SCA, "V", snap[f"h8_{s}_{ot - 2}"])
                        op(SCA, lambda e, ot=ot, bank=bank: e.activation(
                            HB[:, (ot % 2) * 256:((ot % 2) + 1) * 256], bank,
                            AF.Relu, bias=B1[:, ot:ot + 1]))
                        mark(SCA, "S", f"h_{s}_{ot}")
                        if ot % 2 == 0:
                            wt(VEC, "S", snap[f"h_{s}_{ot}"])
                            op(VEC, lambda e, ot=ot: e.tensor_copy(
                                H8[:, ot * 256:(ot + 1) * 256],
                                HB[:, (ot % 2) * 256:((ot % 2) + 1) * 256]))
                            mark(VEC, "V", f"h8_{s}_{ot}")
                        else:
                            op(SCA, lambda e, ot=ot: e.copy(
                                H8[:, ot * 256:(ot + 1) * 256],
                                HB[:, (ot % 2) * 256:((ot % 2) + 1) * 256]))
                            mark(SCA, "S", f"h8_{s}_{ot}")
                return f

            def qW2(ts, m):
                def f():
                    if m == 0:
                        wt(TEN, "V", snap[f"h8_{s}_30"])
                        wt(TEN, "S", snap[f"h8_{s}_31"])
                        if ts == 1:
                            wt(TEN, "S", snap[f"yb_{s}_0"])
                    for kk in range(2 * m, 2 * m + 2):
                        for eh in range(2):
                            op(TEN, lambda e, kk=kk, eh=eh, ts=ts: e.matmul(
                                psY[:, eh * 512:(eh + 1) * 512],
                                AP(H8, kk * 512 + ts * 128,
                                   [[8192, 128], [256, 2], [1, 128]]),
                                AP(W2T, kk * 2048 + eh * 512,
                                   [[32768, 128], [1024, 2], [1, 512]]),
                                start=(kk == 0), stop=False,
                                perf_mode=mybir.MatmulPerfMode.DoubleRow))
                    if m == 7:
                        for eh in range(2):
                            op(TEN, lambda e, eh=eh: e.matmul(
                                psY[:, eh * 512:(eh + 1) * 512],
                                ONE8[0:1, 0:128], B2[0:1, eh * 512:(eh + 1) * 512],
                                start=False, stop=True))
                        mark(TEN, "PE", f"w2_{s}_{ts}")
                return f

            def qEpi(ts):
                def f():
                    wt(SCA, "PE", snap[f"w2_{s}_{ts}"])
                    if ts == 1:
                        wt(SCA, "V", snap[f"f8_{s}_0"])
                    op(SCA, lambda e: e.copy(YB[:], psY[:]))
                    mark(SCA, "S", f"yb_{s}_{ts}")
                    wt(VEC, "S", snap[f"yb_{s}_{ts}"])
                    wt(VEC, "LD", snap[f"mk_{s * 2 + ts}"])
                    op(VEC, lambda e, ts=ts: e.tensor_tensor(
                        L2T[:], YB[:], MB[:, ts * 1024:(ts + 1) * 1024], ALU.mult))
                    op(VEC, lambda e, ts=ts: e.tensor_tensor(
                        X1[:, ts * 1024:(ts + 1) * 1024], L2T[:],
                        X1[:, ts * 1024:(ts + 1) * 1024], ALU.add))
                    mark(VEC, "V", f"f8_{s}_{ts}")
                    wt(GPS, "V", snap[f"f8_{s}_{ts}"])
                    dma(GPS, "ST",
                        y_h.ap()[(s * 2 + ts) * 128:(s * 2 + ts + 1) * 128, :],
                        X1[:, ts * 1024:(ts + 1) * 1024], label=f"st_{s * 2 + ts}")
                    if s + 1 < NSC:
                        wt(SYN, "V", snap[f"f8_{s}_{ts}"])
                        dma(SYN, "LD", MB[:, ts * 1024:(ts + 1) * 1024],
                            mkap[((s + 1) * 2 + ts) * 128:((s + 1) * 2 + ts + 1) * 128, :],
                            label=f"mk_{(s + 1) * 2 + ts}")
                return f

            for k in range(8):
                Q.append((f"w1_{s}_{k}", qW1(k)))
            for m in range(8):
                Q.append((f"w2_{s}_0_{m}", qW2(0, m)))
            Q.append((f"epi_{s}_0", qEpi(0)))
            for m in range(8):
                Q.append((f"w2_{s}_1_{m}", qW2(1, m)))
            Q.append((f"epi_{s}_1", qEpi(1)))

        def ln1_stats(ci):
            sl_ = ci % 2
            x0_ = X0[:, sl_ * 1024:(sl_ + 1) * 1024]
            wt(VEC, "LD", snap[f"x0_{ci}"])
            op(VEC, lambda e: e.reduce_sum(ST_[:, 0:1], x0_, axis=AX.X))
            op(VEC, lambda e: e.drain())
            mark(VEC, "V")
            wt(SCA, "LD", snap[f"x0_{ci}"])
            if ci > 0:
                wt(SCA, "V", snap[f"a3v_{ci - 1}"])
            op(SCA, lambda e: e.activation(SQJ[:], x0_, AF.Square,
                                           accum_out=ST_[:, 1:2]))
            mark(SCA, "S", f"sq1_{ci}")

        def ln1_norm(ci):
            sl_ = ci % 2
            x0_ = X0[:, sl_ * 1024:(sl_ + 1) * 1024]
            mark(VEC, "V", f"a3v_{ci}")
            op(VEC, lambda e, sl_=sl_, x0_=x0_: e.tensor_scalar(
                LT1[:, sl_ * 1024:(sl_ + 1) * 1024], x0_,
                ST_[:, 6:7], ST_[:, 4:5], ALU.add, ALU.mult))
            mark(VEC, "V", f"lt1_{ci}")

        LN1S = (0, 1, 6, 2, 3, 4, 5)
        LN2S = [(8, 10, 20, 12, 14, 16, 18), (9, 11, 21, 13, 15, 17, 19)]

        def emit_ln1(ci):
            ln1_stats(ci)
            wt(VEC, "S", snap[f"sq1_{ci}"])
            _ln_chains(eng, E, op, ST_, [LN1S])
            ln1_norm(ci)

        emit_ln1(0)
        for sc in range(NSC):
            for c in range(2):
                ci = 2 * sc + c
                sl = ci % 2
                x0 = X0[:, sl * 1024:(sl + 1) * 1024]
                LT1c = LT1[:, sl * 1024:(sl + 1) * 1024]
                drain(1)

                if ci == 0:
                    wt(TEN, "LD", snap["consts"])
                wt(TEN, "V", snap[f"lt1_{ci}"])
                if ci > 0:
                    wt(TEN, "S", snap[f"k2f_{ci - 1}"])
                for i in range(8):
                    op(TEN, lambda e, i=i, LT1c=LT1c: e.transpose(
                        psT[:, i * 128:(i + 1) * 128],
                        LT1c[:, i * 128:(i + 1) * 128], IDB[:]))
                mark(TEN, "PE", f"ln1tr_{ci}")
                wt(SCA, "PE", snap[f"ln1tr_{ci}"])
                op(SCA, lambda e: e.copy(L1F[:], psT))
                mark(SCA, "S", f"l1f_{ci}")
                drain(1)

                if ci == 0:
                    wt(TEN, "WL", snap["w_ab"])
                wt(TEN, "S", snap[f"l1f_{ci}"])
                if ci > 0:
                    wt(TEN, "V", snap[f"x1_{ci - 1}"])
                dest = {0: QT, 1: QT, 2: KT, 3: KT, 4: VT, 5: VT}
                def ab_group(gg):
                    out = psAB[gg]
                    for i in range(8):
                        op(TEN, lambda e, i=i, gg=gg, out=out: e.matmul(
                            out, WAB[:, i * 16 + gg * 8: i * 16 + gg * 8 + 8],
                            L1F[:, i * 128:(i + 1) * 128],
                            start=(i == 0), stop=(i == 7)))

                for g in range(6):
                    bank = psP[g % 2]
                    if g >= 2:
                        wt(TEN, "S", snap[f"pj_{ci}_{g - 2}"])
                    elif ci > 0:
                        wt(TEN, "S", snap[f"pj_{ci - 1}_{4 + g}"])
                    else:
                        wt(TEN, "V", snap["bor"])
                    if g == 4:
                        ab_group(0)
                    for i in range(8):
                        op(TEN, lambda e, i=i, g=g, bank=bank: e.matmul(
                            bank, L1F[:, i * 128:(i + 1) * 128],
                            WQKV[:, i * 3072 + g * 512: i * 3072 + (g + 1) * 512],
                            start=(i == 0), stop=(i == 7)))
                    mark(TEN, "PE", f"pjpe_{ci}_{g}")
                    wt(SCA, "PE", snap[f"pjpe_{ci}_{g}"])
                    op(SCA, lambda e, g=g, bank=bank: e.copy(
                        dest[g][:, (g % 2) * 512:((g % 2) + 1) * 512], bank))
                    mark(SCA, "S", f"pj_{ci}_{g}")
                ab_group(1)
                mark(TEN, "PE", f"abpe_{ci}")
                drain(1)
                wt(SCA, "PE", snap[f"abpe_{ci}"])
                if ci > 0:
                    wt(SCA, "V", snap[f"a9v_{ci - 1}"])
                op(SCA, lambda e: e.activation(GSIG[:, 0:128], psAB[0], AF.Sigmoid,
                                               bias=BAB[:, 0:1]))
                op(SCA, lambda e: e.activation(GSIG[:, 128:256], psAB[1], AF.Sigmoid,
                                               bias=BAB[:, 1:2]))
                mark(SCA, "S", f"sig_{ci}")

                wt(VEC, "S", snap[f"sig_{ci}"])
                op(VEC, lambda e: e.tensor_scalar(
                    GBET[:], GSIG[:, 0:128], 0.14900000393390656, 0.85,
                    ALU.mult, ALU.add))
                op(VEC, lambda e: e.tensor_tensor_scan(
                    GD[:], GBET[:], ZER8[:], 1.0, ALU.mult, ALU.add))
                op(VEC, lambda e: e.reciprocal(GRD[:], GD[:]))
                op(VEC, lambda e: e.drain())
                op(VEC, lambda e: e.tensor_scalar(
                    GPK[:, 0:128], GD[:], GRD[:, 127:128], None, ALU.mult))
                op(VEC, lambda e: e.scalar_tensor_tensor(
                    GPK[:, 128:256], GRD[:], GD[:, 127:128], GSIG[:, 128:256],
                    ALU.mult, ALU.mult))
                op(VEC, lambda e: e.tensor_scalar(
                    GPK[:, 256:384], ZER8[:], GD[:, 127:128], None, ALU.add))
                mark(VEC, "V", f"a9v_{ci}")

                wt(TEN, "V", snap[f"a9v_{ci}"])
                wt(TEN, "S", snap[f"sig_{ci}"])
                for j in range(3):
                    op(TEN, lambda e, j=j: e.transpose(
                        psG[:, j * 8:(j + 1) * 8],
                        GPK[0:8, j * 128:(j + 1) * 128], IDF[0:8, 0:8]))
                mark(TEN, "PE", f"gpe_{ci}")
                wt(SCA, "PE", snap[f"gpe_{ci}"])
                op(SCA, lambda e: e.copy(GT[:], psG))
                mark(SCA, "S", f"gt_{ci}")

                wt(VEC, "LD", snap[f"ct_{ci}"])
                wt(VEC, "S", cnt["S"])
                for TSR, gofs, lbl in ((QT, 0, f"qr_{ci}"), (KT, 8, f"kr_{ci}")):
                    op(VEC, lambda e, TSR=TSR: e.tensor_copy(
                        AP(SWQ, 0, [[1024, 128], [128, 8], [1, 64]]),
                        AP(TSR, 64, [[1024, 128], [128, 8], [1, 64]])))
                    op(VEC, lambda e, TSR=TSR: e.tensor_copy(
                        AP(SWQ, 64, [[1024, 128], [128, 8], [1, 64]]),
                        AP(TSR, 0, [[1024, 128], [128, 8], [1, 64]])))
                    op(VEC, lambda e: e.tensor_tensor(
                        SWQ[:], SWQ[:],
                        AP(CT, sl * 256 + 128, [[512, 128], [0, 8], [1, 128]]),
                        ALU.mult))
                    op(VEC, lambda e, TSR=TSR: e.tensor_tensor(
                        TSR[:], TSR[:],
                        AP(CT, sl * 256, [[512, 128], [0, 8], [1, 128]]),
                        ALU.mult))
                    op(VEC, lambda e, TSR=TSR: e.tensor_tensor(
                        TSR[:], TSR[:], SWQ[:], ALU.add))
                    for h in range(H):
                        op(VEC, lambda e, TSR=TSR, h=h, gofs=gofs: e.tensor_scalar(
                            TSR[:, h * 128:(h + 1) * 128],
                            TSR[:, h * 128:(h + 1) * 128],
                            GT[:, gofs + h: gofs + h + 1], None, ALU.mult))
                    mark(VEC, "V", lbl)
                drain(1)

                wt(TEN, "V", snap[f"qr_{ci}"])
                for i in range(8):
                    op(TEN, lambda e, i=i: e.transpose(
                        psT[:, i * 128:(i + 1) * 128],
                        QT[:, i * 128:(i + 1) * 128], IDB[:]))
                mark(TEN, "PE", f"qtr_{ci}")
                wt(VEC, "PE", snap[f"qtr_{ci}"])
                op(VEC, lambda e: e.tensor_copy(Q2F[:], psT))
                mark(VEC, "V", f"q2f_{ci}")
                wt(TEN, "V", max(snap[f"q2f_{ci}"], snap[f"kr_{ci}"]))
                for i in range(8):
                    op(TEN, lambda e, i=i: e.transpose(
                        psT[:, i * 128:(i + 1) * 128],
                        KT[:, i * 128:(i + 1) * 128], IDB[:]))
                mark(TEN, "PE", f"ktr_{ci}")
                wt(SCA, "PE", snap[f"ktr_{ci}"])
                op(SCA, lambda e: e.copy(K2F[:], psT))
                mark(SCA, "S", f"k2f_{ci}")

                for h in range(H):
                    op(VEC, lambda e, h=h: e.tensor_scalar(
                        SH16[:, h * 128:(h + 1) * 128],
                        SST[:, h * 128:(h + 1) * 128],
                        GT[:, 16 + h:17 + h], None, ALU.mult))
                mark(VEC, "V", f"shat_{ci}")
                drain(1)

                wt(TEN, "V", snap[f"q2f_{ci}"])
                wt(TEN, "S", snap[f"k2f_{ci}"])
                wt(TEN, "V", snap[f"x1_{ci - 1}"] if ci > 0 else snap["bor"])
                for g in range(2):
                    for j, h in enumerate(range(4 * g, 4 * g + 4)):
                        op(TEN, lambda e, g=g, j=j, h=h: e.matmul(
                            psAg[g][:, j * 128:(j + 1) * 128],
                            K2F[:, h * 128:(h + 1) * 128],
                            Q2F[:, h * 128:(h + 1) * 128],
                            start=True, stop=True))
                    mark(TEN, "PE", f"ape_{ci}_{g}")
                for g in range(2):
                    wt(VEC, "PE", snap[f"ape_{ci}_{g}"])
                    op(VEC, lambda e, g=g: e.tensor_tensor(
                        AM[:, g * 512:(g + 1) * 512], psAg[g], AMKb4, ALU.mult))
                    mark(VEC, "V", f"am_{ci}_{g}")
                drain(1)

                def o_group(g):
                    for j, h in enumerate(range(4 * g, 4 * g + 4)):
                        s_ = slice(h * 128, (h + 1) * 128)
                        js = slice(j * 128, (j + 1) * 128)
                        op(TEN, lambda e, g=g, s_=s_, js=js: e.matmul(
                            psOg[g][:, js], VT[:, s_],
                            AM[:, s_], start=True, stop=False))
                        op(TEN, lambda e, g=g, s_=s_, js=js: e.matmul(
                            psOg[g][:, js], SH16[:, s_], Q2F[:, s_],
                            start=False, stop=True))
                    mark(TEN, "PE", f"ope_{ci}_{g}")

                def t_group(g):
                    for j, h in enumerate(range(4 * g, 4 * g + 4)):
                        s_ = slice(h * 128, (h + 1) * 128)
                        js = slice(j * 128, (j + 1) * 128)
                        op(TEN, lambda e, g=g, s_=s_, js=js: e.matmul(
                            psT1g[g][:, js], KT[:, s_], VT[:, s_],
                            start=True, stop=True))
                    mark(TEN, "PE", f"t1pe_{ci}_{g}")

                def s_update(g):
                    wt(VEC, "PE", snap[f"t1pe_{ci}_{g}"])
                    for j, h in enumerate(range(4 * g, 4 * g + 4)):
                        s_ = slice(h * 128, (h + 1) * 128)
                        js = slice(j * 128, (j + 1) * 128)
                        op(VEC, lambda e, g=g, s_=s_, js=js, h=h:
                           e.scalar_tensor_tensor(
                               SST[:, s_], SST[:, s_], GT[:, 16 + h:17 + h],
                               psT1g[g][:, js], ALU.mult, ALU.add))
                    mark(VEC, "V", f"supd_{ci}_{g}")

                def of_evac(g):
                    wt(SCA, "PE", snap[f"ope_{ci}_{g}"])
                    op(SCA, lambda e, g=g: e.copy(
                        OF[:, g * 512:(g + 1) * 512], psOg[g]))
                    mark(SCA, "S", f"of_{ci}_{g}")

                wt(TEN, "V", snap[f"am_{ci}_0"])
                wt(TEN, "S", snap[f"gt_{ci}"])
                o_group(0)
                t_group(0)
                wt(TEN, "V", snap[f"am_{ci}_1"])
                o_group(1)
                of_evac(0)
                s_update(0)
                wt(TEN, "S", snap[f"of_{ci}_0"])
                t_group(1)
                of_evac(1)
                s_update(1)
                if c == 0:
                    if ci + 1 < NCH:
                        emit_ln1(ci + 1)
                else:
                    if ci + 1 < NCH:
                        ln1_stats(ci + 1)
                        wt(VEC, "S", snap[f"sq1_{ci + 1}"])
                        _ln_chains(eng, E, op, ST_, [LN1S, LN2S[0]])
                        ln1_norm(ci + 1)
                    else:
                        wt(VEC, "S", cnt["S"])
                        _ln_chains(eng, E, op, ST_, [LN2S[0]])
                        mark(VEC, "V")
                drain(1)

                if ci == 0:
                    wt(TEN, "WL", snap["w_wo"])
                wt(TEN, "V", snap[f"supd_{ci}_0"])
                wt(TEN, "S", snap[f"of_{ci}_1"])
                for mh in range(2):
                    for e8 in range(8):
                        op(TEN, lambda e, mh=mh, e8=e8: e.matmul(
                            psW[mh], OF[:, e8 * 128:(e8 + 1) * 128],
                            WO[:, e8 * 1024 + mh * 512: e8 * 1024 + (mh + 1) * 512],
                            start=(e8 == 0), stop=False))
                    op(TEN, lambda e, mh=mh: e.matmul(
                        psW[mh], C128[:], BOR[:, mh * 512:(mh + 1) * 512],
                        start=False, stop=True))
                mark(TEN, "PE", f"wo_{ci}")

                if sc > 0:
                    drain_to(f"epi_{sc - 1}_{c}")
                if c == 1:
                    op(VEC, lambda e: e.tensor_scalar(
                        L2T[:], X1[:, 0:1024],
                        ST_[:, 20:21], ST_[:, 16:17], ALU.add, ALU.mult))
                    mark(VEC, "V", f"ln2n_{sc}_0")
                    wt(TEN, "V", snap[f"ln2n_{sc}_0"])
                    if sc > 0:
                        wt(TEN, "S", snap[f"yb_{sc - 1}_1"])
                    for i in range(8):
                        op(TEN, lambda e, i=i: e.transpose(
                            psL[:, i * 128:(i + 1) * 128],
                            L2T[:, i * 128:(i + 1) * 128], IDF[:]))
                    mark(TEN, "PE", f"f3pe_{sc}_0")
                    wt(VEC, "PE", snap[f"f3pe_{sc}_0"])
                    op(VEC, lambda e: e.tensor_copy(
                        AP(L2F, 0, [[2048, 128], [256, 8], [1, 128]]),
                        AP(PSA, 0, [[1024, 128], [128, 8], [1, 128]])))
                    mark(VEC, "V", f"l2f_{sc}_0")

                wt(VEC, "PE", snap[f"wo_{ci}"])
                if sc > 0:
                    wt(VEC, "ST", snap[f"st_{(sc - 1) * 2 + c}"])
                op(VEC, lambda e, c=c: e.tensor_tensor(
                    X1[:, c * 1024:c * 1024 + 512], psW[0], x0[:, 0:512], ALU.add))
                op(VEC, lambda e, c=c: e.tensor_tensor(
                    X1[:, c * 1024 + 512:(c + 1) * 1024], psW[1],
                    x0[:, 512:1024], ALU.add))
                mark(VEC, "V", f"x1_{ci}")

                op(VEC, lambda e, c=c: e.reduce_sum(
                    ST_[:, 8 + c:9 + c], X1[:, c * 1024:(c + 1) * 1024], axis=AX.X))
                op(VEC, lambda e: e.drain())
                mark(VEC, "V")
                wt(SCA, "V", snap[f"x1_{ci}"])
                op(SCA, lambda e, c=c: e.activation(
                    SQJ[:], X1[:, c * 1024:(c + 1) * 1024], AF.Square,
                    accum_out=ST_[:, 10 + c:11 + c]))
                mark(SCA, "S", f"fsq_{ci}")
                if c == 1:
                    wt(VEC, "S", snap[f"fsq_{ci}"])
                    _ln_chains(eng, E, op, ST_, [LN2S[1]])
                    mark(VEC, "V")

                if ci + 2 < NCH:
                    wt(SYN, "V", snap[f"x1_{ci}"])
                    dma(SYN, "LD", X0[:, sl * 1024:(sl + 1) * 1024],
                        xap[(ci + 2) * 128:(ci + 3) * 128, :], label=f"x0_{ci + 2}")
                    dma(SYN, "LD", CT[:, sl * 256:(sl + 1) * 256],
                        ctap[(ci + 2) * 128:(ci + 3) * 128, :], label=f"ct_{ci + 2}")

            drain_all()
            op(VEC, lambda e: e.tensor_scalar(
                L2T[:], X1[:, 1024:2048],
                ST_[:, 21:22], ST_[:, 17:18], ALU.add, ALU.mult))
            mark(VEC, "V", f"ln2n_{sc}_1")
            wt(TEN, "V", snap[f"ln2n_{sc}_1"])
            for i in range(8):
                op(TEN, lambda e, i=i: e.transpose(
                    psL[:, i * 128:(i + 1) * 128],
                    L2T[:, i * 128:(i + 1) * 128], IDF[:]))
            mark(TEN, "PE", f"f3pe_{sc}_1")
            wt(VEC, "PE", snap[f"f3pe_{sc}_1"])
            op(VEC, lambda e: e.tensor_copy(
                AP(L2F, 128, [[2048, 128], [256, 8], [1, 128]]),
                AP(PSA, 0, [[1024, 128], [128, 8], [1, 128]])))
            mark(VEC, "V", f"l2f_{sc}_1")
            build_ffn_quanta(sc)

        drain_all()
        wt(GPS, "ST", cnt["ST"])

    def _ln_chain(eng, E, op, ST, c_sum, c_sq, c_negm, c_m2, c_var, c_y, c_t, w):
        _ln_chains(eng, E, op, ST, [(c_sum, c_sq, c_negm, c_m2, c_var, c_y, c_t)])

    def _ln_chains(eng, E, op, ST, specs):
        n = len(specs)
        dr = lambda: op("vector", lambda e: e.drain())

        def rnd(fn):
            for sp in specs:
                c_sum, c_sq, c_negm, c_m2, c_var, c_y, c_t = sp
                sl = lambda c: ST[:, c:c + 1]
                sli = lambda c: ST[:, c:c + 1].bitcast(I32)
                fn(sl, sli, c_sum, c_sq, c_negm, c_m2, c_var, c_y, c_t)
            if n < 3:
                dr()

        rnd(lambda sl, sli, c_sum, c_sq, c_negm, c_m2, c_var, c_y, c_t:
            op("vector", lambda e: e.tensor_scalar(
                sl(c_var), sl(c_sq), 1.0 / D, EPS, ALU.mult, ALU.add)))
        rnd(lambda sl, sli, c_sum, c_sq, c_negm, c_m2, c_var, c_y, c_t:
            op("vector", lambda e: e.tensor_scalar_mul(
                sl(c_negm), sl(c_sum), -1.0 / D)))
        rnd(lambda sl, sli, c_sum, c_sq, c_negm, c_m2, c_var, c_y, c_t:
            op("vector", lambda e: e.tensor_scalar(
                sl(c_m2), sl(c_sum), -1.0 / D, sl(c_negm), ALU.mult, ALU.mult)))
        rnd(lambda sl, sli, c_sum, c_sq, c_negm, c_m2, c_var, c_y, c_t:
            op("vector", lambda e: e.tensor_scalar(
                sl(c_var), sl(c_var), sl(c_m2), None, ALU.subtract)))
        rnd(lambda sl, sli, c_sum, c_sq, c_negm, c_m2, c_var, c_y, c_t:
            op("vector", lambda e: e.tensor_scalar(
                sli(c_y), sli(c_var), 1, 0xFFFFFFFF,
                ALU.arith_shift_right, ALU.bitwise_xor)))
        rnd(lambda sl, sli, c_sum, c_sq, c_negm, c_m2, c_var, c_y, c_t:
            op("vector", lambda e: e.tensor_single_scalar(
                sli(c_y), sli(c_y), MAGIC, ALU.add)))
        for _ in range(2):
            rnd(lambda sl, sli, c_sum, c_sq, c_negm, c_m2, c_var, c_y, c_t:
                op("vector", lambda e: e.tensor_scalar(
                    sl(c_t), sl(c_var), sl(c_y), None, ALU.mult)))
            rnd(lambda sl, sli, c_sum, c_sq, c_negm, c_m2, c_var, c_y, c_t:
                op("vector", lambda e: e.tensor_scalar(
                    sl(c_t), sl(c_t), sl(c_y), -0.5, ALU.mult, ALU.mult)))
            rnd(lambda sl, sli, c_sum, c_sq, c_negm, c_m2, c_var, c_y, c_t:
                op("vector", lambda e: e.tensor_scalar(
                    sl(c_t), sl(c_t), 1.5, None, ALU.add)))
            rnd(lambda sl, sli, c_sum, c_sq, c_negm, c_m2, c_var, c_y, c_t:
                op("vector", lambda e: e.tensor_scalar(
                    sl(c_y), sl(c_y), sl(c_t), None, ALU.mult)))

    with nc.Block() as block:
        @block.sync
        def _(eng):
            schedule("sync", eng)

        @block.gpsimd
        def _(eng):
            schedule("gpsimd", eng)

        @block.tensor
        def _(eng):
            schedule("tensor", eng)

        @block.vector
        def _(eng):
            schedule("vector", eng)

        @block.scalar
        def _(eng):
            schedule("scalar", eng)

    return nc


_NC_CACHE = [None]


def _host_prep(L0_attn_Wq, L0_attn_Wk, L0_attn_Wv, L0_attn_Wbeta, L0_attn_bbeta,
               L0_attn_Walpha, L0_attn_balpha, L0_attn_Wo, L0_attn_bo,
               L0_ffn_W1, L0_ffn_b1, L0_ffn_W2, L0_ffn_b2):
    c = np.ascontiguousarray

    perm = np.concatenate([np.arange(0, DH, 2), np.arange(1, DH, 2)])
    wq = np.asarray(L0_attn_Wq, dtype=np.float32)
    wk = np.asarray(L0_attn_Wk, dtype=np.float32)
    wq_p = wq.reshape(H, DH, D)[:, perm, :].reshape(D, D)
    wk_p = wk.reshape(H, DH, D)[:, perm, :].reshape(D, D)
    wqkv = c(np.concatenate(
        [wq_p.T, wk_p.T, np.asarray(L0_attn_Wv, np.float32).T], axis=1))
    wab = c(np.concatenate(
        [np.asarray(L0_attn_Wbeta, np.float32).T,
         np.asarray(L0_attn_Walpha, np.float32).T], axis=1))
    bab = c(np.concatenate(
        [np.asarray(L0_attn_bbeta, np.float32),
         np.asarray(L0_attn_balpha, np.float32)]).reshape(1, 16))
    wo = c(np.asarray(L0_attn_Wo, np.float32).T)
    bo = c(np.asarray(L0_attn_bo, np.float32).reshape(1, D))
    w1 = c(np.asarray(L0_ffn_W1, np.float32).T)
    b1 = c(np.asarray(L0_ffn_b1, np.float32).reshape(1, 4 * D))
    w2 = c(np.asarray(L0_ffn_W2, np.float32).T)
    b2 = c(np.asarray(L0_ffn_b2, np.float32).reshape(1, D))

    half = DH // 2
    inv_freq = (np.float32(1.0) /
                np.power(np.float32(10000.0),
                         np.arange(half, dtype=np.float32) / np.float32(half)))
    freqs = np.arange(L, dtype=np.float32)[:, None] * inv_freq[None, :]
    cos = np.cos(freqs).astype(np.float32)
    sin = np.sin(freqs).astype(np.float32)
    ct = c(np.concatenate([cos, cos, -sin, sin], axis=1)).astype(ml_dtypes.bfloat16)

    idb = np.eye(128, dtype=ml_dtypes.bfloat16)
    idf = np.eye(128, dtype=np.float32)
    amk = np.triu(np.ones((128, 128), dtype=np.float32)).astype(ml_dtypes.bfloat16)

    import jax
    with jax.default_device(jax.devices("cpu")[0]):
        keep = np.asarray(jax.random.bernoulli(
            jax.random.key(42), 0.8, (NCORES, L, D)))
    mask = np.where(keep, np.float32(1.25), np.float32(0.0)).astype(ml_dtypes.bfloat16)

    shared = dict(wqkv=wqkv, wab=wab, wo=wo, w1=w1, w2=w2, babi=bab, boi=bo,
                  b1i=b1, b2i=b2, ct=ct, idb=idb, idf=idf, amk=amk)
    return shared, mask


def kernel(L0_attn_x, L0_attn_Wq, L0_attn_Wk, L0_attn_Wv, L0_attn_Wbeta,
           L0_attn_bbeta, L0_attn_Walpha, L0_attn_balpha, L0_attn_Wo,
           L0_attn_bo, L0_ffn_W1, L0_ffn_b1, L0_ffn_W2, L0_ffn_b2):
    shared, mask = _host_prep(
        L0_attn_Wq, L0_attn_Wk, L0_attn_Wv, L0_attn_Wbeta, L0_attn_bbeta,
        L0_attn_Walpha, L0_attn_balpha, L0_attn_Wo, L0_attn_bo,
        L0_ffn_W1, L0_ffn_b1, L0_ffn_W2, L0_ffn_b2)

    x = np.asarray(L0_attn_x, dtype=np.float32)
    in_maps = []
    for b in range(NCORES):
        m = dict(shared)
        m["x"] = np.ascontiguousarray(x[b])
        m["mk"] = np.ascontiguousarray(mask[b])
        in_maps.append(m)

    if _NC_CACHE[0] is None:
        _NC_CACHE[0] = build_program()
    nc = _NC_CACHE[0]
    res = run_bass_kernel_spmd(nc, in_maps, core_ids=list(range(NCORES)))
    out = np.stack([res.results[b]["y_out"] for b in range(NCORES)], axis=0)
    return out.astype(np.float32)


# version 23
# speedup vs baseline: 1.5378x; 1.0784x over previous
import numpy as np
import ml_dtypes
import concourse.bass as bass
import concourse.mybir as mybir
from concourse.bass_utils import run_bass_kernel_spmd
from concourse.ap import AP

F32 = mybir.dt.float32
BF16 = mybir.dt.bfloat16
F8 = mybir.dt.float8e5
I32 = mybir.dt.int32
AF = mybir.ActivationFunctionType
ALU = mybir.AluOpType
AX = mybir.AxisListType

NCORES = 8
L = 4096
D = 1024
H = 8
DH = 128
NCH = 32
NSC = 16
EPS = 1e-5
MAGIC = 0x5F3759DF + 1

_DBG = {"taps": None}


def build_program():
    nc = bass.Bass("TRN2", target_bir_lowering=False, debug=False)

    x_h = nc.dram_tensor("x", [L, D], F32, kind="ExternalInput")
    wqkv_h = nc.dram_tensor("wqkv", [D, 3 * D], F32, kind="ExternalInput")
    wab_h = nc.dram_tensor("wab", [D, 16], F32, kind="ExternalInput")
    wo_h = nc.dram_tensor("wo", [D, D], F32, kind="ExternalInput")
    w1_h = nc.dram_tensor("w1", [D, 4 * D], F32, kind="ExternalInput")
    w2_h = nc.dram_tensor("w2", [4 * D, D], F32, kind="ExternalInput")
    bab_h = nc.dram_tensor("babi", [1, 16], F32, kind="ExternalInput")
    bo_h = nc.dram_tensor("boi", [1, D], F32, kind="ExternalInput")
    b1_h = nc.dram_tensor("b1i", [1, 4 * D], F32, kind="ExternalInput")
    b2_h = nc.dram_tensor("b2i", [1, D], F32, kind="ExternalInput")
    ct_h = nc.dram_tensor("ct", [L, 256], BF16, kind="ExternalInput")
    mk_h = nc.dram_tensor("mk", [L, D], BF16, kind="ExternalInput")
    idb_h = nc.dram_tensor("idb", [128, 128], BF16, kind="ExternalInput")
    idf_h = nc.dram_tensor("idf", [128, 128], F32, kind="ExternalInput")
    amk_h = nc.dram_tensor("amk", [128, 128], BF16, kind="ExternalInput")
    y_h = nc.dram_tensor("y_out", [L, D], F32, kind="ExternalOutput")

    dbg_handles = {}
    if _DBG["taps"]:
        for name, shp in _DBG["taps"].items():
            dbg_handles[name] = nc.dram_tensor("dbg_" + name, shp, F32,
                                               kind="ExternalOutput")

    A = nc.alloc_sbuf_tensor
    WQKV = A("WQKV", [128, 8 * 3072], BF16)
    WAB = A("WAB", [128, 128], BF16)
    WO = A("WO", [128, 8 * 1024], BF16)
    W1T = A("W1T", [128, 8 * 4096], F8)
    W2T = A("W2T", [128, 32 * 1024], F8)
    BAB = A("BAB", [8, 2], BF16)
    BO = A("BO", [1, D], BF16)
    B1 = A("B1", [128, 32], F8)
    B2 = A("B2", [1, D], F8)
    ONEB = A("ONEB", [1, 512], BF16)
    ONE8 = A("ONE8", [1, 512], F8)
    IDB = A("IDB", [128, 128], BF16)
    C128 = A("C128", [128, 128], BF16)
    BOR = A("BOR", [128, 1024], BF16)
    IDF = A("IDF", [128, 128], F32)
    AMK = A("AMK", [128, 128], BF16)

    X0 = A("X0", [128, 2048], F32)
    CT = A("CT", [128, 512], BF16)
    MB = A("MB", [128, 2048], BF16)
    X1 = A("X1", [128, 2048], F32)
    LT1 = A("LT1", [128, 2048], BF16)
    L1F = A("L1F", [128, 1024], BF16)
    QT = A("QT", [128, 1024], BF16)
    KT = A("KT", [128, 1024], BF16)
    VT = A("VT", [128, 1024], BF16)
    SWQ = A("SWQ", [128, 1024], BF16)
    YB = A("YB", [128, 1024], BF16)
    Q2F = A("Q2F", [128, 1024], BF16)
    K2F = A("K2F", [128, 1024], BF16)
    AM = A("AM", [128, 1024], BF16)
    OF = A("OF", [128, 1024], BF16)
    SST = A("SST", [128, 1024], F32)
    SH16 = A("SH16", [128, 1024], BF16)
    GSIG = A("GSIG", [8, 256], F32)
    GBET = A("GBET", [8, 128], F32)
    GD = A("GD", [8, 128], F32)
    GRD = A("GRD", [8, 128], F32)
    GPK = A("GPK", [8, 384], F32)
    GT = A("GT", [128, 24], F32)
    ZER8 = A("ZER8", [8, 128], F32)
    ST_ = A("ST_", [128, 32], F32)
    SQJ = A("SQJ", [128, 1024], F8)
    L2T = A("L2T", [128, 1024], F32)
    L2F = A("L2F", [128, 2048], F8)
    HB = A("HB", [128, 512], BF16)
    H8 = A("H8", [128, 8192], F8)

    PSA = nc.alloc_psum_tensor("PSA", [128, 1024], F32)
    PSB = nc.alloc_psum_tensor("PSB", [128, 1024], F32)
    PSC = nc.alloc_psum_tensor("PSC", [128, 1024], F32)
    PSD = nc.alloc_psum_tensor("PSD", [128, 1024], F32)

    sems = {
        "LD": nc.alloc_semaphore(name="sLD"),
        "WL": nc.alloc_semaphore(name="sWL"),
        "ST": nc.alloc_semaphore(name="sST"),
        "PE": nc.alloc_semaphore(name="sPE"),
        "G": nc.alloc_semaphore(name="sG"),
        "V": nc.alloc_semaphore(name="sV"),
        "S": nc.alloc_semaphore(name="sS"),
    }

    def schedule(E, eng):
        cnt = {k: 0 for k in sems}
        snap = {}
        last = [None]

        def dma(q, sem, out, in_, label=None, **kw):
            cnt[sem] += 16
            if E == q:
                eng.dma_start(out=out, in_=in_, **kw).then_inc(sems[sem], 16)
            if label is not None:
                snap[label] = cnt[sem]

        def wt(q, sem, val):
            if E == q and val > 0:
                eng.wait_ge(sems[sem], val)

        def op(q, fn):
            if E == q:
                last[0] = fn(eng)

        def mark(q, sem, label=None):
            cnt[sem] += 1
            if E == q:
                last[0].then_inc(sems[sem], 1)
            if label is not None:
                snap[label] = cnt[sem]

        TEN, VEC, SCA, SYN, GPS = "tensor", "vector", "scalar", "sync", "gpsimd"

        psL = PSA
        psH = [PSA[:, 0:256], PSA[:, 512:768]]
        psY = PSA
        psT = PSB.bitcast(BF16)[:, 0:1024]
        psP = [PSB[:, 512:1024], PSC[:, 0:512]]
        psAg = [PSC[:, 512:1024], PSD[:, 0:512]]
        psOg = [PSD[:, 512:1024], PSD[:, 0:512]]
        psT1g = [PSC[:, 512:1024], PSD[:, 512:1024]]
        psAB = [PSD[0:8, 512:640], PSD[0:8, 0:128]]
        psG = PSD[:, 512:536]
        psW = [PSC[:, 512:1024], PSD[:, 0:512]]

        Q = []
        qpos = [0]

        def drain(n=1):
            while n > 0 and qpos[0] < len(Q):
                Q[qpos[0]][1]()
                qpos[0] += 1
                n -= 1

        def drain_to(name):
            while qpos[0] < len(Q):
                nm, fn = Q[qpos[0]]
                fn()
                qpos[0] += 1
                if nm == name:
                    return

        def drain_all():
            drain(len(Q) - qpos[0])

        if E == SCA:
            eng.preload_act_table(AF.Sigmoid)
        dma(GPS, "WL", BAB[:], AP(bab_h, 0, [[1, 8], [8, 2]]),
            allow_slow_non_contiguous=True)
        dma(GPS, "WL", BO[:], bo_h.ap(), label="w_bo")
        dma(GPS, "WL", WQKV[:],
            AP(wqkv_h, 0, [[3072, 128], [128 * 3072, 8], [1, 3072]]))
        dma(GPS, "WL", WAB[:],
            AP(wab_h, 0, [[16, 128], [128 * 16, 8], [1, 16]]), label="w_ab")
        dma(GPS, "WL", WO[:],
            AP(wo_h, 0, [[1024, 128], [128 * 1024, 8], [1, 1024]]), label="w_wo")
        dma(GPS, "WL", W1T[:],
            AP(w1_h, 0, [[4096, 128], [128 * 4096, 8], [1, 4096]]))
        dma(GPS, "WL", B1[:], AP(b1_h, 0, [[1, 128], [128, 32]]),
            allow_slow_non_contiguous=True)
        for j in range(4):
            dma(GPS, "WL", W2T[:, j * 8192:(j + 1) * 8192],
                AP(w2_h, j * 8 * 128 * 1024,
                   [[1024, 128], [128 * 1024, 8], [1, 1024]]))
        dma(GPS, "WL", B2[:], b2_h.ap(), label="w_ffn")

        dma(SYN, "LD", IDB[:], idb_h.ap())
        dma(SYN, "LD", IDF[:], idf_h.ap())
        dma(SYN, "LD", AMK[:], amk_h.ap(), label="consts")
        xap = x_h.ap()
        ctap = ct_h.ap()
        mkap = mk_h.ap()
        for k in range(2):
            dma(SYN, "LD", X0[:, k * 1024:(k + 1) * 1024],
                xap[k * 128:(k + 1) * 128, :], label=f"x0_{k}")
            dma(SYN, "LD", CT[:, k * 256:(k + 1) * 256],
                ctap[k * 128:(k + 1) * 128, :], label=f"ct_{k}")
            dma(SYN, "LD", MB[:, k * 1024:(k + 1) * 1024],
                mkap[k * 128:(k + 1) * 128, :], label=f"mk_{k}")

        op(VEC, lambda e: e.memset(SST[:], 0.0))
        op(VEC, lambda e: e.memset(ZER8[:], 0.0))
        op(VEC, lambda e: e.memset(ONEB[:], 1.0))
        op(VEC, lambda e: e.memset(ONE8[:], 1.0))
        op(VEC, lambda e: e.memset(C128[:], 1.0 / 128.0))
        mark(VEC, "V", "vinit")
        wt(TEN, "WL", snap["w_bo"])
        wt(TEN, "V", snap["vinit"])
        for mh in range(2):
            op(TEN, lambda e, mh=mh: e.matmul(
                PSC[:, mh * 512:(mh + 1) * 512], ONEB[0:1, 0:128],
                BO[0:1, mh * 512:(mh + 1) * 512], start=True, stop=True))
        mark(TEN, "PE", "borpe")
        wt(VEC, "PE", snap["borpe"])
        op(VEC, lambda e: e.tensor_copy(BOR[:], PSC[:]))
        mark(VEC, "V", "bor")

        AMKb4 = AP(AMK, 0, [[128, 128], [0, 4], [1, 128]])

        def build_ffn_quanta(s):
            def qW1(k):
                def f():
                    if k == 0:
                        wt(TEN, "V", snap[f"l2f_{s}_1"])
                        if s == 0:
                            wt(TEN, "WL", snap["w_ffn"])
                    for ot in range(4 * k, 4 * k + 4):
                        bank = psH[ot % 2]
                        if ot >= 2:
                            wt(TEN, "S", snap[f"h_{s}_{ot - 2}"])
                        for kk in range(4):
                            op(TEN, lambda e, kk=kk, ot=ot, bank=bank: e.matmul(
                                bank,
                                AP(W1T, kk * 8192 + ot * 128,
                                   [[32768, 128], [4096, 2], [1, 128]]),
                                AP(L2F, kk * 512,
                                   [[2048, 128], [256, 2], [1, 256]]),
                                start=(kk == 0), stop=(kk == 3),
                                perf_mode=mybir.MatmulPerfMode.DoubleRow))
                        mark(TEN, "PE", f"w1pe_{s}_{ot}")
                        wt(SCA, "PE", snap[f"w1pe_{s}_{ot}"])
                        if ot >= 2 and ot % 2 == 0:
                            wt(SCA, "V", snap[f"h8_{s}_{ot - 2}"])
                        op(SCA, lambda e, ot=ot, bank=bank: e.activation(
                            HB[:, (ot % 2) * 256:((ot % 2) + 1) * 256], bank,
                            AF.Relu, bias=B1[:, ot:ot + 1]))
                        mark(SCA, "S", f"h_{s}_{ot}")
                        if ot % 2 == 0:
                            wt(VEC, "S", snap[f"h_{s}_{ot}"])
                            op(VEC, lambda e, ot=ot: e.tensor_copy(
                                H8[:, ot * 256:(ot + 1) * 256],
                                HB[:, (ot % 2) * 256:((ot % 2) + 1) * 256]))
                            mark(VEC, "V", f"h8_{s}_{ot}")
                        else:
                            op(SCA, lambda e, ot=ot: e.copy(
                                H8[:, ot * 256:(ot + 1) * 256],
                                HB[:, (ot % 2) * 256:((ot % 2) + 1) * 256]))
                            mark(SCA, "S", f"h8_{s}_{ot}")
                return f

            def qW2(ts, m):
                def f():
                    if m == 0:
                        wt(TEN, "V", snap[f"h8_{s}_30"])
                        wt(TEN, "S", snap[f"h8_{s}_31"])
                        if ts == 1:
                            wt(TEN, "S", snap[f"yb_{s}_0"])
                    for kk in range(2 * m, 2 * m + 2):
                        for eh in range(2):
                            op(TEN, lambda e, kk=kk, eh=eh, ts=ts: e.matmul(
                                psY[:, eh * 512:(eh + 1) * 512],
                                AP(H8, kk * 512 + ts * 128,
                                   [[8192, 128], [256, 2], [1, 128]]),
                                AP(W2T, kk * 2048 + eh * 512,
                                   [[32768, 128], [1024, 2], [1, 512]]),
                                start=(kk == 0), stop=False,
                                perf_mode=mybir.MatmulPerfMode.DoubleRow))
                    if m == 7:
                        for eh in range(2):
                            op(TEN, lambda e, eh=eh: e.matmul(
                                psY[:, eh * 512:(eh + 1) * 512],
                                ONE8[0:1, 0:128], B2[0:1, eh * 512:(eh + 1) * 512],
                                start=False, stop=True))
                        mark(TEN, "PE", f"w2_{s}_{ts}")
                return f

            def qEpi(ts):
                def f():
                    wt(SCA, "PE", snap[f"w2_{s}_{ts}"])
                    if ts == 1:
                        wt(SCA, "V", snap[f"f8_{s}_0"])
                    op(SCA, lambda e: e.copy(YB[:], psY[:]))
                    mark(SCA, "S", f"yb_{s}_{ts}")
                    wt(VEC, "S", snap[f"yb_{s}_{ts}"])
                    wt(VEC, "LD", snap[f"mk_{s * 2 + ts}"])
                    op(VEC, lambda e, ts=ts: e.tensor_tensor(
                        L2T[:], YB[:], MB[:, ts * 1024:(ts + 1) * 1024], ALU.mult))
                    op(VEC, lambda e, ts=ts: e.tensor_tensor(
                        X1[:, ts * 1024:(ts + 1) * 1024], L2T[:],
                        X1[:, ts * 1024:(ts + 1) * 1024], ALU.add))
                    mark(VEC, "V", f"f8_{s}_{ts}")
                    wt(GPS, "V", snap[f"f8_{s}_{ts}"])
                    dma(GPS, "ST",
                        y_h.ap()[(s * 2 + ts) * 128:(s * 2 + ts + 1) * 128, :],
                        X1[:, ts * 1024:(ts + 1) * 1024], label=f"st_{s * 2 + ts}")
                    if s + 1 < NSC:
                        wt(SYN, "V", snap[f"f8_{s}_{ts}"])
                        dma(SYN, "LD", MB[:, ts * 1024:(ts + 1) * 1024],
                            mkap[((s + 1) * 2 + ts) * 128:((s + 1) * 2 + ts + 1) * 128, :],
                            label=f"mk_{(s + 1) * 2 + ts}")
                return f

            for k in range(8):
                Q.append((f"w1_{s}_{k}", qW1(k)))
            for m in range(8):
                Q.append((f"w2_{s}_0_{m}", qW2(0, m)))
            Q.append((f"epi_{s}_0", qEpi(0)))
            for m in range(8):
                Q.append((f"w2_{s}_1_{m}", qW2(1, m)))
            Q.append((f"epi_{s}_1", qEpi(1)))

        def ln1_stats(ci):
            sl_ = ci % 2
            x0_ = X0[:, sl_ * 1024:(sl_ + 1) * 1024]
            wt(VEC, "LD", snap[f"x0_{ci}"])
            op(VEC, lambda e: e.reduce_sum(ST_[:, 0:1], x0_, axis=AX.X))
            op(VEC, lambda e: e.drain())
            mark(VEC, "V")
            wt(SCA, "LD", snap[f"x0_{ci}"])
            if ci > 0:
                wt(SCA, "V", snap[f"a3v_{ci - 1}"])
            op(SCA, lambda e: e.activation(SQJ[:], x0_, AF.Square,
                                           accum_out=ST_[:, 1:2]))
            mark(SCA, "S", f"sq1_{ci}")

        def ln1_norm(ci):
            sl_ = ci % 2
            x0_ = X0[:, sl_ * 1024:(sl_ + 1) * 1024]
            mark(VEC, "V", f"a3v_{ci}")
            op(VEC, lambda e, sl_=sl_, x0_=x0_: e.tensor_scalar(
                LT1[:, sl_ * 1024:(sl_ + 1) * 1024], x0_,
                ST_[:, 6:7], ST_[:, 4:5], ALU.add, ALU.mult))
            mark(VEC, "V", f"lt1_{ci}")

        LN1S = (0, 1, 6, 2, 3, 4, 5)
        LN2S = [(8, 10, 20, 12, 14, 16, 18), (9, 11, 21, 13, 15, 17, 19)]

        def emit_ln1(ci):
            ln1_stats(ci)
            wt(VEC, "S", snap[f"sq1_{ci}"])
            _ln_chains(eng, E, op, ST_, [LN1S])
            ln1_norm(ci)

        def boundary_block(s):
            drain_all()
            op(VEC, lambda e: e.tensor_scalar(
                L2T[:], X1[:, 1024:2048],
                ST_[:, 21:22], ST_[:, 17:18], ALU.add, ALU.mult))
            mark(VEC, "V", f"ln2n_{s}_1")
            wt(TEN, "V", snap[f"ln2n_{s}_1"])
            for i in range(8):
                op(TEN, lambda e, i=i: e.transpose(
                    psL[:, i * 128:(i + 1) * 128],
                    L2T[:, i * 128:(i + 1) * 128], IDF[:]))
            mark(TEN, "PE", f"f3pe_{s}_1")
            wt(VEC, "PE", snap[f"f3pe_{s}_1"])
            op(VEC, lambda e: e.tensor_copy(
                AP(L2F, 128, [[2048, 128], [256, 8], [1, 128]]),
                AP(PSA, 0, [[1024, 128], [128, 8], [1, 128]])))
            mark(VEC, "V", f"l2f_{s}_1")
            build_ffn_quanta(s)

        emit_ln1(0)
        for sc in range(NSC):
            for c in range(2):
                ci = 2 * sc + c
                sl = ci % 2
                x0 = X0[:, sl * 1024:(sl + 1) * 1024]
                LT1c = LT1[:, sl * 1024:(sl + 1) * 1024]
                drain(1)

                if ci == 0:
                    wt(TEN, "LD", snap["consts"])
                wt(TEN, "V", snap[f"lt1_{ci}"])
                if ci > 0:
                    wt(TEN, "S", snap[f"k2f_{ci - 1}"])
                for i in range(8):
                    op(TEN, lambda e, i=i, LT1c=LT1c: e.transpose(
                        psT[:, i * 128:(i + 1) * 128],
                        LT1c[:, i * 128:(i + 1) * 128], IDB[:]))
                mark(TEN, "PE", f"ln1tr_{ci}")
                wt(SCA, "PE", snap[f"ln1tr_{ci}"])
                op(SCA, lambda e: e.copy(L1F[:], psT))
                mark(SCA, "S", f"l1f_{ci}")
                drain(1)

                if ci == 0:
                    wt(TEN, "WL", snap["w_ab"])
                wt(TEN, "S", snap[f"l1f_{ci}"])
                if ci > 0:
                    wt(TEN, "V", snap[f"x1_{ci - 1}"])
                dest = {0: QT, 1: QT, 2: KT, 3: KT, 4: VT, 5: VT}
                def ab_group(gg):
                    out = psAB[gg]
                    for i in range(8):
                        op(TEN, lambda e, i=i, gg=gg, out=out: e.matmul(
                            out, WAB[:, i * 16 + gg * 8: i * 16 + gg * 8 + 8],
                            L1F[:, i * 128:(i + 1) * 128],
                            start=(i == 0), stop=(i == 7)))

                for g in range(6):
                    bank = psP[g % 2]
                    if g >= 2:
                        wt(TEN, "S", snap[f"pj_{ci}_{g - 2}"])
                    elif ci > 0:
                        wt(TEN, "S", snap[f"pj_{ci - 1}_{4 + g}"])
                    else:
                        wt(TEN, "V", snap["bor"])
                    if g == 4:
                        ab_group(0)
                    for i in range(8):
                        op(TEN, lambda e, i=i, g=g, bank=bank: e.matmul(
                            bank, L1F[:, i * 128:(i + 1) * 128],
                            WQKV[:, i * 3072 + g * 512: i * 3072 + (g + 1) * 512],
                            start=(i == 0), stop=(i == 7)))
                    mark(TEN, "PE", f"pjpe_{ci}_{g}")
                    wt(SCA, "PE", snap[f"pjpe_{ci}_{g}"])
                    op(SCA, lambda e, g=g, bank=bank: e.copy(
                        dest[g][:, (g % 2) * 512:((g % 2) + 1) * 512], bank))
                    mark(SCA, "S", f"pj_{ci}_{g}")
                ab_group(1)
                mark(TEN, "PE", f"abpe_{ci}")
                drain(1)
                wt(SCA, "PE", snap[f"abpe_{ci}"])
                if ci > 0:
                    wt(SCA, "V", snap[f"a9v_{ci - 1}"])
                op(SCA, lambda e: e.activation(GSIG[:, 0:128], psAB[0], AF.Sigmoid,
                                               bias=BAB[:, 0:1]))
                op(SCA, lambda e: e.activation(GSIG[:, 128:256], psAB[1], AF.Sigmoid,
                                               bias=BAB[:, 1:2]))
                mark(SCA, "S", f"sig_{ci}")

                if c == 0 and sc > 0:
                    boundary_block(sc - 1)

                wt(VEC, "S", snap[f"sig_{ci}"])
                op(VEC, lambda e: e.tensor_scalar(
                    GBET[:], GSIG[:, 0:128], 0.14900000393390656, 0.85,
                    ALU.mult, ALU.add))
                op(VEC, lambda e: e.tensor_tensor_scan(
                    GD[:], GBET[:], ZER8[:], 1.0, ALU.mult, ALU.add))
                op(VEC, lambda e: e.reciprocal(GRD[:], GD[:]))
                op(VEC, lambda e: e.drain())
                op(VEC, lambda e: e.tensor_scalar(
                    GPK[:, 0:128], GD[:], GRD[:, 127:128], None, ALU.mult))
                op(VEC, lambda e: e.scalar_tensor_tensor(
                    GPK[:, 128:256], GRD[:], GD[:, 127:128], GSIG[:, 128:256],
                    ALU.mult, ALU.mult))
                op(VEC, lambda e: e.tensor_scalar(
                    GPK[:, 256:384], ZER8[:], GD[:, 127:128], None, ALU.add))
                mark(VEC, "V", f"a9v_{ci}")

                wt(TEN, "V", snap[f"a9v_{ci}"])
                wt(TEN, "S", snap[f"sig_{ci}"])
                for j in range(3):
                    op(TEN, lambda e, j=j: e.transpose(
                        psG[:, j * 8:(j + 1) * 8],
                        GPK[0:8, j * 128:(j + 1) * 128], IDF[0:8, 0:8]))
                mark(TEN, "PE", f"gpe_{ci}")
                wt(SCA, "PE", snap[f"gpe_{ci}"])
                op(SCA, lambda e: e.copy(GT[:], psG))
                mark(SCA, "S", f"gt_{ci}")

                wt(VEC, "LD", snap[f"ct_{ci}"])
                wt(VEC, "S", cnt["S"])
                for TSR, gofs, lbl in ((QT, 0, f"qr_{ci}"), (KT, 8, f"kr_{ci}")):
                    op(VEC, lambda e, TSR=TSR: e.tensor_copy(
                        AP(SWQ, 0, [[1024, 128], [128, 8], [1, 64]]),
                        AP(TSR, 64, [[1024, 128], [128, 8], [1, 64]])))
                    op(VEC, lambda e, TSR=TSR: e.tensor_copy(
                        AP(SWQ, 64, [[1024, 128], [128, 8], [1, 64]]),
                        AP(TSR, 0, [[1024, 128], [128, 8], [1, 64]])))
                    op(VEC, lambda e: e.tensor_tensor(
                        SWQ[:], SWQ[:],
                        AP(CT, sl * 256 + 128, [[512, 128], [0, 8], [1, 128]]),
                        ALU.mult))
                    op(VEC, lambda e, TSR=TSR: e.tensor_tensor(
                        TSR[:], TSR[:],
                        AP(CT, sl * 256, [[512, 128], [0, 8], [1, 128]]),
                        ALU.mult))
                    op(VEC, lambda e, TSR=TSR: e.tensor_tensor(
                        TSR[:], TSR[:], SWQ[:], ALU.add))
                    for h in range(H):
                        op(VEC, lambda e, TSR=TSR, h=h, gofs=gofs: e.tensor_scalar(
                            TSR[:, h * 128:(h + 1) * 128],
                            TSR[:, h * 128:(h + 1) * 128],
                            GT[:, gofs + h: gofs + h + 1], None, ALU.mult))
                    mark(VEC, "V", lbl)
                drain(1)

                wt(TEN, "V", snap[f"qr_{ci}"])
                for i in range(8):
                    op(TEN, lambda e, i=i: e.transpose(
                        psT[:, i * 128:(i + 1) * 128],
                        QT[:, i * 128:(i + 1) * 128], IDB[:]))
                mark(TEN, "PE", f"qtr_{ci}")
                wt(VEC, "PE", snap[f"qtr_{ci}"])
                op(VEC, lambda e: e.tensor_copy(Q2F[:], psT))
                mark(VEC, "V", f"q2f_{ci}")
                wt(TEN, "V", max(snap[f"q2f_{ci}"], snap[f"kr_{ci}"]))
                for i in range(8):
                    op(TEN, lambda e, i=i: e.transpose(
                        psT[:, i * 128:(i + 1) * 128],
                        KT[:, i * 128:(i + 1) * 128], IDB[:]))
                mark(TEN, "PE", f"ktr_{ci}")
                wt(SCA, "PE", snap[f"ktr_{ci}"])
                op(SCA, lambda e: e.copy(K2F[:], psT))
                mark(SCA, "S", f"k2f_{ci}")

                for h in range(H):
                    op(VEC, lambda e, h=h: e.tensor_scalar(
                        SH16[:, h * 128:(h + 1) * 128],
                        SST[:, h * 128:(h + 1) * 128],
                        GT[:, 16 + h:17 + h], None, ALU.mult))
                mark(VEC, "V", f"shat_{ci}")
                drain(1)

                wt(TEN, "V", snap[f"q2f_{ci}"])
                wt(TEN, "S", snap[f"k2f_{ci}"])
                wt(TEN, "V", snap[f"x1_{ci - 1}"] if ci > 0 else snap["bor"])
                for g in range(2):
                    for j, h in enumerate(range(4 * g, 4 * g + 4)):
                        op(TEN, lambda e, g=g, j=j, h=h: e.matmul(
                            psAg[g][:, j * 128:(j + 1) * 128],
                            K2F[:, h * 128:(h + 1) * 128],
                            Q2F[:, h * 128:(h + 1) * 128],
                            start=True, stop=True))
                    mark(TEN, "PE", f"ape_{ci}_{g}")
                for g in range(2):
                    wt(VEC, "PE", snap[f"ape_{ci}_{g}"])
                    op(VEC, lambda e, g=g: e.tensor_tensor(
                        AM[:, g * 512:(g + 1) * 512], psAg[g], AMKb4, ALU.mult))
                    mark(VEC, "V", f"am_{ci}_{g}")
                drain(1)

                def o_group(g):
                    for j, h in enumerate(range(4 * g, 4 * g + 4)):
                        s_ = slice(h * 128, (h + 1) * 128)
                        js = slice(j * 128, (j + 1) * 128)
                        op(TEN, lambda e, g=g, s_=s_, js=js: e.matmul(
                            psOg[g][:, js], VT[:, s_],
                            AM[:, s_], start=True, stop=False))
                        op(TEN, lambda e, g=g, s_=s_, js=js: e.matmul(
                            psOg[g][:, js], SH16[:, s_], Q2F[:, s_],
                            start=False, stop=True))
                    mark(TEN, "PE", f"ope_{ci}_{g}")

                def t_group(g):
                    for j, h in enumerate(range(4 * g, 4 * g + 4)):
                        s_ = slice(h * 128, (h + 1) * 128)
                        js = slice(j * 128, (j + 1) * 128)
                        op(TEN, lambda e, g=g, s_=s_, js=js: e.matmul(
                            psT1g[g][:, js], KT[:, s_], VT[:, s_],
                            start=True, stop=True))
                    mark(TEN, "PE", f"t1pe_{ci}_{g}")

                def s_update(g):
                    wt(VEC, "PE", snap[f"t1pe_{ci}_{g}"])
                    for j, h in enumerate(range(4 * g, 4 * g + 4)):
                        s_ = slice(h * 128, (h + 1) * 128)
                        js = slice(j * 128, (j + 1) * 128)
                        op(VEC, lambda e, g=g, s_=s_, js=js, h=h:
                           e.scalar_tensor_tensor(
                               SST[:, s_], SST[:, s_], GT[:, 16 + h:17 + h],
                               psT1g[g][:, js], ALU.mult, ALU.add))
                    mark(VEC, "V", f"supd_{ci}_{g}")

                def of_evac(g):
                    wt(SCA, "PE", snap[f"ope_{ci}_{g}"])
                    op(SCA, lambda e, g=g: e.copy(
                        OF[:, g * 512:(g + 1) * 512], psOg[g]))
                    mark(SCA, "S", f"of_{ci}_{g}")

                wt(TEN, "V", snap[f"am_{ci}_0"])
                wt(TEN, "S", snap[f"gt_{ci}"])
                o_group(0)
                t_group(0)
                wt(TEN, "V", snap[f"am_{ci}_1"])
                o_group(1)
                of_evac(0)
                s_update(0)
                wt(TEN, "S", snap[f"of_{ci}_0"])
                t_group(1)
                of_evac(1)
                s_update(1)
                if c == 0:
                    if ci + 1 < NCH:
                        emit_ln1(ci + 1)
                else:
                    if ci + 1 < NCH:
                        ln1_stats(ci + 1)
                        wt(VEC, "S", snap[f"sq1_{ci + 1}"])
                        _ln_chains(eng, E, op, ST_, [LN1S, LN2S[0]])
                        ln1_norm(ci + 1)
                    else:
                        wt(VEC, "S", cnt["S"])
                        _ln_chains(eng, E, op, ST_, [LN2S[0]])
                        mark(VEC, "V")
                drain(1)

                if ci == 0:
                    wt(TEN, "WL", snap["w_wo"])
                wt(TEN, "V", snap[f"supd_{ci}_0"])
                wt(TEN, "S", snap[f"of_{ci}_1"])
                for mh in range(2):
                    for e8 in range(8):
                        op(TEN, lambda e, mh=mh, e8=e8: e.matmul(
                            psW[mh], OF[:, e8 * 128:(e8 + 1) * 128],
                            WO[:, e8 * 1024 + mh * 512: e8 * 1024 + (mh + 1) * 512],
                            start=(e8 == 0), stop=False))
                    op(TEN, lambda e, mh=mh: e.matmul(
                        psW[mh], C128[:], BOR[:, mh * 512:(mh + 1) * 512],
                        start=False, stop=True))
                mark(TEN, "PE", f"wo_{ci}")

                if sc > 0:
                    drain_to(f"epi_{sc - 1}_{c}")
                if c == 1:
                    op(VEC, lambda e: e.tensor_scalar(
                        L2T[:], X1[:, 0:1024],
                        ST_[:, 20:21], ST_[:, 16:17], ALU.add, ALU.mult))
                    mark(VEC, "V", f"ln2n_{sc}_0")
                    wt(TEN, "V", snap[f"ln2n_{sc}_0"])
                    if sc > 0:
                        wt(TEN, "S", snap[f"yb_{sc - 1}_1"])
                    for i in range(8):
                        op(TEN, lambda e, i=i: e.transpose(
                            psL[:, i * 128:(i + 1) * 128],
                            L2T[:, i * 128:(i + 1) * 128], IDF[:]))
                    mark(TEN, "PE", f"f3pe_{sc}_0")
                    wt(VEC, "PE", snap[f"f3pe_{sc}_0"])
                    op(VEC, lambda e: e.tensor_copy(
                        AP(L2F, 0, [[2048, 128], [256, 8], [1, 128]]),
                        AP(PSA, 0, [[1024, 128], [128, 8], [1, 128]])))
                    mark(VEC, "V", f"l2f_{sc}_0")

                wt(VEC, "PE", snap[f"wo_{ci}"])
                if sc > 0:
                    wt(VEC, "ST", snap[f"st_{(sc - 1) * 2 + c}"])
                op(VEC, lambda e, c=c: e.tensor_tensor(
                    X1[:, c * 1024:c * 1024 + 512], psW[0], x0[:, 0:512], ALU.add))
                op(VEC, lambda e, c=c: e.tensor_tensor(
                    X1[:, c * 1024 + 512:(c + 1) * 1024], psW[1],
                    x0[:, 512:1024], ALU.add))
                mark(VEC, "V", f"x1_{ci}")

                op(VEC, lambda e, c=c: e.reduce_sum(
                    ST_[:, 8 + c:9 + c], X1[:, c * 1024:(c + 1) * 1024], axis=AX.X))
                op(VEC, lambda e: e.drain())
                mark(VEC, "V")
                wt(SCA, "V", snap[f"x1_{ci}"])
                op(SCA, lambda e, c=c: e.activation(
                    SQJ[:], X1[:, c * 1024:(c + 1) * 1024], AF.Square,
                    accum_out=ST_[:, 10 + c:11 + c]))
                mark(SCA, "S", f"fsq_{ci}")
                if c == 1:
                    wt(VEC, "S", snap[f"fsq_{ci}"])
                    _ln_chains(eng, E, op, ST_, [LN2S[1]])
                    mark(VEC, "V")

                if ci + 2 < NCH:
                    wt(SYN, "V", snap[f"x1_{ci}"])
                    dma(SYN, "LD", X0[:, sl * 1024:(sl + 1) * 1024],
                        xap[(ci + 2) * 128:(ci + 3) * 128, :], label=f"x0_{ci + 2}")
                    dma(SYN, "LD", CT[:, sl * 256:(sl + 1) * 256],
                        ctap[(ci + 2) * 128:(ci + 3) * 128, :], label=f"ct_{ci + 2}")

        boundary_block(NSC - 1)
        drain_all()
        wt(GPS, "ST", cnt["ST"])

    def _ln_chain(eng, E, op, ST, c_sum, c_sq, c_negm, c_m2, c_var, c_y, c_t, w):
        _ln_chains(eng, E, op, ST, [(c_sum, c_sq, c_negm, c_m2, c_var, c_y, c_t)])

    def _ln_chains(eng, E, op, ST, specs):
        n = len(specs)
        dr = lambda: op("vector", lambda e: e.drain())

        def rnd(fn):
            for sp in specs:
                c_sum, c_sq, c_negm, c_m2, c_var, c_y, c_t = sp
                sl = lambda c: ST[:, c:c + 1]
                sli = lambda c: ST[:, c:c + 1].bitcast(I32)
                fn(sl, sli, c_sum, c_sq, c_negm, c_m2, c_var, c_y, c_t)
            if n < 3:
                dr()

        rnd(lambda sl, sli, c_sum, c_sq, c_negm, c_m2, c_var, c_y, c_t:
            op("vector", lambda e: e.tensor_scalar(
                sl(c_var), sl(c_sq), 1.0 / D, EPS, ALU.mult, ALU.add)))
        rnd(lambda sl, sli, c_sum, c_sq, c_negm, c_m2, c_var, c_y, c_t:
            op("vector", lambda e: e.tensor_scalar_mul(
                sl(c_negm), sl(c_sum), -1.0 / D)))
        rnd(lambda sl, sli, c_sum, c_sq, c_negm, c_m2, c_var, c_y, c_t:
            op("vector", lambda e: e.tensor_scalar(
                sl(c_m2), sl(c_sum), -1.0 / D, sl(c_negm), ALU.mult, ALU.mult)))
        rnd(lambda sl, sli, c_sum, c_sq, c_negm, c_m2, c_var, c_y, c_t:
            op("vector", lambda e: e.tensor_scalar(
                sl(c_var), sl(c_var), sl(c_m2), None, ALU.subtract)))
        rnd(lambda sl, sli, c_sum, c_sq, c_negm, c_m2, c_var, c_y, c_t:
            op("vector", lambda e: e.tensor_scalar(
                sli(c_y), sli(c_var), 1, 0xFFFFFFFF,
                ALU.arith_shift_right, ALU.bitwise_xor)))
        rnd(lambda sl, sli, c_sum, c_sq, c_negm, c_m2, c_var, c_y, c_t:
            op("vector", lambda e: e.tensor_single_scalar(
                sli(c_y), sli(c_y), MAGIC, ALU.add)))
        for _ in range(2):
            rnd(lambda sl, sli, c_sum, c_sq, c_negm, c_m2, c_var, c_y, c_t:
                op("vector", lambda e: e.tensor_scalar(
                    sl(c_t), sl(c_var), sl(c_y), None, ALU.mult)))
            rnd(lambda sl, sli, c_sum, c_sq, c_negm, c_m2, c_var, c_y, c_t:
                op("vector", lambda e: e.tensor_scalar(
                    sl(c_t), sl(c_t), sl(c_y), -0.5, ALU.mult, ALU.mult)))
            rnd(lambda sl, sli, c_sum, c_sq, c_negm, c_m2, c_var, c_y, c_t:
                op("vector", lambda e: e.tensor_scalar(
                    sl(c_t), sl(c_t), 1.5, None, ALU.add)))
            rnd(lambda sl, sli, c_sum, c_sq, c_negm, c_m2, c_var, c_y, c_t:
                op("vector", lambda e: e.tensor_scalar(
                    sl(c_y), sl(c_y), sl(c_t), None, ALU.mult)))

    with nc.Block() as block:
        @block.sync
        def _(eng):
            schedule("sync", eng)

        @block.gpsimd
        def _(eng):
            schedule("gpsimd", eng)

        @block.tensor
        def _(eng):
            schedule("tensor", eng)

        @block.vector
        def _(eng):
            schedule("vector", eng)

        @block.scalar
        def _(eng):
            schedule("scalar", eng)

    return nc


_NC_CACHE = [None]


def _host_prep(L0_attn_Wq, L0_attn_Wk, L0_attn_Wv, L0_attn_Wbeta, L0_attn_bbeta,
               L0_attn_Walpha, L0_attn_balpha, L0_attn_Wo, L0_attn_bo,
               L0_ffn_W1, L0_ffn_b1, L0_ffn_W2, L0_ffn_b2):
    c = np.ascontiguousarray

    perm = np.concatenate([np.arange(0, DH, 2), np.arange(1, DH, 2)])
    wq = np.asarray(L0_attn_Wq, dtype=np.float32)
    wk = np.asarray(L0_attn_Wk, dtype=np.float32)
    wq_p = wq.reshape(H, DH, D)[:, perm, :].reshape(D, D)
    wk_p = wk.reshape(H, DH, D)[:, perm, :].reshape(D, D)
    wqkv = c(np.concatenate(
        [wq_p.T, wk_p.T, np.asarray(L0_attn_Wv, np.float32).T], axis=1))
    wab = c(np.concatenate(
        [np.asarray(L0_attn_Wbeta, np.float32).T,
         np.asarray(L0_attn_Walpha, np.float32).T], axis=1))
    bab = c(np.concatenate(
        [np.asarray(L0_attn_bbeta, np.float32),
         np.asarray(L0_attn_balpha, np.float32)]).reshape(1, 16))
    wo = c(np.asarray(L0_attn_Wo, np.float32).T)
    bo = c(np.asarray(L0_attn_bo, np.float32).reshape(1, D))
    w1 = c(np.asarray(L0_ffn_W1, np.float32).T)
    b1 = c(np.asarray(L0_ffn_b1, np.float32).reshape(1, 4 * D))
    w2 = c(np.asarray(L0_ffn_W2, np.float32).T)
    b2 = c(np.asarray(L0_ffn_b2, np.float32).reshape(1, D))

    half = DH // 2
    inv_freq = (np.float32(1.0) /
                np.power(np.float32(10000.0),
                         np.arange(half, dtype=np.float32) / np.float32(half)))
    freqs = np.arange(L, dtype=np.float32)[:, None] * inv_freq[None, :]
    cos = np.cos(freqs).astype(np.float32)
    sin = np.sin(freqs).astype(np.float32)
    ct = c(np.concatenate([cos, cos, -sin, sin], axis=1)).astype(ml_dtypes.bfloat16)

    idb = np.eye(128, dtype=ml_dtypes.bfloat16)
    idf = np.eye(128, dtype=np.float32)
    amk = np.triu(np.ones((128, 128), dtype=np.float32)).astype(ml_dtypes.bfloat16)

    import jax
    with jax.default_device(jax.devices("cpu")[0]):
        keep = np.asarray(jax.random.bernoulli(
            jax.random.key(42), 0.8, (NCORES, L, D)))
    mask = np.where(keep, np.float32(1.25), np.float32(0.0)).astype(ml_dtypes.bfloat16)

    shared = dict(wqkv=wqkv, wab=wab, wo=wo, w1=w1, w2=w2, babi=bab, boi=bo,
                  b1i=b1, b2i=b2, ct=ct, idb=idb, idf=idf, amk=amk)
    return shared, mask


def kernel(L0_attn_x, L0_attn_Wq, L0_attn_Wk, L0_attn_Wv, L0_attn_Wbeta,
           L0_attn_bbeta, L0_attn_Walpha, L0_attn_balpha, L0_attn_Wo,
           L0_attn_bo, L0_ffn_W1, L0_ffn_b1, L0_ffn_W2, L0_ffn_b2):
    shared, mask = _host_prep(
        L0_attn_Wq, L0_attn_Wk, L0_attn_Wv, L0_attn_Wbeta, L0_attn_bbeta,
        L0_attn_Walpha, L0_attn_balpha, L0_attn_Wo, L0_attn_bo,
        L0_ffn_W1, L0_ffn_b1, L0_ffn_W2, L0_ffn_b2)

    x = np.asarray(L0_attn_x, dtype=np.float32)
    in_maps = []
    for b in range(NCORES):
        m = dict(shared)
        m["x"] = np.ascontiguousarray(x[b])
        m["mk"] = np.ascontiguousarray(mask[b])
        in_maps.append(m)

    if _NC_CACHE[0] is None:
        _NC_CACHE[0] = build_program()
    nc = _NC_CACHE[0]
    res = run_bass_kernel_spmd(nc, in_maps, core_ids=list(range(NCORES)))
    out = np.stack([res.results[b]["y_out"] for b in range(NCORES)], axis=0)
    return out.astype(np.float32)
